# version 02
# baseline (speedup 1.0000x reference)
import numpy as np
import ml_dtypes
from collections import defaultdict
import concourse.bass as bass
import concourse.mybir as mybir
from concourse.bass_utils import run_bass_kernel_spmd

F32 = mybir.dt.float32
BF16 = mybir.dt.bfloat16
I32 = mybir.dt.int32
U16 = mybir.dt.uint16
AF = mybir.ActivationFunctionType
ALU = mybir.AluOpType
AX = mybir.AxisListType
BF16NP = ml_dtypes.bfloat16

NCORES = 8
D = 2048
T = 16384
TC = T // NCORES
NTILE = TC // 128

ENGINES = ("sync", "tensor", "vector", "scalar", "gpsimd")


class Prog:

    def __init__(self):
        self.ops = []
        self.sem_counts = defaultdict(int)
        self._sem_cms = []
        self.sem_handles = {}
        self._dma_sems = set()

    def op(self, engine, fn, after=(), dma=False, sem=None):
        assert engine in ENGINES, engine
        if sem is None:
            sem = engine + ("_dma" if dma else "")
        inc = 16 if dma else 1
        if dma:
            self._dma_sems.add(sem)
        aft = []
        for a in after:
            if a is None:
                continue
            asem, aval = a
            if asem in self._dma_sems:
                aval = self.sem_counts[asem]
            aft.append((asem, aval))
        self.sem_counts[sem] += inc
        tokval = self.sem_counts[sem]
        self.ops.append((engine, fn, tuple(aft), sem, inc, tokval))
        return (sem, tokval)

    def final_wait(self, toks, engine="sync"):
        return self.op(engine, lambda e: e.nop(), after=toks)

    def alloc_sems(self, nc):
        for name in self.sem_counts:
            cm = nc.semaphore(name)
            h = cm.__enter__()
            self._sem_cms.append(cm)
            self.sem_handles[name] = h

    def emit(self, block):
        per_engine = defaultdict(list)
        for rec in self.ops:
            per_engine[rec[0]].append(rec)
        hwm = {e: defaultdict(int) for e in ENGINES}
        sh = self.sem_handles

        def section_for(engine):
            def _(eng):
                h = hwm[engine]
                for (_, fn, after, sem, inc, tokval) in per_engine[engine]:
                    for (wsem, wval) in after:
                        if h[wsem] < wval:
                            eng.wait_ge(sh[wsem], wval)
                            h[wsem] = wval
                    inst = fn(eng)
                    inst.then_inc(sh[sem], inc)
            return _

        for engine in ENGINES:
            if per_engine[engine]:
                getattr(block, engine)(section_for(engine))

    def build(self, nc):
        self.alloc_sems(nc)
        with nc.Block() as block:
            self.emit(block)


class Alloc:

    def __init__(self, nc, prog=None):
        import contextlib
        self.nc = nc
        self.es = contextlib.ExitStack()
        self.prog = prog
        self._const_tile = None
        self._const_cols = {}
        self._const_tok = None

    def sb(self, name, shape, dt=F32):
        return self.es.enter_context(self.nc.sbuf_tensor(name, list(shape), dt))

    def ps(self, name, shape, dt=F32):
        return self.es.enter_context(self.nc.psum_tensor(name, list(shape), dt))

    def const(self, value):
        if self._const_tile is None:
            self._const_tile = self.sb("consts", [128, 16])
        if value not in self._const_cols:
            c = len(self._const_cols)
            assert c < 16
            self._const_cols[value] = c
            ap = self._const_tile[:, c:c + 1]
            tok = self.prog.op("gpsimd", lambda e, ap=ap, v=value: e.memset(ap, v),
                               after=[self._const_tok])
            self._const_tok = tok
            self._const_cols[value] = (c, tok)
        c, tok = self._const_cols[value]
        return self._const_tile[:, c:c + 1], tok

    def close(self):
        self.es.close()


class Overlay:

    def __init__(self, nc, a, name, size_bytes):
        self.nc = nc
        self.handle = a.sb(name, [128, size_bytes], mybir.dt.uint8)
        self.base = nc.lookup_mloc(self.handle).addr

    def at(self, name, offset, shape, dtype):
        h = self.nc.virtual_sbuf_tensor(name, list(shape), dtype)
        self.nc._tensor(h.name, h.shape, h.dtype, type="SB")
        mloc = self.nc.lookup_mloc(h)
        mloc.allocated = True
        mloc.addr = self.base + offset
        return h


def emit_ln(p, a, xt, xn_out, col, stats, sqscr, deps, prev_sq, eps=1e-5,
            d=D):
    eps_ap, eps_tok = a.const(eps)
    """LayerNorm of one token-major tile xt [128, d] fp32 -> xn_out (bf16 or
    f32). stats: sbuf [128, ncols*6] scratch; col: which stat column group.
    Returns (token_of_xn, token_of_sq) ."""
    c = col * 6
    red = stats[:, c + 0:c + 1]
    mu = stats[:, c + 1:c + 2]
    sqs = stats[:, c + 2:c + 3]
    e2 = stats[:, c + 3:c + 4]
    nvar = stats[:, c + 4:c + 5]
    rstd = stats[:, c + 5:c + 6]
    t1 = p.op("vector", lambda e: e.reduce_sum(red, xt[:], axis=AX.X), after=deps)
    t2 = p.op("scalar", lambda e: e.activation(sqscr[:], xt[:], AF.Square,
                                               accum_out=sqs),
              after=list(deps) + ([prev_sq] if prev_sq else []))
    t3 = p.op("vector", lambda e: e.tensor_scalar(mu, red, 1.0 / d, None, ALU.mult),
              after=[t1])
    t4 = p.op("vector", lambda e: e.tensor_scalar(e2, sqs, 1.0 / d, None, ALU.mult),
              after=[t2])
    t5 = p.op("vector", lambda e: e.scalar_tensor_tensor(nvar, mu, mu, e2,
                                                         ALU.mult, ALU.subtract),
              after=[t3, t4])
    t6 = p.op("scalar", lambda e: e.activation(stats[:, c + 4:c + 5], nvar, AF.Sqrt,
                                               scale=-1.0, bias=eps_ap),
              after=[t5, eps_tok])
    t7 = p.op("vector", lambda e: e.reciprocal(rstd, stats[:, c + 4:c + 5]),
              after=[t6])
    t8 = p.op("vector", lambda e: e.tensor_scalar(xn_out[:], xt[:], mu, rstd,
                                                  ALU.subtract, ALU.mult),
              after=[t3, t7] + list(deps))
    return t8, t2


def build_ln_launch():
    nc = bass.Bass("TRN2", target_bir_lowering=False)
    x = nc.dram_tensor("x", [TC, D], F32, kind="ExternalInput").ap()
    y = nc.dram_tensor("y", [TC, D], BF16, kind="ExternalOutput").ap()
    p = Prog()
    a = Alloc(nc, p)
    xt = [a.sb(f"xt{i}", [128, D]) for i in range(2)]
    xn = [a.sb(f"xn{i}", [128, D], BF16) for i in range(2)]
    stats = a.sb("stats", [128, NTILE * 6])
    sqscr = a.sb("sqscr", [128, D])
    prev_use = [None, None]
    prev_store = [None, None]
    prev_sq = None
    stores = []
    for i in range(NTILE):
        b = i % 2
        ld = p.op("sync", lambda e, i=i, b=b: e.dma_start(
            out=xt[b][:], in_=x[128 * i:128 * (i + 1), :]),
            after=[prev_use[b]], dma=True, sem=f"ld{b}")
        t8, prev_sq = emit_ln(p, a, xt[b], xn[b], i, stats, sqscr,
                              [ld, prev_store[b]], prev_sq)
        prev_use[b] = t8
        st = p.op("scalar", lambda e, i=i, b=b: e.dma_start(
            out=y[128 * i:128 * (i + 1), :], in_=xn[b][:]),
            after=[t8], dma=True, sem=f"st{b}")
        prev_store[b] = st
        stores.append(st)
    p.final_wait(stores)
    p.build(nc)
    a.close()
    return nc


def host_ln(x_full):
    nc = build_ln_launch()
    in_maps = [{"x": np.ascontiguousarray(x_full[c * TC:(c + 1) * TC])}
               for c in range(NCORES)]
    res = run_bass_kernel_spmd(nc, in_maps, core_ids=list(range(NCORES)))
    return np.concatenate([res.results[c]["y"] for c in range(NCORES)], axis=0)


def build_l0tail(tc=TC, d=D, inter=5632):
    nt = tc // 128
    ndc = d // 128
    nts = tc // 512
    nds = d // 512
    nis = inter // 512
    nib = inter // 128
    nc = bass.Bass("TRN2", target_bir_lowering=False)
    x = nc.dram_tensor("x", [tc, d], F32, kind="ExternalInput").ap()
    xcw = nc.dram_tensor("xcw", [tc, d], BF16, kind="ExternalInput").ap()
    wo_t = nc.dram_tensor("wo_t", [d, d], F32, kind="ExternalInput").ap()
    gw_t = nc.dram_tensor("gw_t", [d, inter], F32, kind="ExternalInput").ap()
    uw_t = nc.dram_tensor("uw_t", [d, inter], F32, kind="ExternalInput").ap()
    dw_t = nc.dram_tensor("dw_t", [inter, d], F32, kind="ExternalInput").ap()
    gs = nc.dram_tensor("gs", [128, nib], F32, kind="ExternalInput").ap()
    us = nc.dram_tensor("us", [128, nib], F32, kind="ExternalInput").ap()
    ds = nc.dram_tensor("ds", [1, d], F32, kind="ExternalInput").ap()
    x2o = nc.dram_tensor("x2", [tc, d], F32, kind="ExternalOutput").ap()
    xn3o = nc.dram_tensor("xn3", [tc, d], BF16, kind="ExternalOutput").ap()
    x1s = nc.dram_tensor("x1s", [tc, d], F32, kind="Internal").ap()
    xn2s = nc.dram_tensor("xn2s", [tc, d], BF16, kind="Internal").ap()
    hdns = nc.dram_tensor("hdns", [inter, tc], BF16, kind="Internal").ap()
    dfms = nc.dram_tensor("dfms", [d, tc], BF16, kind="Internal").ap()

    p = Prog()
    a = Alloc(nc, p)
    A = [a.sb(f"A{i}", [128, d], BF16) for i in range(ndc)]
    B = a.sb("B", [128, ndc * d], BF16)
    _dw0 = a.sb("DW0", [128, nib * 128], BF16)
    DW = [_dw0, _dw0]
    xt = [a.sb(f"xt{i}", [128, d]) for i in range(2)]
    x1t = [a.sb(f"x1t{i}", [128, d]) for i in range(2)]
    x2t = xt
    xnt = [a.sb(f"xnt{i}", [128, d], BF16) for i in range(2)]
    dnt = [a.sb(f"dnt{i}", [128, d], BF16) for i in range(2)]
    gst = a.sb("gst", [128, nib])
    ust = a.sb("ust", [128, nib])
    dsrow = a.sb("dsrow", [1, d], BF16)
    ones1 = a.sb("ones1", [1, 128], BF16)
    dsb = a.sb("dsb", [128, d], BF16)
    stats = a.sb("stats", [128, 2 * nt * 6])
    sqscr = a.sb("sqscr", [128, d], BF16)
    hout = [a.sb(f"hout{i}", [128, 512], BF16) for i in range(2)]
    gtmp = a.sb("gtmp", [128, 512])
    stmp = a.sb("stmp", [128, 512], BF16)
    utmp = a.sb("utmp", [128, 512], BF16)
    dftile = hout
    ps = [a.ps(f"ps{i}", [128, 512]) for i in range(6)]

    ld_gs = p.op("sync", lambda e: e.dma_start(out=gst[:], in_=gs[:]), dma=True, sem="lds")
    ld_us = p.op("sync", lambda e: e.dma_start(out=ust[:], in_=us[:]), dma=True, sem="lds")
    ld_ds = p.op("gpsimd", lambda e: e.dma_start(out=dsrow[:], in_=ds[:]), dma=True, sem="ldds")
    on1 = p.op("gpsimd", lambda e: e.memset(ones1[:], 1.0))
    dsb_toks = []
    for s in range(nds):
        mm = p.op("tensor", lambda e, s=s: e.matmul(
            ps[5][:], ones1[:], dsrow[0:1, 512 * s:512 * (s + 1)],
            start=True, stop=True), after=[ld_ds, on1] + dsb_toks)
        ev = p.op("vector", lambda e, s=s: e.tensor_copy(
            dsb[:, 512 * s:512 * (s + 1)], ps[5][:]), after=[mm])
        dsb_toks = [ev]
    dsb_tok = dsb_toks[0]
    ld_wo = p.op("gpsimd", lambda e: e.dma_start(
        out=B[:].rearrange("p (c n) -> p c n", c=ndc),
        in_=wo_t.rearrange("(c p) n -> p c n", p=128)), dma=True, sem="ldwo")
    tp_toks = []
    for c in range(ndc):
        tp = p.op("sync", lambda e, c=c: e.dma_start(
            out=A[c][:, 0:tc], in_=xcw[:, 128 * c:128 * (c + 1)], transpose=True),
            dma=True, sem="ldtp")
        tp_toks.append(tp)

    prev_x = [None, None]
    prev_x1 = [None, None]
    prev_xn = [None, None]
    prev_sq = None
    bank_busy = {i: None for i in range(6)}
    bank_busy[5] = dsb_tok
    lastA = []
    lastB = []
    p1_stores = []
    for i in range(nt):
        b = i % 2
        ldx = p.op("sync", lambda e, i=i, b=b: e.dma_start(
            out=xt[b][:], in_=x[128 * i:128 * (i + 1), :]),
            after=[prev_x[b]], dma=True, sem=f"ldx{b}")
        for s in range(nds):
            bank = ps[s % 4]
            mms = []
            for c in range(ndc):
                mm = p.op("tensor", lambda e, i=i, s=s, c=c, bank=bank: e.matmul(
                    bank[:], A[c][:, 128 * i:128 * (i + 1)],
                    B[:, (c * d) + 512 * s:(c * d) + 512 * (s + 1)],
                    start=(c == 0), stop=(c == ndc - 1)),
                    after=[tp_toks[c], ld_wo] + ([bank_busy[s % 4]] if c == 0 else []))
                mms.append(mm)
            lastA.append(mms[-1])
            lastB.append(mms[-1])
            ev = p.op("vector", lambda e, i=i, s=s, b=b, bank=bank: e.tensor_add(
                x1t[b][:, 512 * s:512 * (s + 1)], bank[:],
                xt[b][:, 512 * s:512 * (s + 1)]), after=[mms[-1], ldx, prev_x1[b]])
            bank_busy[s % 4] = ev
            prev_x1[b] = ev
        t8, prev_sq = emit_ln(p, a, x1t[b], xnt[b], i, stats, sqscr,
                              [prev_x1[b], prev_xn[b]], prev_sq, d=d)
        prev_xn[b] = t8
        st1 = p.op("scalar", lambda e, i=i, b=b: e.dma_start(
            out=x1s[128 * i:128 * (i + 1), :], in_=x1t[b][:]),
            after=[prev_x1[b]], dma=True, sem=f"stx1{b}")
        st2 = p.op("scalar", lambda e, i=i, b=b: e.dma_start(
            out=xn2s[128 * i:128 * (i + 1), :], in_=xnt[b][:]),
            after=[t8], dma=True, sem=f"stxn{b}")
        prev_x[b] = ev
        prev_x1[b] = st1
        prev_xn[b] = st2
        p1_stores += [st1, st2]

    tp2_toks = []
    for c in range(ndc):
        tp = p.op("sync", lambda e, c=c: e.dma_start(
            out=A[c][:, 0:tc], in_=xn2s[:, 128 * c:128 * (c + 1)], transpose=True),
            after=lastA + p1_stores, dma=True, sem="ldtp2")
        tp2_toks.append(tp)
    nseg = (d // 512)
    seg_busy = [lastB + p1_stores] * 4
    hstore_toks = []
    hd_idx = 0
    prev_e2 = prev_e4 = None
    last_p2_mm = None
    for isl in range(nis):
        for mi, (wt, sct) in enumerate(((gw_t, gst), (uw_t, ust))):
            seg = (isl * 2 + mi) % 4
            ldw = p.op("gpsimd", lambda e, isl=isl, wt=wt, seg=seg: e.dma_start(
                out=B[:].rearrange("p (c n) -> p c n", c=ndc)[:, :, 512 * seg:512 * (seg + 1)],
                in_=wt[:, 512 * isl:512 * (isl + 1)].rearrange("(c p) n -> p c n", p=128)),
                after=seg_busy[seg], dma=True, sem=f"ldw{seg}")
            if mi == 0:
                ldg_tok = ldw
            else:
                ldu_tok = ldw
        for ib in range(4):
            iblk = isl * 4 + ib
            for ts in range(nts):
                mmsg, mmsu = [], []
                bg, bu = ps[(ts % 2) * 2], ps[(ts % 2) * 2 + 1]
                gi, ui = (ts % 2) * 2, (ts % 2) * 2 + 1
                for c in range(ndc):
                    lg = B[:, c * d + 512 * ((isl * 2 + 0) % 4) + 128 * ib:
                           c * d + 512 * ((isl * 2 + 0) % 4) + 128 * (ib + 1)]
                    lu = B[:, c * d + 512 * ((isl * 2 + 1) % 4) + 128 * ib:
                           c * d + 512 * ((isl * 2 + 1) % 4) + 128 * (ib + 1)]
                    mg = p.op("tensor", lambda e, lg=lg, c=c, ts=ts, bg=bg: e.matmul(
                        bg[:], lg, A[c][:, 512 * ts:512 * (ts + 1)],
                        start=(c == 0), stop=(c == ndc - 1)),
                        after=[ldg_tok, tp2_toks[c]] + ([bank_busy[gi]] if c == 0 else []))
                    mu = p.op("tensor", lambda e, lu=lu, c=c, ts=ts, bu=bu: e.matmul(
                        bu[:], lu, A[c][:, 512 * ts:512 * (ts + 1)],
                        start=(c == 0), stop=(c == ndc - 1)),
                        after=[ldu_tok, tp2_toks[c]] + ([bank_busy[ui]] if c == 0 else []))
                    mmsg.append(mg)
                    mmsu.append(mu)
                seg_busy[(isl * 2 + 0) % 4] = [mmsg[-1]]
                seg_busy[(isl * 2 + 1) % 4] = [mmsu[-1]]
                last_p2_mm = mmsu[-1]
                hb = hout[hd_idx % 2]
                e1 = p.op("vector", lambda e, bg=bg, iblk=iblk: e.tensor_scalar(
                    gtmp[:], bg[:], gst[:, iblk:iblk + 1], None, ALU.mult),
                    after=[mmsg[-1], ld_gs] + ([prev_e2] if prev_e2 else []))
                e2 = p.op("scalar", lambda e: e.activation(stmp[:], gtmp[:], AF.Silu),
                          after=[e1] + ([prev_e4] if prev_e4 else []))
                e3 = p.op("vector", lambda e, bu=bu, iblk=iblk: e.tensor_scalar(
                    utmp[:], bu[:], ust[:, iblk:iblk + 1], None, ALU.mult),
                    after=[mmsu[-1], ld_us, e1] + ([prev_e4] if prev_e4 else []))
                e4 = p.op("vector", lambda e, hb=hb: e.tensor_mul(hb[:], stmp[:], utmp[:]),
                          after=[e2, e3] + ([hstore_toks[-2]] if len(hstore_toks) >= 2 else []))
                bank_busy[gi] = e1
                bank_busy[ui] = e3
                prev_e2, prev_e4 = e2, e4
                sth = p.op("scalar", lambda e, hb=hb, iblk=iblk, ts=ts: e.dma_start(
                    out=hdns[128 * iblk:128 * (iblk + 1), 512 * ts:512 * (ts + 1)],
                    in_=hb[:]), after=[e4], dma=True, sem=f"sth{hd_idx % 2}")
                hstore_toks.append(sth)
                hd_idx += 1

    def hslice_tile(ibk):
        return A[ibk // 4][:, 512 * (ibk % 4):512 * (ibk % 4 + 1)]
    last_dfm = [None, None]
    prev_dw_ev = None
    df_stores = []
    for ts in range(nts):
        ldh_toks = []
        for c4 in range((nib + 3) // 4):
            n_ib = min(4, nib - c4 * 4)
            dst = A[c4][:].rearrange("p (s n) -> p s n", s=d // 512)[:, 0:n_ib, :]
            src = hdns[128 * 4 * c4:128 * (4 * c4 + n_ib), 512 * ts:512 * (ts + 1)]
            ldh = p.op("sync", lambda e, dst=dst, src=src: e.dma_start(
                out=dst, in_=src.rearrange("(s p) n -> p s n", p=128)),
                after=([last_p2_mm] + hstore_toks) if ts == 0 else [df_stores[-1]],
                dma=True, sem=f"ldh{c4 % 2}")
            ldh_toks.append(ldh)
        for db in range(ndc):
            wb = DW[db % 2]
            ldw = p.op("gpsimd", lambda e, db=db, ts=ts, wb=wb: e.dma_start(
                out=wb[:].rearrange("p (c n) -> p c n", c=nib),
                in_=dw_t[:, 128 * db:128 * (db + 1)].rearrange("(c p) n -> p c n", p=128)),
                after=[prev_dw_ev], dma=True, sem="lddw")
            bank = ps[4 + (db % 2)]
            mms = []
            for ib in range(nib):
                mm = p.op("tensor", lambda e, ib=ib, wb=wb, bank=bank: e.matmul(
                    bank[:], wb[:, 128 * ib:128 * (ib + 1)], hslice_tile(ib),
                    start=(ib == 0), stop=(ib == nib - 1)),
                    after=[ldw] + ldh_toks + ([bank_busy[4 + (db % 2)]] if ib == 0 else []))
                mms.append(mm)
            dft = dftile[db % 2]
            ev = p.op("scalar", lambda e, dft=dft, bank=bank: e.activation(
                dft[:], bank[:], AF.Identity), after=[mms[-1], last_dfm[db % 2]])
            bank_busy[4 + (db % 2)] = ev
            prev_dw_ev = ev
            stf = p.op("scalar", lambda e, db=db, ts=ts, dft=dft: e.dma_start(
                out=dfms[128 * db:128 * (db + 1), 512 * ts:512 * (ts + 1)],
                in_=dft[:]), after=[ev], dma=True, sem=f"stdf{db % 2}")
            last_dfm[db % 2] = stf
            df_stores.append(stf)

    prev2_x1 = [None, None]
    prev2_dn = [None, None]
    prev2_x2 = [None, None]
    prev2_st = [None, None]
    prev_sq = None
    out_stores = []
    for i in range(nt):
        b = i % 2
        ldx1 = p.op("sync", lambda e, i=i, b=b: e.dma_start(
            out=x1t[b][:], in_=x1s[128 * i:128 * (i + 1), :]),
            after=[prev2_x1[b]] + (df_stores if i == 0 else []), dma=True, sem=f"ldx1{b}")
        tpd = p.op("sync", lambda e, i=i, b=b: e.dma_start(
            out=dnt[b][:], in_=dfms[:, 128 * i:128 * (i + 1)], transpose=True),
            after=[prev2_dn[b]] + (df_stores if i == 0 else []), dma=True, sem=f"tpd{b}")
        t1 = p.op("vector", lambda e, b=b: e.tensor_mul(sqscr[:], dnt[b][:], dsb[:]),
                  after=[tpd, dsb_tok, prev_sq])
        t2 = p.op("vector", lambda e, b=b: e.tensor_add(x2t[b][:], sqscr[:], x1t[b][:]),
                  after=[t1, ldx1, prev2_x2[b], prev2_st[b]])
        stx2 = p.op("scalar", lambda e, i=i, b=b: e.dma_start(
            out=x2o[128 * i:128 * (i + 1), :], in_=x2t[b][:]),
            after=[t2], dma=True, sem=f"stx2{b}")
        t8, prev_sq = emit_ln(p, a, x2t[b], xnt[b], nt + i, stats, sqscr,
                              [t2, prev_xn[b]], prev_sq, d=d)
        prev_xn[b] = t8
        stxn = p.op("scalar", lambda e, i=i, b=b: e.dma_start(
            out=xn3o[128 * i:128 * (i + 1), :], in_=xnt[b][:]),
            after=[t8], dma=True, sem=f"stxn3{b}")
        prev_xn[b] = stxn
        prev2_x1[b] = t2
        prev2_dn[b] = t1
        prev2_x2[b] = t8
        prev2_st[b] = stx2
        out_stores += [stx2, stxn]
    p.final_wait(out_stores)
    p.build(nc)
    a.close()
    return nc


def emit_dequant(p, a, pk_tile, sc_tile, out_lo, out_hi, tmp, deps):
    toks = []
    for half, shift in ((0, None), (1, 4)):
        A_, B_, C_, U_ = tmp["a"], tmp["b"], tmp["c"], tmp["u16"]
        if half == 0:
            d0 = p.op("vector", lambda e: e.tensor_scalar(
                A_, pk_tile, 15, None, ALU.bitwise_and), after=deps)
        else:
            d0 = p.op("vector", lambda e: e.tensor_scalar(
                A_, pk_tile, 4, None, ALU.logical_shift_right), after=deps)
        d1 = p.op("vector", lambda e: e.tensor_scalar(
            B_, A_, 7, None, ALU.bitwise_and), after=[d0])
        d2 = p.op("vector", lambda e: e.tensor_scalar(
            C_, A_, 8, 12, ALU.bitwise_and, ALU.logical_shift_left), after=[d0])
        d3 = p.op("vector", lambda e: e.tensor_scalar(
            A_, B_, 64, 16128, ALU.mult, ALU.add), after=[d1, d2])
        d4 = p.op("vector", lambda e: e.tensor_add(A_, A_, C_), after=[d3])
        d5 = p.op("vector", lambda e: e.tensor_scalar(
            C_, B_, 1, None, ALU.is_le), after=[d4, d1])
        d6 = p.op("vector", lambda e: e.tensor_scalar(
            B_, B_, -16064, 16128, ALU.mult, ALU.add), after=[d5])
        d7 = p.op("vector", lambda e: e.tensor_mul(B_, B_, C_), after=[d6, d5])
        d8 = p.op("vector", lambda e: e.tensor_sub(A_, A_, B_), after=[d7, d4])
        d9 = p.op("vector", lambda e: e.tensor_copy(U_, A_), after=[d8])
        out = out_lo if half == 0 else out_hi
        d10 = p.op("vector", lambda e, out=out: e.tensor_mul(
            out, U_.bitcast(BF16), sc_tile), after=[d9] + deps)
        toks.append(d10)
    return toks


def build_l1ffn(t=T, d=D, ic=2048):
    nt512 = t // 512
    ndc = d // 128
    nwd = d // 256
    nwi = ic // 256
    nib = ic // 128
    nds = d // 512
    ntb = t // 128
    wd = d // 2 // 128
    nc = bass.Bass("TRN2", target_bir_lowering=False)
    xn4T = nc.dram_tensor("xn4T", [d, t], BF16, kind="ExternalInput").ap()
    gpk = nc.dram_tensor("gpk", [d // 2, ic], I32, kind="ExternalInput").ap()
    upk = nc.dram_tensor("upk", [d // 2, ic], I32, kind="ExternalInput").ap()
    gsc = nc.dram_tensor("gsc", [d // 2, ic], F32, kind="ExternalInput").ap()
    usc = nc.dram_tensor("usc", [d // 2, ic], F32, kind="ExternalInput").ap()
    dpk = nc.dram_tensor("dpk", [ic // 2, d], I32, kind="ExternalInput").ap()
    dsc = nc.dram_tensor("dsc", [ic // 2, d], F32, kind="ExternalInput").ap()
    part = nc.dram_tensor("part", [t, d], F32, kind="ExternalOutput").ap()
    hdns = nc.dram_tensor("hdns", [ic, t], BF16, kind="Internal").ap()

    p = Prog()
    a = Alloc(nc, p)
    wdq = max(ic, d)
    hw_ = wdq // 2
    GQ = [a.sb(f"GQ{i}", [128, wdq], BF16) for i in range(ndc)]
    UQ = [a.sb(f"UQ{i}", [128, ic], BF16) for i in range(ndc)]
    pk_t = a.sb("pk_t", [128, hw_], I32)
    sc_t = a.sb("sc_t", [128, hw_])
    tmp = {"a": a.sb("tmpa", [128, hw_], I32), "b": a.sb("tmpb", [128, hw_], I32),
           "c": a.sb("tmpc", [128, hw_], I32), "u16": a.sb("tmpu", [128, hw_], U16)}
    _xs0 = a.sb("xs0", [128, ndc * 512], BF16)
    xs = [_xs0, _xs0]
    g2 = a.sb("g2", [128, 512])
    aat = a.sb("aat", [128, 512])
    cct = a.sb("cct", [128, 512])
    tht = a.sb("tht", [128, 512], BF16)
    h1t = a.sb("h1t", [128, 512], BF16)
    ggt = a.sb("ggt", [128, 512])
    hdt = [a.sb(f"hdt{i}", [128, 512], BF16) for i in range(2)]
    hld = [a.sb(f"hld{i}", [128, nib * 128], BF16) for i in range(2)]
    ott = [a.sb(f"ott{i}", [128, 512]) for i in range(2)]
    ps = [a.ps(f"ps{i}", [128, 512]) for i in range(6)]

    dq_toks = []
    ld_prev = None
    hwc = min(hw_, ic)
    nh = max(1, ic // hwc)
    for mat, (pk, sc, out_arena) in enumerate(((gpk, gsc, GQ), (upk, usc, UQ))):
        for k in range(nwd):
            for h in range(nh):
                cs, ce = h * hwc, (h + 1) * hwc
                ldp = p.op("sync", lambda e, pk=pk, k=k, cs=cs, ce=ce: e.dma_start(
                    out=pk_t[:, 0:ce - cs], in_=pk[128 * k:128 * (k + 1), cs:ce]),
                    after=[ld_prev], dma=True, sem="ldpk")
                lds = p.op("sync", lambda e, sc=sc, k=k, cs=cs, ce=ce: e.dma_start(
                    out=sc_t[:, 0:ce - cs], in_=sc[128 * k:128 * (k + 1), cs:ce]),
                    after=[ld_prev], dma=True, sem="ldsc")
                toks = emit_dequant(p, a, pk_t[:, 0:ce - cs], sc_t[:, 0:ce - cs],
                                    out_arena[2 * k][:, cs:ce],
                                    out_arena[2 * k + 1][:, cs:ce],
                                    {k2: v[:, 0:ce - cs] for k2, v in tmp.items()},
                                    [ldp, lds])
                dq_toks += toks
                ld_prev = toks[-1]

    bank_busy = {i: None for i in range(6)}
    hst = [None, None]
    prev_sw = {}
    xs_busy = [None, None]
    last_hstore = []
    for tsl in range(nt512):
        bx = 0
        ldx = p.op("sync", lambda e, tsl=tsl, bx=bx: e.dma_start(
            out=xs[bx][:].rearrange("p (c n) -> p c n", c=ndc),
            in_=xn4T[:, 512 * tsl:512 * (tsl + 1)].rearrange("(c p) n -> p c n", p=128)),
            after=[xs_busy[bx]], dma=True, sem=f"ldxs{bx}")
        for ib in range(nib):
            bg, bu = ps[(ib % 2) * 2], ps[(ib % 2) * 2 + 1]
            mg = mu_ = None
            for c in range(ndc):
                mg = p.op("tensor", lambda e, c=c, ib=ib, bg=bg, bx=bx: e.matmul(
                    bg[:], GQ[c][:, 128 * ib:128 * (ib + 1)],
                    xs[bx][:, 512 * c:512 * (c + 1)],
                    start=(c == 0), stop=(c == ndc - 1)),
                    after=[ldx, dq_toks[-1]] + ([bank_busy[(ib % 2) * 2]] if c == 0 else []))
                mu_ = p.op("tensor", lambda e, c=c, ib=ib, bu=bu, bx=bx: e.matmul(
                    bu[:], UQ[c][:, 128 * ib:128 * (ib + 1)],
                    xs[bx][:, 512 * c:512 * (c + 1)],
                    start=(c == 0), stop=(c == ndc - 1)),
                    after=[ldx] + ([bank_busy[(ib % 2) * 2 + 1]] if c == 0 else []))
            xs_busy[bx] = mu_
            pe = prev_sw
            q1 = p.op("scalar", lambda e, bg=bg: e.activation(g2[:], bg[:], AF.Square),
                      after=[mg, pe.get("q2")])
            q2 = p.op("vector", lambda e: e.tensor_scalar(
                aat[:], g2[:], 0.0356774081, 0.7978845608, ALU.mult, ALU.add),
                after=[q1, pe.get("q3")])
            q3 = p.op("vector", lambda e, bg=bg: e.tensor_mul(cct[:], aat[:], bg[:]),
                      after=[q2, pe.get("q4")])
            q4 = p.op("scalar", lambda e: e.activation(tht[:], cct[:], AF.Tanh),
                      after=[q3, pe.get("q5")])
            q5 = p.op("vector", lambda e: e.tensor_scalar(
                h1t[:], tht[:], 0.5, 0.5, ALU.mult, ALU.add),
                after=[q4, pe.get("q6")])
            q6 = p.op("vector", lambda e, bg=bg: e.tensor_mul(ggt[:], h1t[:], bg[:]),
                      after=[q5, pe.get("q7")])
            bank_busy[(ib % 2) * 2] = q6
            hb = hdt[ib % 2]
            q7 = p.op("vector", lambda e, bu=bu, hb=hb: e.tensor_mul(
                hb[:], ggt[:], bu[:]), after=[q6, mu_, hst[ib % 2]])
            bank_busy[(ib % 2) * 2 + 1] = q7
            st = p.op("scalar", lambda e, ib=ib, tsl=tsl, hb=hb: e.dma_start(
                out=hdns[128 * ib:128 * (ib + 1), 512 * tsl:512 * (tsl + 1)],
                in_=hb[:]), after=[q7], dma=True, sem=f"sthd{ib % 2}")
            hst[ib % 2] = st
            prev_sw = {"q2": q2, "q3": q3, "q4": q4, "q5": q5, "q6": q6, "q7": q7}
            last_hstore.append(st)

    dq2_toks = []
    ld_prev = [bank_busy[0], bank_busy[1], bank_busy[2], bank_busy[3],
               xs_busy[0], xs_busy[1]]
    for k in range(nwi):
        hwd = min(hw_, d)
        for h in range(max(1, d // hwd)):
            cs, ce = h * hwd, (h + 1) * hwd
            ldp = p.op("sync", lambda e, k=k, cs=cs, ce=ce: e.dma_start(
                out=pk_t[:, 0:ce - cs], in_=dpk[128 * k:128 * (k + 1), cs:ce]),
                after=ld_prev, dma=True, sem="ldpk")
            lds = p.op("sync", lambda e, k=k, cs=cs, ce=ce: e.dma_start(
                out=sc_t[:, 0:ce - cs], in_=dsc[128 * k:128 * (k + 1), cs:ce]),
                after=ld_prev, dma=True, sem="ldsc")
            toks = emit_dequant(p, a, pk_t[:, 0:ce - cs], sc_t[:, 0:ce - cs],
                                GQ[2 * k][:, cs:ce], GQ[2 * k + 1][:, cs:ce],
                                {k2: v[:, 0:ce - cs] for k2, v in tmp.items()},
                                [ldp, lds])
            dq2_toks += toks
            ld_prev = [toks[-1]]

    hl_busy = [None, None]
    st_busy = [None, None]
    for tb in range(ntb):
        bh = tb % 2
        ldh = p.op("sync", lambda e, tb=tb, bh=bh: e.dma_start(
            out=hld[bh][:].rearrange("p (c n) -> p c n", c=nib),
            in_=hdns[:, 128 * tb:128 * (tb + 1)].rearrange("(c p) n -> p c n", p=128)),
            after=[hl_busy[bh]] + (last_hstore if tb < 2 else []),
            dma=True, sem=f"ldhd{bh}")
        lastmm = None
        for dsl in range(nds):
            bank = ps[4 + (dsl % 2)]
            mms = []
            for ib in range(nib):
                mm = p.op("tensor", lambda e, ib=ib, dsl=dsl, bh=bh, bank=bank: e.matmul(
                    bank[:], hld[bh][:, 128 * ib:128 * (ib + 1)],
                    GQ[ib][:, 512 * dsl:512 * (dsl + 1)],
                    start=(ib == 0), stop=(ib == nib - 1)),
                    after=[ldh, dq2_toks[-1]] + ([bank_busy[4 + (dsl % 2)]] if ib == 0 else []))
                mms.append(mm)
            lastmm = mms[-1]
            ev = p.op("vector", lambda e, dsl=dsl, bank=bank: e.tensor_copy(
                ott[dsl % 2][:], bank[:]), after=[mms[-1], st_busy[dsl % 2]])
            bank_busy[4 + (dsl % 2)] = ev
            st = p.op("scalar", lambda e, tb=tb, dsl=dsl: e.dma_start(
                out=part[128 * tb:128 * (tb + 1), 512 * dsl:512 * (dsl + 1)],
                in_=ott[dsl % 2][:]), after=[ev], dma=True, sem=f"stp{dsl % 2}")
            st_busy[dsl % 2] = st
        hl_busy[bh] = lastmm
    p.final_wait([st_busy[0], st_busy[1]])
    p.build(nc)
    a.close()
    return nc


def build_l0attn(t=T, d=D, nsteps=None, debug_stop=None):
    K = 16
    HD = 64
    NHL = 4
    n_chunk = t // K if nsteps is None else nsteps
    tt = n_chunk * K
    ndc = d // 128
    nts = tt // 512
    G = 32
    ngr = n_chunk // G
    nc = bass.Bass("TRN2", target_bir_lowering=False)
    ln1T = nc.dram_tensor("ln1T", [d, t], BF16, kind="ExternalInput").ap()
    wqT = nc.dram_tensor("wqT", [d, 256], F32, kind="ExternalInput").ap()
    wkT = nc.dram_tensor("wkT", [d, 256], F32, kind="ExternalInput").ap()
    wvT = nc.dram_tensor("wvT", [d, 256], F32, kind="ExternalInput").ap()
    lwT4 = nc.dram_tensor("lwT4", [d, 4], F32, kind="ExternalInput").ap()
    lb4 = nc.dram_tensor("lb4", [4, 1], F32, kind="ExternalInput").ap()
    lnw_rep = nc.dram_tensor("lnw_rep", [128, HD], F32, kind="ExternalInput").ap()
    lnb_row = nc.dram_tensor("lnb_row", [1, 256], F32, kind="ExternalInput").ap()
    s0 = nc.dram_tensor("s0", [65, NHL * 64], F32, kind="ExternalInput").ap()
    xcw = nc.dram_tensor("xcw", [tt, 256], BF16, kind="ExternalOutput").ap()
    xbT_s = nc.dram_tensor("xbT_s", [256, tt], BF16, kind="Internal").ap()
    xcT_s = nc.dram_tensor("xcT_s", [256, tt], BF16, kind="Internal").ap()
    sp_s = nc.dram_tensor("sp_s", [tt, 260], F32, kind="Internal").ap()
    xa_s = nc.dram_tensor("xa_s", [tt, 256], F32, kind="Internal").ap()
    xc_s = nc.dram_tensor("xc_s", [tt, 256], F32, kind="Internal").ap()
    cobx_s = nc.dram_tensor("cobx_s", [n_chunk, NHL, K, 65], F32, kind="Internal").ap()
    xa2_s = nc.dram_tensor("xa2_s", [n_chunk, NHL, K, HD], F32, kind="Internal").ap()
    xc2_s = nc.dram_tensor("xc2_s", [n_chunk, NHL, K, HD], F32, kind="Internal").ap()

    p = Prog()
    a = Alloc(nc, p)
    KB = 1024
    SL = min(2048, tt)
    nsl = tt // SL
    LT = [a.sb(f"LT{i}", [128, SL], BF16) for i in range(ndc)]
    wq_sb = a.sb("wq_sb", [128, ndc * 256], BF16)
    wk_sb = a.sb("wk_sb", [128, ndc * 256], BF16)
    wv_sb = a.sb("wv_sb", [128, ndc * 256], BF16)
    lw_sb = a.sb("lw_sb", [128, ndc * 4], BF16)
    lb_sb = a.sb("lb_sb", [4, 1])
    lbr = a.sb("lbr", [1, 4], BF16)
    lnbr = a.sb("lnbr", [1, 256], BF16)
    nones = a.sb("nones", [1, 128], BF16)
    g_t = a.sb("g_t", [128, HD])
    g2_t = a.sb("g2_t", [128, HD])
    evq = [a.sb(f"evq{i}", [128, 512], BF16) for i in range(2)]
    sgq = a.sb("sgq", [128, 512])
    rc_col = a.sb("rc_col", [128, 1])
    onesp = a.sb("onesp", [1, 512], BF16)
    it_col = a.sb("it_col", [128, 1], I32)
    sp_t = [a.sb(f"sp_t{i}", [128, 260]) for i in range(2)]
    xa_t = [a.sb(f"xa_t{i}", [128, 256]) for i in range(2)]
    xc_t = [a.sb(f"xc_t{i}", [128, 256]) for i in range(2)]
    xbts = [a.sb(f"xbts{i}", [65, NHL * G * K], BF16) for i in range(2)]
    xcts = [a.sb(f"xcts{i}", [65, NHL * G * K], BF16) for i in range(2)]
    cbx = [a.sb(f"cbx{i}", [128, G * 65], BF16) for i in range(2)]
    xas = [a.sb(f"xas{i}", [128, G * HD]) for i in range(2)]
    xcs = [a.sb(f"xcs{i}", [128, G * HD]) for i in range(2)]
    outb = [a.sb(f"outb{i}", [128, G * HD], BF16) for i in range(2)]
    xag_g = [a.sb(f"xag_g{i}", [128, G * HD]) for i in range(2)]
    S_all = a.sb("S_all", [65, NHL * 65])
    Sb = a.sb("Sb", [65, NHL * 65], BF16)
    maskT = a.sb("maskT", [128, K])
    zcb = a.sb("zcb", [128, HD])
    sqsc = a.sb("sqsc", [128, HD], BF16)
    xhb = a.sb("xhb", [128, HD])
    t5b = a.sb("t5b", [128, HD])
    xagb = a.sb("xagb", [128, HD])
    grb = a.sb("grb", [128, HD])
    t4b = a.sb("t4b", [128, HD], BF16)
    u1b = a.sb("u1b", [128, HD])
    ngb = a.sb("ngb", [128, HD], BF16)
    mco = a.sb("mco", [128, K])
    m2b = a.sb("m2b", [128, K], BF16)
    zc2b = a.sb("zc2b", [128, HD])
    o1b = a.sb("o1b", [128, HD])
    o2b = a.sb("o2b", [128, HD])
    stt = a.sb("stt", [128, 16])
    cobx65 = None
    ps_z = a.ps("ps_z", [128, 512])
    ps_at = a.ps("ps_at", [128, 512])
    ps_zb = a.ps("ps_zb", [128, 512])
    ps_u = a.ps("ps_u", [128, 512])
    ps_p1 = [a.ps(f"ps_p1{i}", [128, 512]) for i in range(2)]
    ps_p2 = [a.ps(f"ps_p2{i}", [128, 512]) for i in range(2)]

    eps_ap, eps_tok = a.const(1e-6)

    ldw = []
    for nm, (sb, dr, w) in {"q": (wq_sb, wqT, 256), "k": (wk_sb, wkT, 256),
                            "v": (wv_sb, wvT, 256), "l": (lw_sb, lwT4, 4)}.items():
        ldw.append(p.op("gpsimd", lambda e, sb=sb, dr=dr, w=w: e.dma_start(
            out=sb[:].rearrange("p (c n) -> p c n", c=ndc),
            in_=dr.rearrange("(c p) n -> p c n", p=128)), dma=True, sem="ldw"))
    ldb = p.op("sync", lambda e: e.dma_start(out=lb_sb[:], in_=lb4[:]),
               dma=True, sem="lds")
    ldbrow = p.op("gpsimd", lambda e: e.dma_start(
        out=lbr[:], in_=lb4.rearrange("h o -> o h")), dma=True, sem="ldbg")
    ldg = p.op("sync", lambda e: e.dma_start(out=g_t[:], in_=lnw_rep[:]),
               dma=True, sem="lds")
    ldbr = p.op("gpsimd", lambda e: e.dma_start(out=lnbr[:], in_=lnb_row[:]),
                dma=True, sem="ldbg")
    lds0 = p.op("sync", lambda e: e.dma_start(
        out=S_all[:].rearrange("p (h n) -> p h n", h=NHL)[:, :, 0:64],
        in_=s0.rearrange("p (h n) -> p h n", h=NHL)), dma=True, sem="lds")

    g2k = p.op("vector", lambda e: e.tensor_mul(g2_t[:], g_t[:], g_t[:]),
               after=[ldg])
    mn1 = p.op("gpsimd", lambda e: e.memset(nones[:], -1.0))
    mp1 = p.op("gpsimd", lambda e: e.memset(onesp[:], 1.0))
    mm0 = p.op("gpsimd", lambda e: e.memset(maskT[:], 1.0))
    mm1 = p.op("gpsimd", lambda e: e.affine_select(
        maskT[:], maskT[:], [[1, K]], ALU.is_ge, 0.0,
        base=0, channel_multiplier=-1), after=[mm0])
    it0 = p.op("gpsimd", lambda e: e.iota(it_col[:], [[0, 1]], base=0,
                                          channel_multiplier=1))
    rc1 = p.op("vector", lambda e: e.tensor_scalar(
        it_col[:], it_col[:], 15, None, ALU.bitwise_and), after=[it0])
    rc2 = p.op("vector", lambda e: e.tensor_copy(rc_col[:], it_col[:]), after=[rc1])
    rc3 = p.op("vector", lambda e: e.tensor_scalar(
        rc_col[:], rc_col[:], 1.0, None, ALU.add), after=[rc2])
    rc4 = p.op("vector", lambda e: e.reciprocal(rc_col[:], rc_col[:]), after=[rc3])
    rc5 = p.op("vector", lambda e: e.tensor_scalar(
        rc_col[:], rc_col[:], 1.0 / 64.0, None, ALU.mult), after=[rc4])
    mk1 = p.op("gpsimd", lambda e: e.iota(maskT[:], [[1, K]], base=0,
               channel_multiplier=0, allow_small_or_imprecise_dtypes=True),
               after=[mm1])
    mk2 = p.op("vector", lambda e: e.tensor_copy(zcb[:, 0:1], it_col[:]),
               after=[rc1, mk1])
    mk3 = p.op("vector", lambda e: e.tensor_scalar(
        maskT[:], maskT[:], zcb[:, 0:1], None, ALU.subtract), after=[mk2])
    mk4 = p.op("vector", lambda e: e.tensor_scalar(
        maskT[:], maskT[:], -0.5, None, ALU.is_gt), after=[mk3])
    mask_tok = mk4

    sr_toks = []
    for hl in range(NHL):
        sr_toks.append(p.op("vector", lambda e, hl=hl: e.reduce_sum(
            S_all[:, 65 * hl + 64:65 * hl + 65],
            S_all[:].rearrange("p (h n) -> p h n", h=NHL)[:, hl, 0:64],
            axis=AX.X), after=[lds0] + sr_toks[-1:]))
    sb0 = p.op("vector", lambda e: e.tensor_copy(Sb[:], S_all[:]),
               after=sr_toks)

    p1_st = []
    prev_p1st = [None, None]
    bank_busy = [None, None, None, None]
    ltp = [None] * ndc
    slice_consumers = []
    for sl in range(nsl):
      t0 = sl * SL
      new_ltp = []
      for c in range(ndc):
        new_ltp.append(p.op("sync", lambda e, c=c, t0=t0: e.dma_start(
            out=LT[c][:], in_=ln1T[128 * c:128 * (c + 1), t0:t0 + SL]),
            after=slice_consumers + [ltp[c]], dma=True, sem=f"ldlt{c % 2}"))
      ltp = new_ltp
      slice_consumers = []
      for half in range(2):
        for tsl_ in range(SL // 512):
            tsl = sl * (SL // 512) + tsl_
            for wi, (wsb, outdr) in enumerate(((wk_sb, xbT_s), (wq_sb, xcT_s))):
                bk = ps_p1[wi]
                mms = []
                for c in range(ndc):
                    mms.append(p.op("tensor", lambda e, wsb=wsb, c=c, half=half,
                               tsl_=tsl_, bk=bk: e.matmul(
                        bk[:], wsb[:, 256 * c + 128 * half:256 * c + 128 * (half + 1)],
                        LT[c][:, 512 * tsl_:512 * (tsl_ + 1)],
                        start=(c == 0), stop=(c == ndc - 1)),
                        after=[ldw[wi], ltp[c]] + ([bank_busy[wi]] if c == 0 else [])))
                slice_consumers.append(mms[-1])
                ev = p.op("vector", lambda e, wi=wi, bk=bk: e.tensor_copy(
                    evq[wi][:], bk[:]), after=[mms[-1], prev_p1st[wi]])
                bank_busy[wi] = ev
                st = p.op("scalar", lambda e, wi=wi, half=half, tsl=tsl,
                          outdr=outdr: e.dma_start(
                    out=outdr[128 * half:128 * (half + 1), 512 * tsl:512 * (tsl + 1)],
                    in_=evq[wi][:]), after=[ev], dma=True, sem=f"stp1{wi}")
                p1_st.append(st)
                prev_p1st[wi] = st

      if sl == 0:
        p2_st = []
        prev_sp = [None, None]
        prev_xa = [None, None]
        prev_xc = [None, None]
      for i_ in range(SL // 128):
        i = sl * (SL // 128) + i_
        b = i % 2
        bkb, bka = ps_p2[0], ps_p2[1]
        bkc, bki = ps_p1[0], ps_p1[1]
        mmsb, mmsa, mmsc, mmsi = [], [], [], []
        for c in range(ndc):
            lhs = LT[c][:, 128 * i_:128 * (i_ + 1)]
            mmsb.append(p.op("tensor", lambda e, c=c, i=i, lhs=lhs: e.matmul(
                bkb[:, 0:256], lhs, wk_sb[:, 256 * c:256 * (c + 1)],
                start=(c == 0), stop=(c == ndc - 1)),
                after=[ldw[1], ltp[c]] + ([prev_sp[b], bank_busy[0]] if c == 0 else [])))
            mmsa.append(p.op("tensor", lambda e, c=c, i=i, lhs=lhs: e.matmul(
                bka[:, 0:256], lhs, wv_sb[:, 256 * c:256 * (c + 1)],
                start=(c == 0), stop=False),
                after=[ldw[2], ltp[c]] + ([prev_xa[b], bank_busy[1]] if c == 0 else [])))
            mmsc.append(p.op("tensor", lambda e, c=c, i=i, lhs=lhs: e.matmul(
                bkc[:, 0:256], lhs, wq_sb[:, 256 * c:256 * (c + 1)],
                start=(c == 0), stop=False),
                after=[ldw[0], ltp[c]] + ([prev_xc[b], bank_busy[2]] if c == 0 else [])))
            mmsi.append(p.op("tensor", lambda e, c=c, i=i, lhs=lhs: e.matmul(
                bki[:, 0:4], lhs, lw_sb[:, 4 * c:4 * (c + 1)],
                start=(c == 0), stop=False),
                after=[ldw[3], ltp[c]] + ([bank_busy[3]] if c == 0 else [])))
        mma2 = p.op("tensor", lambda e: e.matmul(
            bka[:, 0:256], nones[:], lnbr[:], start=False, stop=True),
            after=[mn1, ldbr])
        mmc2 = p.op("tensor", lambda e: e.matmul(
            bkc[:, 0:256], onesp[0:1, 0:128], lnbr[:],
            start=False, stop=True), after=[mp1, ldbr])
        mmib = p.op("tensor", lambda e, bki=bki: e.matmul(
            bki[:, 0:4], onesp[0:1, 0:128], lbr[:], start=False, stop=True),
            after=[mp1, ldbrow, mmsi[-1]])
        slice_consumers.append(mmib)
        evi = p.op("scalar", lambda e, bki=bki: e.activation(
            sgq[:, 0:4], bki[:, 0:4], AF.Sigmoid),
            after=[mmib, bank_busy[3]])
        coi = p.op("vector", lambda e: e.tensor_scalar(
            sgq[:, 4:8], sgq[:, 0:4], rc_col[:, 0:1], None, ALU.mult),
            after=[evi, rc5])
        bank_busy[3] = coi
        evb = p.op("vector", lambda e, b=b, bkb=bkb: e.tensor_copy(
            sp_t[b][:].rearrange("p (h n) -> p h n", h=NHL)[:, :, 0:64],
            bkb[:, 0:256].rearrange("p (h n) -> p h n", h=NHL)),
            after=[mmsb[-1], prev_sp[b]])
        cts = []
        for hl in range(NHL):
            c1_ = p.op("vector", lambda e, b=b, hl=hl: e.tensor_scalar(
                sp_t[b][:, 65 * hl:65 * hl + 64], sp_t[b][:, 65 * hl:65 * hl + 64],
                sgq[:, 4 + hl:5 + hl], None, ALU.mult), after=[evb, coi])
            c2_ = p.op("vector", lambda e, b=b, hl=hl: e.tensor_copy(
                sp_t[b][:, 65 * hl + 64:65 * hl + 65], sgq[:, 4 + hl:5 + hl]),
                after=[evb, coi])
            cts += [c1_, c2_]
        stsp = p.op("scalar", lambda e, i=i, b=b: e.dma_start(
            out=sp_s[128 * i:128 * (i + 1), :], in_=sp_t[b][:]),
            after=cts, dma=True, sem=f"stsp{b}")
        prev_sp[b] = stsp
        eva = p.op("vector", lambda e, b=b, bka=bka: e.tensor_copy(
            xa_t[b][:], bka[:, 0:256]), after=[mma2, prev_xa[b]])
        eva2 = p.op("vector", lambda e, b=b, bkb=bkb: e.tensor_sub(
            xa_t[b][:], xa_t[b][:], bkb[:, 0:256]), after=[eva, mmsb[-1]])
        bank_busy[0] = eva2
        bank_busy[1] = eva2
        sta = p.op("scalar", lambda e, i=i, b=b: e.dma_start(
            out=xa_s[128 * i:128 * (i + 1), :], in_=xa_t[b][:]),
            after=[eva2], dma=True, sem=f"sta{b}")
        prev_xa[b] = sta
        evc = p.op("vector", lambda e, b=b, bkc=bkc: e.tensor_copy(
            xc_t[b][:], bkc[:, 0:256]), after=[mmc2, prev_xc[b]])
        bank_busy[2] = evc
        stc = p.op("scalar", lambda e, i=i, b=b: e.dma_start(
            out=xc_s[128 * i:128 * (i + 1), :], in_=xc_t[b][:]),
            after=[evc], dma=True, sem=f"stc{b}")
        prev_xc[b] = stc
        p2_st += [stsp, sta, stc]

    re_toks = []
    for k in range(K):
        re_toks.append(p.op("gpsimd", lambda e, k=k: e.dma_start(
            out=cobx_s[:, :, k, :],
            in_=sp_s.rearrange("(n k) (h g) -> n k h g", k=K, h=NHL)[:, k, :, :]),
            after=p2_st, dma=True, sem=f"re{k % 4}"))
        re_toks.append(p.op("gpsimd", lambda e, k=k: e.dma_start(
            out=xa2_s[:, :, k, :],
            in_=xa_s.rearrange("(n k) (h g) -> n k h g", k=K, h=NHL)[:, k, :, :]),
            after=p2_st, dma=True, sem=f"re{k % 4}"))
        re_toks.append(p.op("gpsimd", lambda e, k=k: e.dma_start(
            out=xc2_s[:, :, k, :],
            in_=xc_s.rearrange("(n k) (h g) -> n k h g", k=K, h=NHL)[:, k, :, :]),
            after=p2_st, dma=True, sem=f"re{k % 4}"))

    if debug_stop == "p2":
        p.final_wait(p2_st)
        p.build(nc)
        a.close()
        return nc
    if debug_stop == "re":
        p.final_wait(re_toks)
        p.build(nc)
        a.close()
        return nc
    init_toks = []
    for buf in xbts + xcts:
        init_toks.append(p.op("gpsimd", lambda e, buf=buf: e.memset(buf[:], 1.0)))
    for buf in cbx + xas + xcs:
        init_toks.append(p.op("gpsimd", lambda e, buf=buf: e.memset(buf[:], 0.0)))

    GK = G * K
    wr = {}
    from collections import defaultdict as _dd
    rd = _dd(list)

    def OP(engine, fn, reads=(), writes=(), extra=(), dma=False, sem=None):
        after = list(extra)
        for t_ in writes:
            after += rd[t_] + [wr.get(t_)]
        for t_ in reads:
            after.append(wr.get(t_))
        tok = p.op(engine, fn, after=after, dma=dma, sem=sem)
        for t_ in writes:
            rd[t_] = []
            wr[t_] = tok
        for t_ in reads:
            rd[t_].append(tok)
        return tok

    for tok, names in ((sb0, ["Sb", "S_all"]), (mask_tok, ["maskT", "zcb"]),
                       (g2k, ["g2", "g"]), (rc5, ["rc"])):
        for nmx in names:
            wr[nmx] = tok
    wr["stream"] = init_toks[-1]
    for bank, width, names in ((ps_z, 65, ["psz0", "psz1", "psz2", "psz3"]),
                               (ps_at, K, ["psat0", "psat1", "psat2", "psat3"]),
                               (ps_zb, 65, ["pszb0", "pszb1", "pszb2", "pszb3"]),
                               (ps_u, NHL * HD, ["psu0", "psu1", "psu2", "psu3"])):
        wtok = p.op("tensor", lambda e, bank=bank, width=width: e.matmul(
            bank[:, 0:width], onesp[0:1, 0:128], onesp[0:1, 0:width],
            start=True, stop=True), after=[mp1])
        for nmx in names:
            wr[nmx] = wtok

    out_stores = []
    pending = None

    def emit_outln(pend):
        pb_, gs_, zp_, n0_ = pend
        zbank_ = ps_zb if zp_ == 0 else ps_p1[0]
        zb_names = [f"pszb{zp_}_{hl}" for hl in range(NHL)]
        OP("scalar", lambda e, zbank_=zbank_: e.activation(
            stt[:, 8:9], zbank_[:, 64:65], AF.Identity, scale=-1.0 / 64),
            reads=zb_names, writes=["nmu2"])
        OP("scalar", lambda e, zbank_=zbank_: e.activation(
            zc2b[:], zbank_[:, 0:64], AF.Identity, bias=stt[:, 8:9]),
            reads=["nmu2"] + zb_names, writes=["zc2b"])
        OP("scalar", lambda e: e.activation(
            sqsc[:], zc2b[:], AF.Square, accum_out=stt[:, 9:10]),
            reads=["zc2b"], writes=["sqs2", "sqsc"])
        OP("scalar", lambda e: e.activation(
            stt[:, 10:11], stt[:, 9:10], AF.Sqrt, scale=1.0 / 64, bias=eps_ap),
            reads=["sqs2"], writes=["std2"], extra=[eps_tok])
        OP("vector", lambda e: e.reciprocal(stt[:, 11:12], stt[:, 10:11]),
            reads=["std2"], writes=["rstd2"])
        OP("vector", lambda e: e.tensor_scalar(
            o1b[:], zc2b[:], stt[:, 11:12], None, ALU.mult),
            reads=["rstd2", "zc2b"], writes=["o1b"])
        OP("vector", lambda e: e.tensor_mul(o2b[:], o1b[:], g_t[:]),
            reads=["o1b", "g"], writes=["o2b"])
        lastz = OP("vector", lambda e, pb_=pb_, gs_=gs_: e.tensor_add(
            outb[pb_][:, HD * gs_:HD * (gs_ + 1)], o2b[:],
            xcs[pb_][:, HD * gs_:HD * (gs_ + 1)]),
            reads=["o2b"] + [f"xcs{pb_}_{hl}" for hl in range(NHL)],
            writes=[f"outb{pb_}"])
        if gs_ == G - 1:
            for hl in range(NHL):
                st = OP("scalar", lambda e, hl=hl, pb_=pb_, n0_=n0_: e.dma_start(
                    out=xcw.rearrange("(n k) (h g) -> k n g h", k=K, h=NHL)
                        [:, n0_:n0_ + G, :, hl],
                    in_=outb[pb_][32 * hl:32 * hl + 16, :].rearrange(
                        "p (n g) -> p n g", g=HD)),
                    reads=[f"outb{pb_}"], writes=[], dma=True,
                    sem=f"stout{pb_}")
                out_stores.append(st)
            wr[f"outb{pb_}"] = out_stores[-1]

    for gr in range(ngr):
        if debug_stop == f"gr{gr}":
            p.final_wait([v for v in wr.values() if v is not None] + out_stores)
            p.build(nc); a.close(); return nc
        pb = gr % 2
        n0 = gr * G
        ld_toks = []
        for hl in range(NHL):
            ld_toks.append(OP("sync", lambda e, hl=hl, pb=pb, n0=n0: e.dma_start(
                out=xbts[pb][0:64, GK * hl:GK * (hl + 1)],
                in_=xbT_s[64 * hl:64 * (hl + 1), K * n0:K * (n0 + G)]),
                reads=[], writes=[f"xbts{pb}_{hl}"], dma=True, sem=f"lsb{pb}",
                extra=[wr["stream"]] + p1_st))
            ld_toks.append(OP("sync", lambda e, hl=hl, pb=pb, n0=n0: e.dma_start(
                out=xcts[pb][0:64, GK * hl:GK * (hl + 1)],
                in_=xcT_s[64 * hl:64 * (hl + 1), K * n0:K * (n0 + G)]),
                reads=[], writes=[f"xcts{pb}_{hl}"], dma=True, sem=f"lsc{pb}",
                extra=[wr["stream"]] + p1_st))
            ld_toks.append(OP("gpsimd", lambda e, hl=hl, pb=pb, n0=n0: e.dma_start(
                out=cbx[pb][32 * hl:32 * hl + 16, :].rearrange(
                    "p (n g) -> p n g", g=65),
                in_=cobx_s[n0:n0 + G, hl].rearrange("n k g -> k n g")),
                reads=[], writes=[f"cbx{pb}_{hl}"], dma=True, sem=f"lcb{pb}",
                extra=[wr["stream"]] + re_toks))
            ld_toks.append(OP("sync", lambda e, hl=hl, pb=pb, n0=n0: e.dma_start(
                out=xas[pb][32 * hl:32 * hl + 16, :].rearrange(
                    "p (n g) -> p n g", g=HD),
                in_=xa2_s[n0:n0 + G, hl].rearrange("n k g -> k n g")),
                reads=[], writes=[f"xas{pb}_{hl}"], dma=True, sem=f"lxa{pb}",
                extra=[wr["stream"]] + re_toks))
            ld_toks.append(OP("sync", lambda e, hl=hl, pb=pb, n0=n0: e.dma_start(
                out=xcs[pb][32 * hl:32 * hl + 16, :].rearrange(
                    "p (n g) -> p n g", g=HD),
                in_=xc2_s[n0:n0 + G, hl].rearrange("n k g -> k n g")),
                reads=[], writes=[f"xcs{pb}_{hl}"], dma=True, sem=f"lxc{pb}",
                extra=[wr["stream"]] + re_toks))
        if debug_stop == "ld0" and gr == 0:
            p.final_wait(ld_toks)
            p.build(nc)
            a.close()
            return nc
        OP("vector", lambda e, pb=pb: e.tensor_tensor(
            xag_g[pb][:], xas[pb][:],
            g_t[:].rearrange("p n -> p n").broadcast_to([128, G, HD])
            if False else g_t[:].rearrange("p (o n) -> p o n", o=1)
                .broadcast_to([128, G, HD]),
            ALU.mult),
            reads=[f"xas{pb}_0", f"xas{pb}_1", f"xas{pb}_2", f"xas{pb}_3", "g"],
            writes=[f"xag{pb}"])
        snames = ([f"xbts{pb}_{hl}" for hl in range(NHL)] +
                  [f"xcts{pb}_{hl}" for hl in range(NHL)] +
                  [f"cbx{pb}_{hl}" for hl in range(NHL)] +
                  [f"xas{pb}_{hl}" for hl in range(NHL)] +
                  [f"xcs{pb}_{hl}" for hl in range(NHL)])
        for gs in range(G):
            for hl in range(NHL):
                OP("tensor", lambda e, hl=hl, pb=pb, gs=gs: e.matmul(
                    ps_z[32 * hl:32 * hl + 16, 0:65],
                    xbts[pb][:, GK * hl + K * gs:GK * hl + K * (gs + 1)],
                    Sb[:, 65 * hl:65 * (hl + 1)],
                    start=True, stop=True, tile_position=(0, 32 * hl)),
                    reads=[f"xbts{pb}_{hl}", "Sb"], writes=[f"psz{hl}"])
            if debug_stop == "z1":
                p.final_wait([v for v in wr.values() if v is not None])
                p.build(nc); a.close(); return nc
            OP("scalar", lambda e: e.activation(
                stt[:, 0:1], ps_z[:, 64:65], AF.Identity, scale=-1.0 / 64),
                reads=["psz0", "psz1", "psz2", "psz3"], writes=["nmu"])
            OP("scalar", lambda e: e.activation(
                zcb[:], ps_z[:, 0:64], AF.Identity, bias=stt[:, 0:1]),
                reads=["nmu", "psz0", "psz1", "psz2", "psz3"], writes=["zcb"])
            OP("scalar", lambda e: e.activation(
                sqsc[:], zcb[:], AF.Square, accum_out=stt[:, 1:2]),
                reads=["zcb"], writes=["sqs", "sqsc"])
            OP("scalar", lambda e: e.activation(
                stt[:, 2:3], stt[:, 1:2], AF.Sqrt, scale=1.0 / 64, bias=eps_ap),
                reads=["sqs"], writes=["std"], extra=[eps_tok])
            OP("vector", lambda e: e.reciprocal(stt[:, 3:4], stt[:, 2:3]),
                reads=["std"], writes=["rstd"])
            OP("vector", lambda e: e.tensor_scalar(
                xhb[:], zcb[:], stt[:, 3:4], None, ALU.mult),
                reads=["rstd", "zcb"], writes=["xhb"])
            OP("vector", lambda e: e.tensor_mul(t5b[:], xhb[:], g2_t[:]),
                reads=["xhb", "g2"], writes=["t5b"])
            OP("vector", lambda e, pb=pb, gs=gs: e.scalar_tensor_tensor(
                grb[:], t5b[:], 1.0, xag_g[pb][:, HD * gs:HD * (gs + 1)],
                ALU.mult, ALU.subtract, accum_out=stt[:, 4:5]),
                reads=["t5b", f"xag{pb}"], writes=["grb", "s1"])
            OP("vector", lambda e: e.scalar_tensor_tensor(
                t4b[:], grb[:], 1.0, xhb[:], ALU.mult, ALU.mult,
                accum_out=stt[:, 5:6]),
                reads=["grb", "xhb"], writes=["t4b", "s2"])
            OP("vector", lambda e: e.tensor_scalar(
                stt[:, 6:7], stt[:, 4:5], stt[:, 3:4], 1.0 / 64, ALU.mult, ALU.mult),
                reads=["s1", "rstd"], writes=["c1"])
            OP("vector", lambda e: e.tensor_scalar(
                stt[:, 7:8], stt[:, 5:6], stt[:, 3:4], 1.0 / 64, ALU.mult, ALU.mult),
                reads=["s2", "rstd"], writes=["c2"])
            OP("vector", lambda e: e.tensor_scalar(
                u1b[:], grb[:], stt[:, 3:4], stt[:, 6:7], ALU.mult, ALU.subtract),
                reads=["grb", "rstd", "c1"], writes=["u1b"])
            OP("vector", lambda e: e.scalar_tensor_tensor(
                ngb[:], xhb[:], stt[:, 7:8], u1b[:], ALU.mult, ALU.subtract),
                reads=["xhb", "c2", "u1b"], writes=["ngb"])
            if debug_stop == "ng":
                p.final_wait([v for v in wr.values() if v is not None])
                p.build(nc); a.close(); return nc
            for hl in range(NHL):
                OP("tensor", lambda e, hl=hl, pb=pb, gs=gs: e.matmul(
                    ps_at[32 * hl:32 * hl + 16, 0:K],
                    xbts[pb][0:64, GK * hl + K * gs:GK * hl + K * (gs + 1)],
                    xcts[pb][0:64, GK * hl + K * gs:GK * hl + K * (gs + 1)],
                    start=True, stop=True, tile_position=(0, 32 * hl)),
                    reads=[f"xbts{pb}_{hl}", f"xcts{pb}_{hl}"],
                    writes=[f"psat{hl}"])
            OP("vector", lambda e, pb=pb, gs=gs: e.tensor_copy(
                stt[:, 12:13], cbx[pb][:, 65 * gs + 64:65 * gs + 65]),
                reads=[f"cbx{pb}_0", f"cbx{pb}_1", f"cbx{pb}_2", f"cbx{pb}_3"],
                writes=["cof"])
            OP("vector", lambda e: e.tensor_scalar(
                mco[:], maskT[:], stt[:, 12:13], None, ALU.mult),
                reads=["maskT", "cof"], writes=["mco"])
            OP("vector", lambda e: e.scalar_tensor_tensor(
                m2b[:], ps_at[:, 0:K], 1.0, mco[:], ALU.add, ALU.mult),
                reads=["mco", "psat0", "psat1", "psat2", "psat3"], writes=["m2b"])
            if debug_stop == "m2":
                p.final_wait([v for v in wr.values() if v is not None])
                p.build(nc); a.close(); return nc
            zpar = (gr * G + gs) % 2
            zbank = ps_zb if zpar == 0 else ps_p1[0]
            for hl in range(NHL):
                OP("tensor", lambda e, hl=hl, pb=pb, gs=gs, zbank=zbank: e.matmul(
                    zbank[32 * hl:32 * hl + 16, 0:65],
                    xcts[pb][:, GK * hl + K * gs:GK * hl + K * (gs + 1)],
                    Sb[:, 65 * hl:65 * (hl + 1)],
                    start=True, stop=False, tile_position=(0, 32 * hl)),
                    reads=[f"xcts{pb}_{hl}", "Sb"], writes=[f"pszb{zpar}_{hl}"])
                OP("tensor", lambda e, hl=hl, zbank=zbank: e.matmul(
                    zbank[32 * hl:32 * hl + 16, 0:64],
                    m2b[32 * hl:32 * hl + 16, :],
                    ngb[32 * hl:32 * hl + 16, :],
                    start=False, stop=True, tile_position=(32 * hl, 32 * hl)),
                    reads=["m2b", "ngb"], writes=[f"pszb{zpar}_{hl}"])
            if debug_stop == "zb":
                p.final_wait([v for v in wr.values() if v is not None])
                p.build(nc); a.close(); return nc
            for hl in range(NHL):
                OP("tensor", lambda e, hl=hl, pb=pb, gs=gs: e.matmul(
                    ps_u[0:65, HD * hl:HD * (hl + 1)],
                    cbx[pb][32 * hl:32 * hl + 16, 65 * gs:65 * (gs + 1)],
                    ngb[32 * hl:32 * hl + 16, :],
                    start=True, stop=True, tile_position=(32 * hl, 0)),
                    reads=[f"cbx{pb}_{hl}", "ngb"] +
                          ([f"psu{hl - 1}"] if hl > 0 else []),
                    writes=[f"psu{hl}"])
            if debug_stop == "u":
                p.final_wait([v for v in wr.values() if v is not None])
                p.build(nc); a.close(); return nc
            OP("vector", lambda e: e.tensor_tensor(
                S_all[:].rearrange("p (h n) -> p h n", h=NHL)[:, :, 0:64],
                S_all[:].rearrange("p (h n) -> p h n", h=NHL)[:, :, 0:64],
                ps_u[0:65, 0:NHL * HD].rearrange("p (h n) -> p h n", h=NHL),
                ALU.add),
                reads=["psu0", "psu1", "psu2", "psu3"], writes=["S_all"])
            OP("vector", lambda e: e.reduce_sum(
                S_all[:].rearrange("p (h n) -> p h n", h=NHL)[:, :, 64:65],
                S_all[:].rearrange("p (h n) -> p h n", h=NHL)[:, :, 0:64],
                axis=AX.X), reads=["S_all"], writes=["S_all"])
            OP("vector", lambda e: e.tensor_copy(Sb[:], S_all[:]),
               reads=["S_all"], writes=["Sb"])
            if pending is not None:
                emit_outln(pending)
            pending = (pb, gs, zpar, n0)

    if pending is not None:
        emit_outln(pending)
    p.final_wait(out_stores)
    p.build(nc)
    a.close()
    return nc


def build_reduce(tc=TC, d=D, nparts=NCORES):
    nt = tc // 128
    nc = bass.Bass("TRN2", target_bir_lowering=False)
    x3 = nc.dram_tensor("x3", [tc, d], F32, kind="ExternalInput").ap()
    pp = nc.dram_tensor("pp", [nparts, tc, d], F32, kind="ExternalInput").ap()
    y = nc.dram_tensor("y", [tc, d], F32, kind="ExternalOutput").ap()
    p = Prog()
    a = Alloc(nc, p)
    acc = [a.sb(f"acc{i}", [128, d]) for i in range(2)]
    pt = [a.sb(f"pt{i}", [128, d]) for i in range(3)]
    prev_acc = [None, None]
    prev_pt = [None, None, None]
    stores = []
    for i in range(nt):
        b = i % 2
        lda = p.op("sync", lambda e, i=i, b=b: e.dma_start(
            out=acc[b][:], in_=x3[128 * i:128 * (i + 1), :]),
            after=[prev_acc[b]], dma=True, sem=f"la{b}")
        tok = lda
        for j in range(nparts):
            pb = (i * nparts + j) % 3
            ldp = p.op("gpsimd" if j % 2 else "sync",
                       lambda e, i=i, j=j, pb=pb: e.dma_start(
                out=pt[pb][:], in_=pp[j, 128 * i:128 * (i + 1), :]),
                after=[prev_pt[pb]], dma=True, sem=f"lp{pb}")
            tok = p.op("vector", lambda e, b=b, pb=pb: e.tensor_add(
                acc[b][:], acc[b][:], pt[pb][:]), after=[ldp, tok])
            prev_pt[pb] = tok
        st = p.op("scalar", lambda e, i=i, b=b: e.dma_start(
            out=y[128 * i:128 * (i + 1), :], in_=acc[b][:]),
            after=[tok], dma=True, sem=f"st{b}")
        prev_acc[b] = st
        stores.append(st)
    p.final_wait(stores)
    p.build(nc)
    a.close()
    return nc


def pi_perm(n):
    perm = np.empty(n, dtype=np.int64)
    ar = 2 * np.arange(128)
    for k in range(n // 256):
        perm[256 * k:256 * k + 128] = 256 * k + ar
        perm[256 * k + 128:256 * k + 256] = 256 * k + ar + 1
    return perm


def prep_l1ffn_core(c, ncores, gate_pk, gate_sc, up_pk, up_sc, down_pk,
                    down_sc, d, inter):
    ic = inter // ncores
    i_lo = c * ic
    pil = pi_perm(ic)
    ws_d = np.arange(d // 2) // 16
    ws_i = (i_lo // 2 + np.arange(ic // 2)) // 16
    o = {}
    o["gpk"] = np.ascontiguousarray(gate_pk[i_lo:i_lo + ic].T[:, pil]).astype(np.int32)
    o["upk"] = np.ascontiguousarray(up_pk[i_lo:i_lo + ic].T[:, pil]).astype(np.int32)
    o["gsc"] = np.ascontiguousarray(gate_sc[i_lo + pil][:, ws_d].T).astype(np.float32)
    o["usc"] = np.ascontiguousarray(up_sc[i_lo + pil][:, ws_d].T).astype(np.float32)
    o["dpk"] = np.ascontiguousarray(down_pk.T[i_lo // 2:(i_lo + ic) // 2, :]).astype(np.int32)
    o["dsc"] = np.ascontiguousarray(down_sc[:, ws_i].T).astype(np.float32)
    return o


F32R = mybir.dt.float32r


def build_l1attn(mode, tc=TC, d=D, nh=32, NZ=64):
    hd = 64
    dn = 64
    CH = 256
    nzl = tc // CH
    ndc = d // 128
    nts = tc // 512
    ntk = tc // 128
    full = (mode == "main")
    nc = bass.Bass("TRN2", target_bir_lowering=False)
    xn3 = nc.dram_tensor("xn3", [tc, d], BF16, kind="ExternalInput").ap()
    bwT = nc.dram_tensor("bwT", [d, dn], F32, kind="ExternalInput").ap()
    cwT = nc.dram_tensor("cwT", [d, dn], F32, kind="ExternalInput").ap()
    dtwT = nc.dram_tensor("dtwT", [d, nh], F32, kind="ExternalInput").ap()
    bb = nc.dram_tensor("bb", [dn, 1], F32, kind="ExternalInput").ap()
    cb = nc.dram_tensor("cb", [dn, 1], F32, kind="ExternalInput").ap()
    dtb = nc.dram_tensor("dtb", [nh, 1], F32, kind="ExternalInput").ap()
    alog = nc.dram_tensor("alog", [nh, 1], F32, kind="ExternalInput").ap()
    if full:
        x2 = nc.dram_tensor("x2", [tc, d], F32, kind="ExternalInput").ap()
        drow = nc.dram_tensor("drow", [1, nh], F32, kind="ExternalInput").ap()
        prevc = nc.dram_tensor("prevc", [nh, nzl, dn * hd], BF16,
                               kind="ExternalInput").ap()
        owT = nc.dram_tensor("owT", [d, d], F32, kind="ExternalInput").ap()
        ob = nc.dram_tensor("ob", [1, d], F32, kind="ExternalInput").ap()
        x3o = nc.dram_tensor("x3", [tc, d], F32, kind="ExternalOutput").ap()
        xn4o = nc.dram_tensor("xn4", [tc, d], BF16, kind="ExternalOutput").ap()
        yts = nc.dram_tensor("ytok_s", [tc, d], BF16, kind="Internal").ap()
    else:
        stex = nc.dram_tensor("stex", [nzl, nh * dn * hd], F32,
                              kind="ExternalOutput").ap()
        dasum = nc.dram_tensor("dasum", [nh, nzl], F32, kind="ExternalOutput").ap()

    p = Prog()
    a = Alloc(nc, p)
    KB = 1024
    ov = Overlay(nc, a, "arena", 174 * KB)
    XT = [ov.at(f"XT{i}", i * tc * 2, [128, tc], BF16) for i in range(ndc)]
    xdr = [ov.at(f"xdr{i}", i * d * 2, [128, d], BF16) for i in range(4)]
    corow = ov.at("corow", 16 * KB, [128, 8 * CH], F32R)
    firow = ov.at("firow", 24 * KB, [128, 8 * CH], F32R)
    prevt = [ov.at(f"prevt{i}", 32 * KB + i * 4 * KB, [dn, nh * hd], BF16)
             for i in range(2)]
    yt = [ov.at(f"yt{i}", 40 * KB + i * d * 2, [128, d], BF16) for i in range(2)]
    scb = [ov.at(f"scb{i}", 48 * KB + i * CH * 2, [128, CH], BF16) for i in range(2)]
    expl = ov.at("expl", 49 * KB, [128, CH], F32)
    exp3 = ov.at("exp3", 50 * KB, [dn, CH], F32)
    mst = [ov.at(f"mst{i}", 51 * KB + i * CH * 2, [128, CH], BF16) for i in range(2)]
    mst3 = ov.at("mst3", 52 * KB, [dn, CH], BF16)
    mneg = [ov.at(f"mneg{i}", 53 * KB + i * CH * 4, [128, CH], F32R) for i in range(2)]
    YT = [ov.at(f"YT{i}", i * tc * 2, [128, tc], BF16) for i in range(ndc)]
    X16 = [ov.at(f"X16{i}", 64 * KB + i * d * 2, [128, d], BF16) for i in range(ntk)]
    ow = [ov.at(f"ow{i}", 64 * KB + i * d * 2, [128, d], BF16) for i in range(ndc)]
    BT = ov.at("BT", 128 * KB, [dn, tc], F32)
    CT = ov.at("CT", 128 * KB + tc * 4, [dn, tc], F32)
    BTb = ov.at("BTb", 128 * KB + tc * 8, [dn, tc], BF16)
    CTb = ov.at("CTb", 128 * KB + tc * 10, [dn, tc], BF16)
    dt_sb = ov.at("dt_sb", 128 * KB + tc * 12, [nh, tc], F32)
    x2t = [ov.at(f"x2t{i}", 128 * KB + i * d * 4, [128, d], F32) for i in range(2)]
    x3t = [ov.at(f"x3t{i}", 128 * KB + 16 * KB + i * d * 4, [128, d], F32)
           for i in range(2)]
    dt_bf = ov.at("dt_bf", 160 * KB, [nh, tc], BF16)
    spr = ov.at("spr", 160 * KB + tc * 2, [nh, 512], F32)
    spab = ov.at("spab", 160 * KB + tc * 2 + 2 * KB, [nh, 512], F32)
    xnt = [ov.at(f"xnt{i}", 160 * KB + i * d * 2, [128, d], BF16) for i in range(2)]
    bw_sb = ov.at("bw_sb", 168 * KB, [128, ndc * dn], BF16)
    cw_sb = ov.at("cw_sb", 168 * KB + ndc * dn * 2, [128, ndc * dn], BF16)
    dtw_sb = ov.at("dtw_sb", 168 * KB + ndc * dn * 4, [128, ndc * nh], BF16)
    stats = ov.at("stats", 168 * KB, [128, ntk * 6], F32)
    sqscr = ov.at("sqscr", 168 * KB + ntk * 6 * 4, [128, d], BF16)
    bvec = a.sb("bvec", [dn, 2])
    dtb_sb = a.sb("dtb_sb", [nh, 1])
    alog_sb = a.sb("alog_sb", [nh, 1])
    na = a.sb("na", [nh, 1])
    ident = a.sb("ident", [128, 128])
    id_bf = a.sb("id_bf", [128, 128], BF16)
    id_r = a.sb("id_r", [128, 128], F32R)
    ones_r = a.sb("ones_r", [128, 128], F32R)
    ones_bf = a.sb("ones_bf", [1, 128], BF16)
    dttok = [a.sb(f"dttok{i}", [128, nh]) for i in range(ntk)]
    cs_c = a.sb("cs_c", [nh, CH])
    zro = a.sb("zro", [nh, CH])
    cofi = a.sb("cofi", [nh, 2 * CH])
    cofi_r = a.sb("cofi_r", [nh, 2 * CH], F32R)
    ctsb = a.sb("ctsb", [128, 4 * nh])
    negct = a.sb("negct", [128, 2 * nh])
    dasb = a.sb("dasb", [nh, max(nzl, 2)])
    modi = a.sb("modi", [nh, CH], I32)
    dbc = a.sb("dbc", [128, nh])
    dbr = a.sb("dbr", [1, nh], F32R)
    obrow = a.sb("obrow", [1, d], BF16)
    if not full:
        Btok = [a.sb(f"Btok{i}", [128, dn], BF16) for i in range(ntk)]
        dcy = a.sb("dcy", [nh, CH], BF16)
        dcyT = a.sb("dcyT", [128, 2 * nh])
        xw = [a.sb(f"xw{i}", [128, dn], BF16) for i in range(2)]
        stx = a.sb("stx", [dn, nh * hd])
    ps = [a.ps(f"ps{i}", [128, 512]) for i in range(6)]
    pst = a.ps("pst", [128, 512])

    ld0 = p.op("gpsimd", lambda e: e.dma_start(
        out=bw_sb[:].rearrange("p (c n) -> p c n", c=ndc),
        in_=bwT.rearrange("(c p) n -> p c n", p=128)), dma=True, sem="ldw0")
    ld1 = p.op("gpsimd", lambda e: e.dma_start(
        out=cw_sb[:].rearrange("p (c n) -> p c n", c=ndc),
        in_=cwT.rearrange("(c p) n -> p c n", p=128)), dma=True, sem="ldw0")
    ld2 = p.op("gpsimd", lambda e: e.dma_start(
        out=dtw_sb[:].rearrange("p (c n) -> p c n", c=ndc),
        in_=dtwT.rearrange("(c p) n -> p c n", p=128)), dma=True, sem="ldw0")
    ld3 = p.op("sync", lambda e: e.dma_start(out=bvec[:, 0:1], in_=bb[:]),
               dma=True, sem="ldw1")
    ld4 = p.op("sync", lambda e: e.dma_start(out=bvec[:, 1:2], in_=cb[:]),
               dma=True, sem="ldw1")
    ld5 = p.op("sync", lambda e: e.dma_start(out=dtb_sb[:], in_=dtb[:]),
               dma=True, sem="ldw1")
    ld6 = p.op("sync", lambda e: e.dma_start(out=alog_sb[:], in_=alog[:]),
               dma=True, sem="ldw1")
    one_ap, one_tok = a.const(1.0)
    mz = p.op("gpsimd", lambda e: e.memset(ident[:], 1.0))
    idt = p.op("gpsimd", lambda e: e.affine_select(
        ident[:], ident[:], [[-1, 128]], ALU.is_equal, 0.0,
        base=0, channel_multiplier=1), after=[mz])
    idb = p.op("vector", lambda e: e.tensor_copy(id_bf[:], ident[:]), after=[idt])
    idr = p.op("vector", lambda e: e.tensor_copy(id_r[:], ident[:]), after=[idt])
    ones_f = a.sb("ones_f", [128, 128])
    _onf = p.op("gpsimd", lambda e: e.memset(ones_f[:], 1.0))
    onr = p.op("vector", lambda e: e.tensor_copy(ones_r[:], ones_f[:]), after=[_onf])
    onb = p.op("vector", lambda e: e.tensor_copy(ones_bf[:], ones_f[0:1, :]), after=[_onf])
    zr0 = p.op("gpsimd", lambda e: e.memset(zro[:], 0.0))
    ea = p.op("scalar", lambda e: e.activation(na[:], alog_sb[:], AF.Exp),
              after=[ld6])
    ea2 = p.op("vector", lambda e: e.tensor_scalar(na[:], na[:], -1.0, None,
                                                   ALU.mult), after=[ea])
    if full:
        ldd = p.op("sync", lambda e: e.dma_start(out=dbc[0:1, :], in_=drow[:]),
                   dma=True, sem="ldw2")
        ldo = p.op("gpsimd", lambda e: e.dma_start(out=obrow[:], in_=ob[:]),
                   dma=True, sem="ldob")
        c2r = p.op("vector", lambda e: e.tensor_copy(dbr[:], dbc[0:1, :]), after=[ldd])
        mmD = p.op("tensor", lambda e: e.matmul(
            pst[:, 0:nh], ones_r[0:1, :], dbr[:], start=True, stop=True),
            after=[c2r, onr])
        evD = p.op("vector", lambda e: e.tensor_copy(dbc[:], pst[:, 0:nh]),
                   after=[mmD])

    tpx = []
    for c in range(ndc):
        tpx.append(p.op("sync", lambda e, c=c: e.dma_start(
            out=XT[c][:, 0:tc], in_=xn3[:, 128 * c:128 * (c + 1)], transpose=True),
            dma=True, sem="ldtp"))
    ldx16 = []
    for i in range(ntk):
        ldx16.append(p.op("gpsimd", lambda e, i=i: e.dma_start(
            out=X16[i][:], in_=xn3[128 * i:128 * (i + 1), :]),
            dma=True, sem="ldx16"))

    bank_busy = {i: None for i in range(6)}
    evB = evC = None
    sp_prev = None
    for ts in range(nts):
        for wi, (wsb, width) in enumerate(((bw_sb, dn), (cw_sb, dn), (dtw_sb, nh))):
            bank = ps[wi % 3]
            mms = []
            for c in range(ndc):
                mm = p.op("tensor", lambda e, wsb=wsb, c=c, ts=ts, width=width,
                          bank=bank: e.matmul(
                    bank[0:width, :], wsb[:, width * c:width * (c + 1)],
                    XT[c][:, 512 * ts:512 * (ts + 1)],
                    start=(c == 0), stop=(c == ndc - 1)),
                    after=[ld0, ld1, ld2, tpx[c]] + ([bank_busy[wi % 3]] if c == 0 else []))
                mms.append(mm)
            if wi == 0:
                evB = p.op("scalar", lambda e, ts=ts, bank=bank: e.activation(
                    BT[:, 512 * ts:512 * (ts + 1)], bank[0:dn, :], AF.Identity,
                    bias=bvec[:, 0:1]), after=[mms[-1], ld3])
                bank_busy[wi % 3] = evB
            elif wi == 1:
                evC = p.op("scalar", lambda e, ts=ts, bank=bank: e.activation(
                    CT[:, 512 * ts:512 * (ts + 1)], bank[0:dn, :], AF.Identity,
                    bias=bvec[:, 1:2]), after=[mms[-1], ld4])
                bank_busy[wi % 3] = evC
            else:
                s1 = p.op("scalar", lambda e, bank=bank: e.activation(
                    spr[0:nh, :], bank[0:nh, :], AF.Relu, bias=dtb_sb[:]),
                    after=[mms[-1], ld5, sp_prev])
                s2 = p.op("scalar", lambda e, bank=bank: e.activation(
                    spab[0:nh, :], bank[0:nh, :], AF.Abs, bias=dtb_sb[:]),
                    after=[mms[-1], s1])
                s3 = p.op("scalar", lambda e: e.activation(
                    spab[0:nh, :], spab[0:nh, :], AF.Exp, scale=-1.0), after=[s2])
                s4 = p.op("scalar", lambda e: e.activation(
                    spab[0:nh, :], spab[0:nh, :], AF.Ln, bias=one_ap[0:nh, 0:1]),
                    after=[s3, one_tok])
                s5 = p.op("vector", lambda e, ts=ts: e.tensor_add(
                    dt_sb[:, 512 * ts:512 * (ts + 1)], spr[0:nh, :], spab[0:nh, :]),
                    after=[s4])
                sp_prev = s5
                bank_busy[wi % 3] = s5
    cb1 = p.op("vector", lambda e: e.tensor_copy(BTb[:], BT[:]), after=[evB])
    cb2 = p.op("vector", lambda e: e.tensor_copy(CTb[:], CT[:]), after=[evC])
    cb3 = p.op("vector", lambda e: e.tensor_copy(dt_bf[:], dt_sb[:]), after=[sp_prev])

    proj_done = [cb1, cb2, cb3, bank_busy[0], bank_busy[1], bank_busy[2]]
    tk_toks = []
    prev_e1 = prev_e2 = None
    for i in range(ntk):
        if not full:
            m1 = p.op("tensor", lambda e, i=i: e.matmul(
                pst[:, 0:dn], BTb[:, 128 * i:128 * (i + 1)], id_bf[0:dn, 0:dn],
                start=True, stop=True), after=proj_done + [idb, prev_e1])
            prev_e1 = p.op("vector", lambda e, i=i: e.tensor_copy(
                Btok[i][:], pst[:, 0:dn]), after=[m1])
        m2 = p.op("tensor", lambda e, i=i: e.matmul(
            pst[:, 64:64 + nh], dt_bf[:, 128 * i:128 * (i + 1)], id_bf[0:nh, 0:nh],
            start=True, stop=True), after=proj_done + [idb, prev_e2, prev_e1])
        prev_e2 = p.op("vector", lambda e, i=i: e.tensor_copy(
            dttok[i][:], pst[:, 64:64 + nh]), after=[m2])
        tk_toks.append(prev_e2)

    if full:
        mk_toks = []
        mnegf = a.sb("mnegf", [128, CH])
        for half in range(2):
            mz2 = p.op("gpsimd", lambda e: e.memset(mnegf[:], 0.0),
                       after=proj_done + tk_toks + mk_toks)
            mk0 = p.op("gpsimd", lambda e, half=half: e.affine_select(
                mnegf[:], mnegf[:], [[1, CH]], ALU.is_ge, -1e30,
                base=-128 * half, channel_multiplier=-1), after=[mz2])
            mk = p.op("vector", lambda e, half=half: e.tensor_copy(
                mneg[half][:], mnegf[:]), after=[mk0])
            mk_toks.append(mk)

    st_stores = []
    y_stores = []
    prev_stx = None
    prev_da = None
    last_chunk_toks = list(tk_toks) + proj_done
    pv_busy = [None, None]
    yt_busy = [None, None]
    xdr_busy = [None, None, None, None]
    for z in range(nzl):
        deps = last_chunk_toks
        c0 = CH * z
        xa, xb_ = xdr[2 * (z % 2)], xdr[2 * (z % 2) + 1]
        xd_toks = []
        for li, xti in ((0, xa), (1, xb_)):
            for h in range(nh):
                tkx = p.op("vector", lambda e, z=z, li=li, h=h, xti=xti: e.tensor_scalar(
                    xti[:, hd * h:hd * (h + 1)],
                    X16[2 * z + li][:, hd * h:hd * (h + 1)],
                    dttok[2 * z + li][:, h:h + 1], None, ALU.mult),
                    after=[tk_toks[2 * z + li], ldx16[2 * z + li],
                           xdr_busy[2 * (z % 2) + li]] +
                          ([] if z >= 2 else (tpx if full else tpx)) +
                          ([deps[-1]] if z == 0 else []))
                xd_toks.append(tkx)
        xdt_done = xd_toks[-1]
        t1 = p.op("vector", lambda e, c0=c0: e.tensor_scalar(
            cs_c[:], dt_sb[:, c0:c0 + CH], na[0:nh, 0:1], None, ALU.mult),
            after=deps + [ea2])
        t2 = p.op("vector", lambda e: e.tensor_tensor_scan(
            cs_c[:], cs_c[:], zro[:, 0:CH], 0.0, ALU.add, ALU.add),
            after=[t1, zr0])
        if not full:
            t3 = p.op("scalar", lambda e: e.activation(
                dcy[:], cs_c[:], AF.Exp, scale=-1.0, bias=cs_c[:, CH - 1:CH]),
                after=[t2])
            t4 = p.op("vector", lambda e, z=z: e.tensor_copy(
                dasb[:, z:z + 1], cs_c[:, CH - 1:CH]), after=[t2, prev_da])
            prev_da = t4
        else:
            t5a = p.op("vector", lambda e: e.tensor_scalar(
                cofi[:, CH:2 * CH], cs_c[:], 1.0 / 64.0, None, ALU.mult),
                after=[t2])
            t5b = p.op("vector", lambda e: e.tensor_copy(
                modi[:, 0:CH], cofi[:, CH:2 * CH]), after=[t5a])
            t5c = p.op("vector", lambda e: e.tensor_copy(
                cofi[:, CH:2 * CH], modi[:, 0:CH]), after=[t5b])
            t6a = p.op("vector", lambda e: e.tensor_scalar(
                cofi[:, 0:CH], cofi[:, CH:2 * CH], 64.0, None, ALU.mult),
                after=[t5c])
            t6 = p.op("vector", lambda e: e.tensor_sub(
                cofi[:, CH:2 * CH], cs_c[:], cofi[:, 0:CH]), after=[t6a])
            t7 = p.op("vector", lambda e: e.tensor_copy(cofi_r[:], cofi[:]),
                      after=[t6])
            clps = []
            for gi in range(nh // 8):
                for (buf, off) in ((corow, 0), (firow, CH)):
                    clp = p.op("gpsimd", lambda e, gi=gi, buf=buf, off=off: e.dma_start(
                        out=buf[32 * gi:32 * gi + 1, :].rearrange(
                            "o (h n) -> o h n", h=8),
                        in_=cofi_r[8 * gi:8 * (gi + 1), off:off + CH]),
                        after=[t7], dma=True, sem=f"clp{gi % 2}")
                    clps.append(clp)
            tms = []
            for q in range(4):
                part, half = q // 2, q % 2
                tm = p.op("tensor", lambda e, part=part, half=half, q=q: e.matmul(
                    pst[:, 512 - 4 * nh + nh * q:512 - 4 * nh + nh * (q + 1)],
                    cofi_r[:, CH * part + 128 * half:CH * part + 128 * (half + 1)],
                    id_r[0:nh, 0:nh], start=True, stop=True),
                    after=[t7, idr] + tms)
                tms.append(tm)
            tA = p.op("vector", lambda e: e.tensor_copy(
                ctsb[:, 0:4 * nh], pst[:, 512 - 4 * nh:512]), after=tms)
            ncg = []
            for half in range(2):
                ncg.append(p.op("vector", lambda e, half=half: e.scalar_tensor_tensor(
                    negct[:, nh * half:nh * (half + 1)],
                    ctsb[:, nh * half:nh * (half + 1)], -1.0,
                    ctsb[:, 2 * nh + nh * half:2 * nh + nh * (half + 1)],
                    ALU.mult, ALU.subtract), after=[tA]))
        if not full:
            dts = []
            for half in range(2):
                tm = p.op("tensor", lambda e, half=half: e.matmul(
                    pst[:, 256 + nh * half:256 + nh * (half + 1)],
                    dcy[:, 128 * half:128 * (half + 1)], id_bf[0:nh, 0:nh],
                    start=True, stop=True), after=[t3, idb] + dts)
                dts.append(tm)
            dtv = p.op("vector", lambda e: e.tensor_copy(
                dcyT[:], pst[:, 256:256 + 2 * nh]), after=dts)
            evs = []
            for h in range(nh):
                w1 = p.op("vector", lambda e, h=h, xa=xa: e.tensor_scalar(
                    xw[0][:], xa[:, hd * h:hd * (h + 1)],
                    dcyT[:, h:h + 1], None, ALU.mult),
                    after=[dtv, xdt_done] + evs[-1:])
                w2 = p.op("vector", lambda e, h=h, xb_=xb_: e.tensor_scalar(
                    xw[1][:], xb_[:, hd * h:hd * (h + 1)],
                    dcyT[:, nh + h:nh + h + 1], None, ALU.mult),
                    after=[dtv, xdt_done] + evs[-1:])
                m1 = p.op("tensor", lambda e, z=z: e.matmul(
                    pst[0:dn, 0:hd], Btok[2 * z][:], xw[0][:],
                    start=True, stop=False), after=[w1, prev_e1])
                m2 = p.op("tensor", lambda e, z=z: e.matmul(
                    pst[0:dn, 0:hd], Btok[2 * z + 1][:], xw[1][:],
                    start=False, stop=True), after=[w2])
                ev = p.op("vector", lambda e, h=h: e.tensor_copy(
                    stx[:, hd * h:hd * (h + 1)], pst[0:dn, 0:hd]),
                    after=[m2, prev_stx])
                evs.append(ev)
            sst = p.op("scalar", lambda e, z=z: e.dma_start(
                out=stex[z].rearrange("(h n q) -> n h q", h=nh, n=dn),
                in_=stx[:].rearrange("n (h q) -> n h q", h=nh)),
                after=evs, dma=True, sem="stst")
            st_stores.append(sst)
            prev_stx = sst
            xdr_busy[2 * (z % 2)] = evs[-1]
            xdr_busy[2 * (z % 2) + 1] = evs[-1]
            last_chunk_toks = [evs[-1], t2]
            continue
        sc_toks = []
        for half in range(2):
            mm = p.op("tensor", lambda e, half=half, c0=c0: e.matmul(
                ps[4 + half][:, 0:CH], BTb[:, c0 + 128 * half:c0 + 128 * (half + 1)],
                CTb[:, c0:c0 + CH], start=True, stop=True),
                after=[cb1, cb2] + ([bank_busy[4 + half]] if bank_busy[4 + half] else []))
            evv = p.op("vector", lambda e, half=half: e.tensor_copy(
                scb[half][:], ps[4 + half][:, 0:CH]),
                after=[mm] + y_stores[-2:])
            bank_busy[4 + half] = evv
            sc_toks.append(evv)
        bpv = z % 2
        ldpv = p.op("sync", lambda e, z=z, bpv=bpv: e.dma_start(
            out=prevt[bpv][:].rearrange("n (h q) -> n h q", h=nh),
            in_=prevc[:, z, :].rearrange("h (n q) -> n h q", n=dn)),
            after=[pv_busy[bpv]] + (deps if z < 2 else []),
            dma=True, sem=f"ldpv{bpv}")
        ymms_last = None
        prev_ms = None
        prev_e3 = None
        for h in range(nh):
            gi, gc = h // 8, h % 8
            eL = []
            for half in range(2):
                tp_ = (96, 0) if gi == 3 else None
                b1 = p.op("tensor", lambda e, gi=gi, gc=gc, half=half, tp_=tp_: e.matmul(
                    ps[half][:, 0:CH], ones_r[32 * gi:32 * gi + 1, :],
                    corow[32 * gi:32 * gi + 1, CH * gc:CH * (gc + 1)],
                    start=True, stop=False, tile_position=tp_),
                    after=clps + [onr] + ([bank_busy[half]] if bank_busy[half] else []))
                b2 = p.op("tensor", lambda e, gi=gi, gc=gc, half=half, tp_=tp_: e.matmul(
                    ps[half][:, 0:CH], ones_r[32 * gi:32 * gi + 1, :],
                    firow[32 * gi:32 * gi + 1, CH * gc:CH * (gc + 1)],
                    start=False, stop=False, tile_position=tp_), after=[])
                b3 = p.op("tensor", lambda e, half=half: e.matmul(
                    ps[half][:, 0:CH], id_r[:], mneg[half][:],
                    start=False, stop=True), after=[mk_toks[half]])
                ex = p.op("scalar", lambda e, h=h, half=half: e.activation(
                    expl[:], ps[half][:, 0:CH], AF.Exp,
                    bias=negct[:, nh * half + h:nh * half + h + 1]),
                    after=[b3, prev_ms] + ncg)
                ms = p.op("vector", lambda e, half=half: e.tensor_mul(
                    mst[half][:], scb[half][:], expl[:]),
                    after=[ex, sc_toks[half]])
                prev_ms = ms
                bank_busy[half] = ms
                eL.append(ms)
            tp_ = (96, 0) if gi == 3 else None
            b4 = p.op("tensor", lambda e, gi=gi, gc=gc, tp_=tp_: e.matmul(
                ps[2][0:dn, 0:CH], ones_r[32 * gi:32 * gi + 1, 0:dn],
                corow[32 * gi:32 * gi + 1, CH * gc:CH * (gc + 1)],
                start=True, stop=False, tile_position=tp_),
                after=clps + ([bank_busy[2]] if bank_busy[2] else []))
            b5 = p.op("tensor", lambda e, gi=gi, gc=gc, tp_=tp_: e.matmul(
                ps[2][0:dn, 0:CH], ones_r[32 * gi:32 * gi + 1, 0:dn],
                firow[32 * gi:32 * gi + 1, CH * gc:CH * (gc + 1)],
                start=False, stop=True, tile_position=tp_), after=[])
            ex3 = p.op("scalar", lambda e: e.activation(
                exp3[:], ps[2][0:dn, 0:CH], AF.Exp), after=[b5, prev_e3])
            ms3 = p.op("vector", lambda e, c0=c0: e.tensor_mul(
                mst3[:], CT[:, c0:c0 + CH], exp3[:]), after=[ex3, evC])
            prev_e3 = ms3
            bank_busy[2] = ms3
            for lqh in range(2):
                ya = p.op("tensor", lambda e, h=h, lqh=lqh, xa=xa: e.matmul(
                    ps[3][:, 0:hd], mst[0][:, 128 * lqh:128 * (lqh + 1)],
                    xa[:, hd * h:hd * (h + 1)], start=True, stop=False),
                    after=[eL[0], xdt_done] + ([bank_busy[3]] if bank_busy[3] else []))
                yb = p.op("tensor", lambda e, h=h, lqh=lqh, xb_=xb_: e.matmul(
                    ps[3][:, 0:hd], mst[1][:, 128 * lqh:128 * (lqh + 1)],
                    xb_[:, hd * h:hd * (h + 1)], start=False, stop=False),
                    after=[eL[1]])
                yc = p.op("tensor", lambda e, h=h, lqh=lqh, bpv=bpv: e.matmul(
                    ps[3][:, 0:hd], mst3[:, 128 * lqh:128 * (lqh + 1)],
                    prevt[bpv][:, hd * h:hd * (h + 1)], start=False, stop=True),
                    after=[ms3, ldpv])
                yev = p.op("vector", lambda e, z=z, h=h, lqh=lqh: e.scalar_tensor_tensor(
                    yt[lqh][:, hd * h:hd * (h + 1)],
                    X16[2 * z + lqh][:, hd * h:hd * (h + 1)],
                    dbc[0:128, h:h + 1], ps[3][:, 0:hd],
                    ALU.mult, ALU.add),
                    after=[yc, evD] + ([yt_busy[lqh]] if h == 0 else []))
                bank_busy[3] = yev
                ymms_last = yev
                if h == nh - 1:
                    yst = p.op("scalar", lambda e, z=z, lqh=lqh: e.dma_start(
                        out=yts[128 * (2 * z + lqh):128 * (2 * z + lqh + 1), :],
                        in_=yt[lqh][:]), after=[yev], dma=True, sem=f"sty{lqh}")
                    y_stores.append(yst)
                    yt_busy[lqh] = yst
        pv_busy[bpv] = ymms_last
        xdr_busy[2 * (z % 2)] = ymms_last
        xdr_busy[2 * (z % 2) + 1] = ymms_last
        last_chunk_toks = [ymms_last]

    if not full:
        sda = p.op("scalar", lambda e: e.dma_start(
            out=dasum[:, 0:nzl], in_=dasb[:, 0:nzl]),
            after=[prev_da] + st_stores, dma=True, sem="stda")
        p.final_wait(st_stores + [sda])
        p.build(nc)
        a.close()
        return nc

    ldow_toks = []
    for c in range(ndc):
        ldow_toks.append(p.op("gpsimd", lambda e, c=c: e.dma_start(
            out=ow[c][:], in_=owT[128 * c:128 * (c + 1), :]),
            after=y_stores, dma=True, sem="ldow"))
    ytp = []
    for c in range(ndc):
        ytp.append(p.op("sync", lambda e, c=c: e.dma_start(
            out=YT[c][:, 0:tc], in_=yts[:, 128 * c:128 * (c + 1)], transpose=True),
            after=y_stores, dma=True, sem="ldytp"))
    prev_x2 = [None, None]
    prev_x3 = [None, None]
    prev_xn = [None, None]
    prev_sq = None
    out_stores = []
    for i in range(ntk):
        b = i % 2
        ldx2 = p.op("sync", lambda e, i=i, b=b: e.dma_start(
            out=x2t[b][:], in_=x2[128 * i:128 * (i + 1), :]),
            after=[prev_x2[b]] + (proj_done + y_stores if i < 2 else []),
            dma=True, sem=f"ldx2{b}")
        for s in range(d // 512):
            bank = ps[s % 2]
            mms = []
            for c in range(ndc):
                mm = p.op("tensor", lambda e, i=i, s=s, c=c, bank=bank: e.matmul(
                    bank[:, 0:512], YT[c][:, 128 * i:128 * (i + 1)],
                    ow[c][:, 512 * s:512 * (s + 1)],
                    start=(c == 0), stop=False),
                    after=[ytp[c], ldow_toks[c]] + ([bank_busy[s % 2]] if c == 0 else []))
                mms.append(mm)
            mob = p.op("tensor", lambda e, i=i, s=s, bank=bank: e.matmul(
                bank[:, 0:512], ones_bf[:],
                obrow[0:1, 512 * s:512 * (s + 1)], start=False, stop=True),
                after=[onb, ldo])
            ev2 = p.op("vector", lambda e, s=s, b=b, bank=bank: e.tensor_add(
                x3t[b][:, 512 * s:512 * (s + 1)], bank[:, 0:512],
                x2t[b][:, 512 * s:512 * (s + 1)]),
                after=[mob, ldx2, prev_x3[b]])
            bank_busy[s % 2] = ev2
            prev_x3[b] = ev2
        stx3 = p.op("scalar", lambda e, i=i, b=b: e.dma_start(
            out=x3o[128 * i:128 * (i + 1), :], in_=x3t[b][:]),
            after=[prev_x3[b]], dma=True, sem=f"stx3{b}")
        t8, prev_sq = emit_ln(p, a, x3t[b], xnt[b], i, stats, sqscr,
                              [prev_x3[b], prev_xn[b]], prev_sq, d=d)
        stxn = p.op("scalar", lambda e, i=i, b=b: e.dma_start(
            out=xn4o[128 * i:128 * (i + 1), :], in_=xnt[b][:]),
            after=[t8], dma=True, sem=f"stxn4{b}")
        prev_xn[b] = stxn
        prev_x2[b] = ev2
        prev_x3[b] = stx3
        out_stores += [stx3, stxn]
    p.final_wait(out_stores)
    p.build(nc)
    a.close()
    return nc


def build_l1prev(nh=32, NZ=64, dn=64, hd=64):
    nc = bass.Bass("TRN2", target_bir_lowering=False)
    stin = nc.dram_tensor("stin", [nh, NZ, dn * hd], F32, kind="ExternalInput").ap()
    dasT = nc.dram_tensor("dasT", [nh, NZ], F32, kind="ExternalInput").ap()
    prevo = nc.dram_tensor("prev", [nh, NZ, dn * hd], BF16, kind="ExternalOutput").ap()
    p = Prog()
    a = Alloc(nc, p)
    das = a.sb("das", [nh, NZ])
    zro = a.sb("zro", [nh, NZ])
    css = a.sb("css", [nh, NZ + 1])
    cofi = a.sb("cofi", [nh, 2 * (NZ + 1)])
    cofi_r = a.sb("cofi_r", [nh, 2 * (NZ + 1)], F32R)
    corow = a.sb("corow", [1, nh * (NZ + 1)], F32R)
    firow = a.sb("firow", [1, nh * (NZ + 1)], F32R)
    ctsb = a.sb("ctsb", [NZ + 1, 2 * nh])
    negct = a.sb("negct", [NZ + 1, nh])
    mneg = a.sb("mneg", [NZ + 1, NZ], F32R)
    modi = a.sb("modi", [nh, NZ + 1], I32)
    ident = a.sb("ident", [128, 128])
    id_r = a.sb("id_r", [128, 128], F32R)
    ones_r = a.sb("ones_r", [1, 128], F32R)
    wt = a.sb("wt", [NZ + 1, NZ], BF16)
    sext = a.sb("sext", [NZ + 1, dn * hd], BF16)
    pv = a.sb("pv", [NZ, dn * hd], BF16)
    pst = a.ps("pst", [128, 512])
    ps2 = a.ps("ps2", [128, 512])

    ld1 = p.op("gpsimd", lambda e: e.dma_start(out=das[:], in_=dasT[:]),
               dma=True, sem="ld")
    zr = p.op("gpsimd", lambda e: e.memset(zro[:], 0.0))
    z1 = p.op("gpsimd", lambda e: e.memset(css[:, 0:1], 0.0))
    t2 = p.op("vector", lambda e: e.tensor_tensor_scan(
        css[:, 1:NZ + 1], das[:], zro[:], 0.0, ALU.add, ALU.add),
        after=[ld1, zr])
    t5a = p.op("vector", lambda e: e.tensor_scalar(
        cofi[:, NZ + 1:], css[:], 1.0 / 64.0, None, ALU.mult), after=[t2, z1])
    t5b = p.op("vector", lambda e: e.tensor_copy(modi[:], cofi[:, NZ + 1:]),
               after=[t5a])
    t5c = p.op("vector", lambda e: e.tensor_copy(cofi[:, NZ + 1:], modi[:]),
               after=[t5b])
    t6a = p.op("vector", lambda e: e.tensor_scalar(
        cofi[:, 0:NZ + 1], cofi[:, NZ + 1:], 64.0, None, ALU.mult), after=[t5c])
    t6 = p.op("vector", lambda e: e.tensor_sub(
        cofi[:, NZ + 1:], css[:], cofi[:, 0:NZ + 1]), after=[t6a])
    t7 = p.op("vector", lambda e: e.tensor_copy(cofi_r[:], cofi[:]), after=[t6])
    t8 = p.op("gpsimd", lambda e: e.dma_start(
        out=corow[0:1, :].rearrange("o (h n) -> o h n", h=nh),
        in_=cofi_r[:, 0:NZ + 1]), after=[t7], dma=True, sem="clp0")
    t9 = p.op("gpsimd", lambda e: e.dma_start(
        out=firow[0:1, :].rearrange("o (h n) -> o h n", h=nh),
        in_=cofi_r[:, NZ + 1:]), after=[t7], dma=True, sem="clp1")
    mz = p.op("gpsimd", lambda e: e.memset(ident[:], 1.0))
    idt = p.op("gpsimd", lambda e: e.affine_select(
        ident[:], ident[:], [[-1, 128]], ALU.is_equal, 0.0,
        base=0, channel_multiplier=1), after=[mz])
    idr = p.op("vector", lambda e: e.tensor_copy(id_r[:], ident[:]), after=[idt])
    ones_f = a.sb("ones_f", [1, 128])
    mnegf = a.sb("mnegf", [NZ + 1, NZ])
    _onf = p.op("gpsimd", lambda e: e.memset(ones_f[:], 1.0))
    onr = p.op("vector", lambda e: e.tensor_copy(ones_r[:], ones_f[:]), after=[_onf])
    mz2 = p.op("gpsimd", lambda e: e.memset(mnegf[:], 0.0))
    mk0 = p.op("gpsimd", lambda e: e.affine_select(
        mnegf[:], mnegf[:], [[1, NZ]], ALU.is_ge, -1e30,
        base=0, channel_multiplier=-1), after=[mz2])
    mk = p.op("vector", lambda e: e.tensor_copy(mneg[:], mnegf[:]), after=[mk0])
    tm1 = p.op("tensor", lambda e: e.matmul(
        pst[0:NZ + 1, 0:nh], cofi_r[:, 0:NZ + 1], id_r[0:nh, 0:nh],
        start=True, stop=True), after=[t7, idr])
    tm2 = p.op("tensor", lambda e: e.matmul(
        pst[0:NZ + 1, nh:2 * nh], cofi_r[:, NZ + 1:], id_r[0:nh, 0:nh],
        start=True, stop=True), after=[t7, idr, tm1])
    tA = p.op("vector", lambda e: e.tensor_copy(
        ctsb[:], pst[0:NZ + 1, 0:2 * nh]), after=[tm1, tm2])
    ncg = p.op("vector", lambda e: e.scalar_tensor_tensor(
        negct[:], ctsb[:, 0:nh], -1.0, ctsb[:, nh:2 * nh],
        ALU.mult, ALU.subtract), after=[tA])
    stores = []
    prev_ld = None
    prev_wt = None
    for h in range(nh):
        b1 = p.op("tensor", lambda e, h=h: e.matmul(
            ps2[0:NZ + 1, 0:NZ], ones_r[0:1, 0:NZ + 1],
            corow[:, (NZ + 1) * h:(NZ + 1) * h + NZ], start=True, stop=False),
            after=[t8, onr, prev_wt])
        b2 = p.op("tensor", lambda e, h=h: e.matmul(
            ps2[0:NZ + 1, 0:NZ], ones_r[0:1, 0:NZ + 1],
            firow[:, (NZ + 1) * h:(NZ + 1) * h + NZ], start=False, stop=False),
            after=[t9])
        b3 = p.op("tensor", lambda e: e.matmul(
            ps2[0:NZ + 1, 0:NZ], id_r[0:NZ + 1, 0:NZ + 1], mneg[:],
            start=False, stop=True), after=[mk])
        exw = p.op("scalar", lambda e, h=h: e.activation(
            wt[:], ps2[0:NZ + 1, 0:NZ], AF.Exp, bias=negct[:, h:h + 1]),
            after=[b3, ncg, prev_wt])
        prev_wt = exw
        z0m = p.op("gpsimd", lambda e: e.memset(sext[0:1, :], 0.0),
                   after=[prev_ld])
        ldst = p.op("gpsimd", lambda e, h=h: e.dma_start(
            out=sext[1:NZ + 1, :], in_=stin[h]), after=[z0m], dma=True, sem="ldst")
        evs = []
        for s in range(dn * hd // 512):
            mm = p.op("tensor", lambda e, s=s: e.matmul(
                pst[0:NZ, 0:512], wt[:], sext[:, 512 * s:512 * (s + 1)],
                start=True, stop=True), after=[exw, ldst] + evs)
            ev = p.op("vector", lambda e, s=s: e.tensor_copy(
                pv[:, 512 * s:512 * (s + 1)], pst[0:NZ, 0:512]),
                after=[mm] + (stores[-1:] if stores else []))
            evs.append(ev)
        st = p.op("scalar", lambda e, h=h: e.dma_start(
            out=prevo[h], in_=pv[:]), after=evs, dma=True, sem="stpv")
        stores.append(st)
        prev_ld = evs[-1]
    p.final_wait(stores)
    p.build(nc)
    a.close()
    return nc


_BUILD_CACHE = {}


def _cached(name, fn):
    if name not in _BUILD_CACHE:
        _BUILD_CACHE[name] = fn()
    return _BUILD_CACHE[name]


def kernel(**inputs):
    ii = {k: np.asarray(v) for k, v in inputs.items()}
    x = ii["L0_attn_hidden_states"][0].astype(np.float32)
    cores = list(range(NCORES))

    ncA = _cached("A", build_ln_launch)
    res = run_bass_kernel_spmd(ncA, [
        {"x": np.ascontiguousarray(x[c * TC:(c + 1) * TC])} for c in cores],
        core_ids=cores)
    ln1 = np.concatenate([res.results[c]["y"] for c in cores], 0)

    ln1T = np.ascontiguousarray(ln1.T)
    Wq, Wk, Wv = ii["L0_attn_Wq"], ii["L0_attn_Wk"], ii["L0_attn_Wv"]
    lwf = ii["L0_attn_linear_weight"][:, 0, :]
    lbf = ii["L0_attn_linear_bias"][:, 0]
    lnw, lnb = ii["L0_attn_ln_weight"], ii["L0_attn_ln_bias"]
    W1i, b1i = ii["L0_attn_W1_init"], ii["L0_attn_b1_init"]
    ncB = _cached("B", build_l0attn)
    mapsB = []
    for c in cores:
        hs = slice(4 * c, 4 * c + 4)
        lnw_rep = np.zeros((128, 64), np.float32)
        for hl in range(4):
            lnw_rep[32 * hl:32 * hl + 16] = lnw[4 * c + hl]
        s0 = np.zeros((65, 256), np.float32)
        for hl in range(4):
            s0[0:64, 64 * hl:64 * (hl + 1)] = W1i[4 * c + hl]
            s0[64, 64 * hl:64 * (hl + 1)] = b1i[4 * c + hl, 0]
        mapsB.append({
            "ln1T": ln1T,
            "wqT": np.ascontiguousarray(Wq[256 * c:256 * (c + 1)].T).astype(np.float32),
            "wkT": np.ascontiguousarray(Wk[256 * c:256 * (c + 1)].T).astype(np.float32),
            "wvT": np.ascontiguousarray(Wv[256 * c:256 * (c + 1)].T).astype(np.float32),
            "lwT4": np.ascontiguousarray(lwf[hs].T).astype(np.float32),
            "lb4": np.ascontiguousarray(lbf[hs].reshape(4, 1)).astype(np.float32),
            "lnw_rep": lnw_rep,
            "lnb_row": np.ascontiguousarray(lnb[hs].reshape(1, 256)).astype(np.float32),
            "s0": s0,
        })
    res = run_bass_kernel_spmd(ncB, mapsB, core_ids=cores)
    XCW = np.concatenate([np.asarray(res.results[c]["xcw"]) for c in cores], 1)

    ncC = _cached("C", build_l0tail)
    wo_t = np.ascontiguousarray(ii["L0_attn_Wo"].T).astype(np.float32)
    gw_t = np.ascontiguousarray(ii["L0_ffn_gate_w"].T).astype(np.float32)
    uw_t = np.ascontiguousarray(ii["L0_ffn_up_w"].T).astype(np.float32)
    dw_t = np.ascontiguousarray(ii["L0_ffn_down_w"].T).astype(np.float32)
    gs = np.ascontiguousarray(ii["L0_ffn_gate_s"].reshape(-1, 128).T).astype(np.float32)
    us = np.ascontiguousarray(ii["L0_ffn_up_s"].reshape(-1, 128).T).astype(np.float32)
    ds = ii["L0_ffn_down_s"].reshape(1, -1).astype(np.float32)
    mapsC = [{
        "x": np.ascontiguousarray(x[c * TC:(c + 1) * TC]),
        "xcw": np.ascontiguousarray(XCW[c * TC:(c + 1) * TC]),
        "wo_t": wo_t, "gw_t": gw_t, "uw_t": uw_t, "dw_t": dw_t,
        "gs": gs, "us": us, "ds": ds} for c in cores]
    res = run_bass_kernel_spmd(ncC, mapsC, core_ids=cores)
    x2 = np.concatenate([np.asarray(res.results[c]["x2"]) for c in cores], 0)
    xn3 = np.concatenate([np.asarray(res.results[c]["xn3"]) for c in cores], 0)

    bwT = np.ascontiguousarray(ii["L1_attn_B_w"].T).astype(np.float32)
    cwT = np.ascontiguousarray(ii["L1_attn_C_w"].T).astype(np.float32)
    dtwT = np.ascontiguousarray(ii["L1_attn_dt_w"].T).astype(np.float32)
    bb = ii["L1_attn_B_b"].reshape(-1, 1).astype(np.float32)
    cb = ii["L1_attn_C_b"].reshape(-1, 1).astype(np.float32)
    dtb = ii["L1_attn_dt_b"].reshape(-1, 1).astype(np.float32)
    alog = ii["L1_attn_A_log"].reshape(-1, 1).astype(np.float32)
    baseD = {"bwT": bwT, "cwT": cwT, "dtwT": dtwT, "bb": bb, "cb": cb,
             "dtb": dtb, "alog": alog}
    ncD1 = _cached("D1", lambda: build_l1attn("states"))
    mapsD1 = [dict(baseD, xn3=np.ascontiguousarray(xn3[c * TC:(c + 1) * TC]))
              for c in cores]
    res = run_bass_kernel_spmd(ncD1, mapsD1, core_ids=cores)
    nh, dn, hd, nzl = 32, 64, 64, TC // 256
    NZ = NCORES * nzl
    states = np.zeros((nh, NZ, dn * hd), np.float32)
    dasT = np.zeros((nh, NZ), np.float32)
    for c in cores:
        st = np.asarray(res.results[c]["stex"]).reshape(nzl, nh, dn * hd)
        states[:, nzl * c:nzl * (c + 1), :] = st.transpose(1, 0, 2)
        dasT[:, nzl * c:nzl * (c + 1)] = np.asarray(res.results[c]["dasum"])

    ncD2a = _cached("D2a", lambda: build_l1prev(nh=nh, NZ=NZ))
    res = run_bass_kernel_spmd(ncD2a, [{"stin": states, "dasT": dasT}] * NCORES,
                               core_ids=cores)
    prev = np.asarray(res.results[0]["prev"])

    owT = np.ascontiguousarray(ii["L1_attn_out_w"].T).astype(np.float32)
    ob = ii["L1_attn_out_b"].reshape(1, -1).astype(np.float32)
    drow = ii["L1_attn_D"].reshape(1, -1).astype(np.float32)
    ncD2b = _cached("D2b", lambda: build_l1attn("main"))
    mapsD2 = [dict(baseD,
                   xn3=np.ascontiguousarray(xn3[c * TC:(c + 1) * TC]),
                   x2=np.ascontiguousarray(x2[c * TC:(c + 1) * TC]),
                   drow=drow, owT=owT, ob=ob,
                   prevc=np.ascontiguousarray(prev[:, nzl * c:nzl * (c + 1), :]))
              for c in cores]
    res = run_bass_kernel_spmd(ncD2b, mapsD2, core_ids=cores)
    x3 = np.concatenate([np.asarray(res.results[c]["x3"]) for c in cores], 0)
    xn4 = np.concatenate([np.asarray(res.results[c]["xn4"]) for c in cores], 0)

    pid = pi_perm(D)
    xn4T = np.ascontiguousarray(xn4.T[pid, :])
    ncE = _cached("E", lambda: build_l1ffn())
    mapsE = []
    for c in cores:
        m = {"xn4T": xn4T}
        m.update(prep_l1ffn_core(c, NCORES, ii["L1_ffn_gate_W_packed"],
                                 ii["L1_ffn_gate_scales"],
                                 ii["L1_ffn_up_W_packed"], ii["L1_ffn_up_scales"],
                                 ii["L1_ffn_down_W_packed"],
                                 ii["L1_ffn_down_scales"], D, 16384))
        mapsE.append(m)
    res = run_bass_kernel_spmd(ncE, mapsE, core_ids=cores)
    parts = [np.asarray(res.results[c]["part"]) for c in cores]

    ncF = _cached("F", build_reduce)
    mapsF = []
    for c in cores:
        sl = slice(c * TC, (c + 1) * TC)
        pp = np.stack([pt[sl] for pt in parts], 0)
        mapsF.append({"x3": np.ascontiguousarray(x3[sl]),
                      "pp": np.ascontiguousarray(pp)})
    res = run_bass_kernel_spmd(ncF, mapsF, core_ids=cores)
    x4 = np.concatenate([np.asarray(res.results[c]["y"]) for c in cores], 0)
    return x4[None].astype(np.float32)


# version 03
# speedup vs baseline: 1.0181x; 1.0181x over previous
import numpy as np
import ml_dtypes
from collections import defaultdict
import concourse.bass as bass
import concourse.mybir as mybir
from concourse.bass_utils import run_bass_kernel_spmd

F32 = mybir.dt.float32
BF16 = mybir.dt.bfloat16
I32 = mybir.dt.int32
U16 = mybir.dt.uint16
AF = mybir.ActivationFunctionType
ALU = mybir.AluOpType
AX = mybir.AxisListType
BF16NP = ml_dtypes.bfloat16

NCORES = 8
D = 2048
T = 16384
TC = T // NCORES
NTILE = TC // 128

ENGINES = ("sync", "tensor", "vector", "scalar", "gpsimd")


class Prog:

    def __init__(self):
        self.ops = []
        self.sem_counts = defaultdict(int)
        self._sem_cms = []
        self.sem_handles = {}
        self._dma_sems = set()

    def op(self, engine, fn, after=(), dma=False, sem=None):
        assert engine in ENGINES, engine
        if sem is None:
            sem = engine + ("_dma" if dma else "")
        inc = 16 if dma else 1
        if dma:
            self._dma_sems.add(sem)
        aft = []
        for a in after:
            if a is None:
                continue
            asem, aval = a
            if asem in self._dma_sems:
                aval = self.sem_counts[asem]
            aft.append((asem, aval))
        self.sem_counts[sem] += inc
        tokval = self.sem_counts[sem]
        self.ops.append((engine, fn, tuple(aft), sem, inc, tokval))
        return (sem, tokval)

    def final_wait(self, toks, engine="sync"):
        return self.op(engine, lambda e: e.nop(), after=toks)

    def alloc_sems(self, nc):
        for name in self.sem_counts:
            cm = nc.semaphore(name)
            h = cm.__enter__()
            self._sem_cms.append(cm)
            self.sem_handles[name] = h

    def emit(self, block):
        per_engine = defaultdict(list)
        for rec in self.ops:
            per_engine[rec[0]].append(rec)
        hwm = {e: defaultdict(int) for e in ENGINES}
        sh = self.sem_handles

        def section_for(engine):
            def _(eng):
                h = hwm[engine]
                for (_, fn, after, sem, inc, tokval) in per_engine[engine]:
                    for (wsem, wval) in after:
                        if h[wsem] < wval:
                            eng.wait_ge(sh[wsem], wval)
                            h[wsem] = wval
                    inst = fn(eng)
                    inst.then_inc(sh[sem], inc)
            return _

        for engine in ENGINES:
            if per_engine[engine]:
                getattr(block, engine)(section_for(engine))

    def build(self, nc):
        self.alloc_sems(nc)
        with nc.Block() as block:
            self.emit(block)


class Alloc:

    def __init__(self, nc, prog=None):
        import contextlib
        self.nc = nc
        self.es = contextlib.ExitStack()
        self.prog = prog
        self._const_tile = None
        self._const_cols = {}
        self._const_tok = None

    def sb(self, name, shape, dt=F32):
        return self.es.enter_context(self.nc.sbuf_tensor(name, list(shape), dt))

    def ps(self, name, shape, dt=F32):
        return self.es.enter_context(self.nc.psum_tensor(name, list(shape), dt))

    def const(self, value):
        if self._const_tile is None:
            self._const_tile = self.sb("consts", [128, 16])
        if value not in self._const_cols:
            c = len(self._const_cols)
            assert c < 16
            self._const_cols[value] = c
            ap = self._const_tile[:, c:c + 1]
            tok = self.prog.op("gpsimd", lambda e, ap=ap, v=value: e.memset(ap, v),
                               after=[self._const_tok])
            self._const_tok = tok
            self._const_cols[value] = (c, tok)
        c, tok = self._const_cols[value]
        return self._const_tile[:, c:c + 1], tok

    def close(self):
        self.es.close()


class Overlay:

    def __init__(self, nc, a, name, size_bytes):
        self.nc = nc
        self.handle = a.sb(name, [128, size_bytes], mybir.dt.uint8)
        self.base = nc.lookup_mloc(self.handle).addr

    def at(self, name, offset, shape, dtype):
        h = self.nc.virtual_sbuf_tensor(name, list(shape), dtype)
        self.nc._tensor(h.name, h.shape, h.dtype, type="SB")
        mloc = self.nc.lookup_mloc(h)
        mloc.allocated = True
        mloc.addr = self.base + offset
        return h


def emit_ln(p, a, xt, xn_out, col, stats, sqscr, deps, prev_sq, eps=1e-5,
            d=D):
    eps_ap, eps_tok = a.const(eps)
    """LayerNorm of one token-major tile xt [128, d] fp32 -> xn_out (bf16 or
    f32). stats: sbuf [128, ncols*6] scratch; col: which stat column group.
    Returns (token_of_xn, token_of_sq) ."""
    c = col * 6
    red = stats[:, c + 0:c + 1]
    mu = stats[:, c + 1:c + 2]
    sqs = stats[:, c + 2:c + 3]
    e2 = stats[:, c + 3:c + 4]
    nvar = stats[:, c + 4:c + 5]
    rstd = stats[:, c + 5:c + 6]
    t1 = p.op("vector", lambda e: e.reduce_sum(red, xt[:], axis=AX.X), after=deps)
    t2 = p.op("scalar", lambda e: e.activation(sqscr[:], xt[:], AF.Square,
                                               accum_out=sqs),
              after=list(deps) + ([prev_sq] if prev_sq else []))
    t3 = p.op("vector", lambda e: e.tensor_scalar(mu, red, 1.0 / d, None, ALU.mult),
              after=[t1])
    t4 = p.op("vector", lambda e: e.tensor_scalar(e2, sqs, 1.0 / d, None, ALU.mult),
              after=[t2])
    t5 = p.op("vector", lambda e: e.scalar_tensor_tensor(nvar, mu, mu, e2,
                                                         ALU.mult, ALU.subtract),
              after=[t3, t4])
    t6 = p.op("scalar", lambda e: e.activation(stats[:, c + 4:c + 5], nvar, AF.Sqrt,
                                               scale=-1.0, bias=eps_ap),
              after=[t5, eps_tok])
    t7 = p.op("vector", lambda e: e.reciprocal(rstd, stats[:, c + 4:c + 5]),
              after=[t6])
    t8 = p.op("vector", lambda e: e.tensor_scalar(xn_out[:], xt[:], mu, rstd,
                                                  ALU.subtract, ALU.mult),
              after=[t3, t7] + list(deps))
    return t8, t2


def build_ln_launch():
    nc = bass.Bass("TRN2", target_bir_lowering=False)
    x = nc.dram_tensor("x", [TC, D], F32, kind="ExternalInput").ap()
    y = nc.dram_tensor("y", [TC, D], BF16, kind="ExternalOutput").ap()
    p = Prog()
    a = Alloc(nc, p)
    xt = [a.sb(f"xt{i}", [128, D]) for i in range(2)]
    xn = [a.sb(f"xn{i}", [128, D], BF16) for i in range(2)]
    stats = a.sb("stats", [128, NTILE * 6])
    sqscr = a.sb("sqscr", [128, D])
    prev_use = [None, None]
    prev_store = [None, None]
    prev_sq = None
    stores = []
    for i in range(NTILE):
        b = i % 2
        ld = p.op("sync", lambda e, i=i, b=b: e.dma_start(
            out=xt[b][:], in_=x[128 * i:128 * (i + 1), :]),
            after=[prev_use[b]], dma=True, sem=f"ld{b}")
        t8, prev_sq = emit_ln(p, a, xt[b], xn[b], i, stats, sqscr,
                              [ld, prev_store[b]], prev_sq)
        prev_use[b] = t8
        st = p.op("scalar", lambda e, i=i, b=b: e.dma_start(
            out=y[128 * i:128 * (i + 1), :], in_=xn[b][:]),
            after=[t8], dma=True, sem=f"st{b}")
        prev_store[b] = st
        stores.append(st)
    p.final_wait(stores)
    p.build(nc)
    a.close()
    return nc


def host_ln(x_full):
    nc = build_ln_launch()
    in_maps = [{"x": np.ascontiguousarray(x_full[c * TC:(c + 1) * TC])}
               for c in range(NCORES)]
    res = run_bass_kernel_spmd(nc, in_maps, core_ids=list(range(NCORES)))
    return np.concatenate([res.results[c]["y"] for c in range(NCORES)], axis=0)


def build_l0tail(tc=TC, d=D, inter=5632):
    nt = tc // 128
    ndc = d // 128
    nts = tc // 512
    nds = d // 512
    nis = inter // 512
    nib = inter // 128
    nc = bass.Bass("TRN2", target_bir_lowering=False)
    x = nc.dram_tensor("x", [tc, d], F32, kind="ExternalInput").ap()
    xcw = nc.dram_tensor("xcw", [tc, d], BF16, kind="ExternalInput").ap()
    wo_t = nc.dram_tensor("wo_t", [d, d], F32, kind="ExternalInput").ap()
    gw_t = nc.dram_tensor("gw_t", [d, inter], F32, kind="ExternalInput").ap()
    uw_t = nc.dram_tensor("uw_t", [d, inter], F32, kind="ExternalInput").ap()
    dw_t = nc.dram_tensor("dw_t", [inter, d], F32, kind="ExternalInput").ap()
    gs = nc.dram_tensor("gs", [128, nib], F32, kind="ExternalInput").ap()
    us = nc.dram_tensor("us", [128, nib], F32, kind="ExternalInput").ap()
    ds = nc.dram_tensor("ds", [1, d], F32, kind="ExternalInput").ap()
    x2o = nc.dram_tensor("x2", [tc, d], F32, kind="ExternalOutput").ap()
    xn3o = nc.dram_tensor("xn3", [tc, d], BF16, kind="ExternalOutput").ap()
    x1s = nc.dram_tensor("x1s", [tc, d], F32, kind="Internal").ap()
    xn2s = nc.dram_tensor("xn2s", [tc, d], BF16, kind="Internal").ap()
    hdns = nc.dram_tensor("hdns", [inter, tc], BF16, kind="Internal").ap()
    dfms = nc.dram_tensor("dfms", [d, tc], BF16, kind="Internal").ap()

    p = Prog()
    a = Alloc(nc, p)
    A = [a.sb(f"A{i}", [128, d], BF16) for i in range(ndc)]
    B = a.sb("B", [128, ndc * d], BF16)
    _dw0 = a.sb("DW0", [128, nib * 128], BF16)
    DW = [_dw0, None]
    xt = [a.sb(f"xt{i}", [128, d]) for i in range(2)]
    x1t = [a.sb(f"x1t{i}", [128, d]) for i in range(2)]
    x2t = xt
    xnt = [a.sb(f"xnt{i}", [128, d], BF16) for i in range(2)]
    dnt = [a.sb(f"dnt{i}", [128, d], BF16) for i in range(2)]
    gst = a.sb("gst", [128, nib])
    ust = a.sb("ust", [128, nib])
    dsrow = a.sb("dsrow", [1, d], BF16)
    ones1 = a.sb("ones1", [1, 128], BF16)
    dsb = a.sb("dsb", [128, d], BF16)
    stats = a.sb("stats", [128, 2 * nt * 6])
    sqscr = a.sb("sqscr", [128, d], BF16)
    hout = [a.sb(f"hout{i}", [128, 512], BF16) for i in range(2)]
    gtmp = a.sb("gtmp", [128, 512])
    stmp = a.sb("stmp", [128, 512], BF16)
    utmp = a.sb("utmp", [128, 512], BF16)
    dftile = hout
    ps = [a.ps(f"ps{i}", [128, 512]) for i in range(6)]
    DW[1] = B[:, 0:nib * 128]

    ld_gs = p.op("sync", lambda e: e.dma_start(out=gst[:], in_=gs[:]), dma=True, sem="lds")
    ld_us = p.op("sync", lambda e: e.dma_start(out=ust[:], in_=us[:]), dma=True, sem="lds")
    ld_ds = p.op("gpsimd", lambda e: e.dma_start(out=dsrow[:], in_=ds[:]), dma=True, sem="ldds")
    on1 = p.op("gpsimd", lambda e: e.memset(ones1[:], 1.0))
    dsb_toks = []
    for s in range(nds):
        mm = p.op("tensor", lambda e, s=s: e.matmul(
            ps[5][:], ones1[:], dsrow[0:1, 512 * s:512 * (s + 1)],
            start=True, stop=True), after=[ld_ds, on1] + dsb_toks)
        ev = p.op("vector", lambda e, s=s: e.tensor_copy(
            dsb[:, 512 * s:512 * (s + 1)], ps[5][:]), after=[mm])
        dsb_toks = [ev]
    dsb_tok = dsb_toks[0]
    ld_wo = p.op("gpsimd", lambda e: e.dma_start(
        out=B[:].rearrange("p (c n) -> p c n", c=ndc),
        in_=wo_t.rearrange("(c p) n -> p c n", p=128)), dma=True, sem="ldwo")
    tp_toks = []
    for c in range(ndc):
        tp = p.op("sync", lambda e, c=c: e.dma_start(
            out=A[c][:, 0:tc], in_=xcw[:, 128 * c:128 * (c + 1)], transpose=True),
            dma=True, sem="ldtp")
        tp_toks.append(tp)

    prev_x = [None, None]
    prev_x1 = [None, None]
    prev_xn = [None, None]
    prev_sq = None
    bank_busy = {i: None for i in range(6)}
    bank_busy[5] = dsb_tok
    lastA = []
    lastB = []
    p1_stores = []
    for i in range(nt):
        b = i % 2
        ldx = p.op("sync", lambda e, i=i, b=b: e.dma_start(
            out=xt[b][:], in_=x[128 * i:128 * (i + 1), :]),
            after=[prev_x[b]], dma=True, sem=f"ldx{b}")
        for s in range(nds):
            bank = ps[s % 4]
            mms = []
            for c in range(ndc):
                mm = p.op("tensor", lambda e, i=i, s=s, c=c, bank=bank: e.matmul(
                    bank[:], A[c][:, 128 * i:128 * (i + 1)],
                    B[:, (c * d) + 512 * s:(c * d) + 512 * (s + 1)],
                    start=(c == 0), stop=(c == ndc - 1)),
                    after=[tp_toks[c], ld_wo] + ([bank_busy[s % 4]] if c == 0 else []))
                mms.append(mm)
            lastA.append(mms[-1])
            lastB.append(mms[-1])
            ev = p.op("vector", lambda e, i=i, s=s, b=b, bank=bank: e.tensor_add(
                x1t[b][:, 512 * s:512 * (s + 1)], bank[:],
                xt[b][:, 512 * s:512 * (s + 1)]), after=[mms[-1], ldx, prev_x1[b]])
            bank_busy[s % 4] = ev
            prev_x1[b] = ev
        t8, prev_sq = emit_ln(p, a, x1t[b], xnt[b], i, stats, sqscr,
                              [prev_x1[b], prev_xn[b]], prev_sq, d=d)
        prev_xn[b] = t8
        st1 = p.op("scalar", lambda e, i=i, b=b: e.dma_start(
            out=x1s[128 * i:128 * (i + 1), :], in_=x1t[b][:]),
            after=[prev_x1[b]], dma=True, sem=f"stx1{b}")
        st2 = p.op("scalar", lambda e, i=i, b=b: e.dma_start(
            out=xn2s[128 * i:128 * (i + 1), :], in_=xnt[b][:]),
            after=[t8], dma=True, sem=f"stxn{b}")
        prev_x[b] = ev
        prev_x1[b] = st1
        prev_xn[b] = st2
        p1_stores += [st1, st2]

    tp2_toks = []
    for c in range(ndc):
        tp = p.op("sync", lambda e, c=c: e.dma_start(
            out=A[c][:, 0:tc], in_=xn2s[:, 128 * c:128 * (c + 1)], transpose=True),
            after=lastA + p1_stores, dma=True, sem="ldtp2")
        tp2_toks.append(tp)
    nseg = (d // 512)
    seg_busy = [lastB + p1_stores] * 4
    hstore_toks = []
    hd_idx = 0
    prev_e2 = prev_e4 = None
    last_p2_mm = None
    for isl in range(nis):
        for mi, (wt, sct) in enumerate(((gw_t, gst), (uw_t, ust))):
            seg = (isl * 2 + mi) % 4
            ldw = p.op("gpsimd", lambda e, isl=isl, wt=wt, seg=seg: e.dma_start(
                out=B[:].rearrange("p (c n) -> p c n", c=ndc)[:, :, 512 * seg:512 * (seg + 1)],
                in_=wt[:, 512 * isl:512 * (isl + 1)].rearrange("(c p) n -> p c n", p=128)),
                after=seg_busy[seg], dma=True, sem=f"ldw{seg}")
            if mi == 0:
                ldg_tok = ldw
            else:
                ldu_tok = ldw
        for ib in range(4):
            iblk = isl * 4 + ib
            for ts in range(nts):
                mmsg, mmsu = [], []
                bg, bu = ps[(ts % 2) * 2], ps[(ts % 2) * 2 + 1]
                gi, ui = (ts % 2) * 2, (ts % 2) * 2 + 1
                for c in range(ndc):
                    lg = B[:, c * d + 512 * ((isl * 2 + 0) % 4) + 128 * ib:
                           c * d + 512 * ((isl * 2 + 0) % 4) + 128 * (ib + 1)]
                    lu = B[:, c * d + 512 * ((isl * 2 + 1) % 4) + 128 * ib:
                           c * d + 512 * ((isl * 2 + 1) % 4) + 128 * (ib + 1)]
                    mg = p.op("tensor", lambda e, lg=lg, c=c, ts=ts, bg=bg: e.matmul(
                        bg[:], lg, A[c][:, 512 * ts:512 * (ts + 1)],
                        start=(c == 0), stop=(c == ndc - 1)),
                        after=[ldg_tok, tp2_toks[c]] + ([bank_busy[gi]] if c == 0 else []))
                    mu = p.op("tensor", lambda e, lu=lu, c=c, ts=ts, bu=bu: e.matmul(
                        bu[:], lu, A[c][:, 512 * ts:512 * (ts + 1)],
                        start=(c == 0), stop=(c == ndc - 1)),
                        after=[ldu_tok, tp2_toks[c]] + ([bank_busy[ui]] if c == 0 else []))
                    mmsg.append(mg)
                    mmsu.append(mu)
                seg_busy[(isl * 2 + 0) % 4] = [mmsg[-1]]
                seg_busy[(isl * 2 + 1) % 4] = [mmsu[-1]]
                last_p2_mm = mmsu[-1]
                hb = hout[hd_idx % 2]
                e1 = p.op("vector", lambda e, bg=bg, iblk=iblk: e.tensor_scalar(
                    gtmp[:], bg[:], gst[:, iblk:iblk + 1], None, ALU.mult),
                    after=[mmsg[-1], ld_gs] + ([prev_e2] if prev_e2 else []))
                e2 = p.op("scalar", lambda e: e.activation(stmp[:], gtmp[:], AF.Silu),
                          after=[e1] + ([prev_e4] if prev_e4 else []))
                e3 = p.op("vector", lambda e, bu=bu, iblk=iblk: e.tensor_scalar(
                    utmp[:], bu[:], ust[:, iblk:iblk + 1], None, ALU.mult),
                    after=[mmsu[-1], ld_us, e1] + ([prev_e4] if prev_e4 else []))
                e4 = p.op("vector", lambda e, hb=hb: e.tensor_mul(hb[:], stmp[:], utmp[:]),
                          after=[e2, e3] + ([hstore_toks[-2]] if len(hstore_toks) >= 2 else []))
                bank_busy[gi] = e1
                bank_busy[ui] = e3
                prev_e2, prev_e4 = e2, e4
                sth = p.op("scalar", lambda e, hb=hb, iblk=iblk, ts=ts: e.dma_start(
                    out=hdns[128 * iblk:128 * (iblk + 1), 512 * ts:512 * (ts + 1)],
                    in_=hb[:]), after=[e4], dma=True, sem=f"sth{hd_idx % 2}")
                hstore_toks.append(sth)
                hd_idx += 1

    def hslice_tile(ibk):
        return A[ibk // 4][:, 512 * (ibk % 4):512 * (ibk % 4 + 1)]
    last_dfm = [None, None]
    dw_busy = [None, None]
    first_dw = [True]
    df_stores = []
    for ts in range(nts):
        ldh_toks = []
        for c4 in range((nib + 3) // 4):
            n_ib = min(4, nib - c4 * 4)
            dst = A[c4][:].rearrange("p (s n) -> p s n", s=d // 512)[:, 0:n_ib, :]
            src = hdns[128 * 4 * c4:128 * (4 * c4 + n_ib), 512 * ts:512 * (ts + 1)]
            ldh = p.op("sync", lambda e, dst=dst, src=src: e.dma_start(
                out=dst, in_=src.rearrange("(s p) n -> p s n", p=128)),
                after=([last_p2_mm] + hstore_toks) if ts == 0 else [df_stores[-1]],
                dma=True, sem=f"ldh{c4 % 2}")
            ldh_toks.append(ldh)
        for db in range(ndc):
            wb = DW[db % 2]
            ldw = p.op("gpsimd", lambda e, db=db, ts=ts, wb=wb: e.dma_start(
                out=wb.rearrange("p (c n) -> p c n", c=nib)
                if wb is DW[1] else wb[:].rearrange("p (c n) -> p c n", c=nib),
                in_=dw_t[:, 128 * db:128 * (db + 1)].rearrange("(c p) n -> p c n", p=128)),
                after=[dw_busy[db % 2]] +
                      (sum(seg_busy, []) if first_dw[0] else []),
                dma=True, sem=f"lddw{db % 2}")
            first_dw[0] = False
            bank = ps[4 + (db % 2)]
            mms = []
            for ib in range(nib):
                mm = p.op("tensor", lambda e, ib=ib, wb=wb, bank=bank: e.matmul(
                    bank[:], wb[:, 128 * ib:128 * (ib + 1)], hslice_tile(ib),
                    start=(ib == 0), stop=(ib == nib - 1)),
                    after=[ldw] + ldh_toks + ([bank_busy[4 + (db % 2)]] if ib == 0 else []))
                mms.append(mm)
            dw_busy[db % 2] = mms[-1]
            dft = dftile[db % 2]
            ev = p.op("scalar", lambda e, dft=dft, bank=bank: e.activation(
                dft[:], bank[:], AF.Identity), after=[mms[-1], last_dfm[db % 2]])
            bank_busy[4 + (db % 2)] = ev
            stf = p.op("scalar", lambda e, db=db, ts=ts, dft=dft: e.dma_start(
                out=dfms[128 * db:128 * (db + 1), 512 * ts:512 * (ts + 1)],
                in_=dft[:]), after=[ev], dma=True, sem=f"stdf{db % 2}")
            last_dfm[db % 2] = stf
            df_stores.append(stf)

    prev2_x1 = [None, None]
    prev2_dn = [None, None]
    prev2_x2 = [None, None]
    prev2_st = [None, None]
    prev_sq = None
    out_stores = []
    for i in range(nt):
        b = i % 2
        ldx1 = p.op("sync", lambda e, i=i, b=b: e.dma_start(
            out=x1t[b][:], in_=x1s[128 * i:128 * (i + 1), :]),
            after=[prev2_x1[b]] + (df_stores if i == 0 else []), dma=True, sem=f"ldx1{b}")
        tpd = p.op("sync", lambda e, i=i, b=b: e.dma_start(
            out=dnt[b][:], in_=dfms[:, 128 * i:128 * (i + 1)], transpose=True),
            after=[prev2_dn[b]] + (df_stores if i == 0 else []), dma=True, sem=f"tpd{b}")
        t1 = p.op("vector", lambda e, b=b: e.tensor_mul(sqscr[:], dnt[b][:], dsb[:]),
                  after=[tpd, dsb_tok, prev_sq])
        t2 = p.op("vector", lambda e, b=b: e.tensor_add(x2t[b][:], sqscr[:], x1t[b][:]),
                  after=[t1, ldx1, prev2_x2[b], prev2_st[b]])
        stx2 = p.op("scalar", lambda e, i=i, b=b: e.dma_start(
            out=x2o[128 * i:128 * (i + 1), :], in_=x2t[b][:]),
            after=[t2], dma=True, sem=f"stx2{b}")
        t8, prev_sq = emit_ln(p, a, x2t[b], xnt[b], nt + i, stats, sqscr,
                              [t2, prev_xn[b]], prev_sq, d=d)
        prev_xn[b] = t8
        stxn = p.op("scalar", lambda e, i=i, b=b: e.dma_start(
            out=xn3o[128 * i:128 * (i + 1), :], in_=xnt[b][:]),
            after=[t8], dma=True, sem=f"stxn3{b}")
        prev_xn[b] = stxn
        prev2_x1[b] = t2
        prev2_dn[b] = t1
        prev2_x2[b] = t8
        prev2_st[b] = stx2
        out_stores += [stx2, stxn]
    p.final_wait(out_stores)
    p.build(nc)
    a.close()
    return nc


def emit_dequant(p, a, pk_tile, sc_tile, out_lo, out_hi, tmp, deps):
    toks = []
    for half, shift in ((0, None), (1, 4)):
        A_, B_, C_, U_ = tmp["a"], tmp["b"], tmp["c"], tmp["u16"]
        if half == 0:
            d0 = p.op("vector", lambda e: e.tensor_scalar(
                A_, pk_tile, 15, None, ALU.bitwise_and), after=deps)
        else:
            d0 = p.op("vector", lambda e: e.tensor_scalar(
                A_, pk_tile, 4, None, ALU.logical_shift_right), after=deps)
        d1 = p.op("vector", lambda e: e.tensor_scalar(
            B_, A_, 7, None, ALU.bitwise_and), after=[d0])
        d2 = p.op("vector", lambda e: e.tensor_scalar(
            C_, A_, 8, 12, ALU.bitwise_and, ALU.logical_shift_left), after=[d0])
        d3 = p.op("vector", lambda e: e.tensor_scalar(
            A_, B_, 64, 16128, ALU.mult, ALU.add), after=[d1, d2])
        d4 = p.op("vector", lambda e: e.tensor_add(A_, A_, C_), after=[d3])
        d5 = p.op("vector", lambda e: e.tensor_scalar(
            C_, B_, 1, None, ALU.is_le), after=[d4, d1])
        d6 = p.op("vector", lambda e: e.tensor_scalar(
            B_, B_, -16064, 16128, ALU.mult, ALU.add), after=[d5])
        d7 = p.op("vector", lambda e: e.tensor_mul(B_, B_, C_), after=[d6, d5])
        d8 = p.op("vector", lambda e: e.tensor_sub(A_, A_, B_), after=[d7, d4])
        d9 = p.op("vector", lambda e: e.tensor_copy(U_, A_), after=[d8])
        out = out_lo if half == 0 else out_hi
        d10 = p.op("vector", lambda e, out=out: e.tensor_mul(
            out, U_.bitcast(BF16), sc_tile), after=[d9] + deps)
        toks.append(d10)
    return toks


def build_l1ffn(t=T, d=D, ic=2048):
    nt512 = t // 512
    ndc = d // 128
    nwd = d // 256
    nwi = ic // 256
    nib = ic // 128
    nds = d // 512
    ntb = t // 128
    wd = d // 2 // 128
    nc = bass.Bass("TRN2", target_bir_lowering=False)
    xn4T = nc.dram_tensor("xn4T", [d, t], BF16, kind="ExternalInput").ap()
    gpk = nc.dram_tensor("gpk", [d // 2, ic], I32, kind="ExternalInput").ap()
    upk = nc.dram_tensor("upk", [d // 2, ic], I32, kind="ExternalInput").ap()
    gsc = nc.dram_tensor("gsc", [d // 2, ic], F32, kind="ExternalInput").ap()
    usc = nc.dram_tensor("usc", [d // 2, ic], F32, kind="ExternalInput").ap()
    dpk = nc.dram_tensor("dpk", [ic // 2, d], I32, kind="ExternalInput").ap()
    dsc = nc.dram_tensor("dsc", [ic // 2, d], F32, kind="ExternalInput").ap()
    part = nc.dram_tensor("part", [t, d], F32, kind="ExternalOutput").ap()
    hdns = nc.dram_tensor("hdns", [ic, t], BF16, kind="Internal").ap()

    p = Prog()
    a = Alloc(nc, p)
    wdq = max(ic, d)
    hw_ = wdq // 2
    GQ = [a.sb(f"GQ{i}", [128, wdq], BF16) for i in range(ndc)]
    UQ = [a.sb(f"UQ{i}", [128, ic], BF16) for i in range(ndc)]
    pk_t = a.sb("pk_t", [128, hw_], I32)
    sc_t = a.sb("sc_t", [128, hw_])
    tmp = {"a": a.sb("tmpa", [128, hw_], I32), "b": a.sb("tmpb", [128, hw_], I32),
           "c": a.sb("tmpc", [128, hw_], I32), "u16": a.sb("tmpu", [128, hw_], U16)}
    _xs0 = a.sb("xs0", [128, ndc * 512], BF16)
    xs = [_xs0, _xs0]
    g2 = a.sb("g2", [128, 512])
    aat = a.sb("aat", [128, 512])
    cct = a.sb("cct", [128, 512])
    tht = a.sb("tht", [128, 512], BF16)
    h1t = a.sb("h1t", [128, 512], BF16)
    ggt = a.sb("ggt", [128, 512])
    hdt = [a.sb(f"hdt{i}", [128, 512], BF16) for i in range(2)]
    hld = [a.sb(f"hld{i}", [128, nib * 128], BF16) for i in range(2)]
    ott = [a.sb(f"ott{i}", [128, 512]) for i in range(2)]
    ps = [a.ps(f"ps{i}", [128, 512]) for i in range(6)]

    dq_toks = []
    ld_prev = None
    hwc = min(hw_, ic)
    nh = max(1, ic // hwc)
    dqg = [None] * ndc
    dqu = [None] * ndc
    for mat, (pk, sc, out_arena) in enumerate(((gpk, gsc, GQ), (upk, usc, UQ))):
        for k in range(nwd):
            for h in range(nh):
                cs, ce = h * hwc, (h + 1) * hwc
                ldp = p.op("sync", lambda e, pk=pk, k=k, cs=cs, ce=ce: e.dma_start(
                    out=pk_t[:, 0:ce - cs], in_=pk[128 * k:128 * (k + 1), cs:ce]),
                    after=[ld_prev], dma=True, sem="ldpk")
                lds = p.op("sync", lambda e, sc=sc, k=k, cs=cs, ce=ce: e.dma_start(
                    out=sc_t[:, 0:ce - cs], in_=sc[128 * k:128 * (k + 1), cs:ce]),
                    after=[ld_prev], dma=True, sem="ldsc")
                toks = emit_dequant(p, a, pk_t[:, 0:ce - cs], sc_t[:, 0:ce - cs],
                                    out_arena[2 * k][:, cs:ce],
                                    out_arena[2 * k + 1][:, cs:ce],
                                    {k2: v[:, 0:ce - cs] for k2, v in tmp.items()},
                                    [ldp, lds])
                dq_toks += toks
                ld_prev = toks[-1]
                tgt = dqg if mat == 0 else dqu
                tgt[2 * k] = ld_prev
                tgt[2 * k + 1] = ld_prev

    bank_busy = {i: None for i in range(6)}
    hst = [None, None]
    prev_sw = {}
    xs_busy = [None, None]
    last_hstore = []
    for tsl in range(nt512):
        bx = 0
        ldx = p.op("sync", lambda e, tsl=tsl, bx=bx: e.dma_start(
            out=xs[bx][:].rearrange("p (c n) -> p c n", c=ndc),
            in_=xn4T[:, 512 * tsl:512 * (tsl + 1)].rearrange("(c p) n -> p c n", p=128)),
            after=[xs_busy[bx]], dma=True, sem=f"ldxs{bx}")
        for ib in range(nib):
            bg, bu = ps[(ib % 2) * 2], ps[(ib % 2) * 2 + 1]
            mg = mu_ = None
            for c in range(ndc):
                mg = p.op("tensor", lambda e, c=c, ib=ib, bg=bg, bx=bx: e.matmul(
                    bg[:], GQ[c][:, 128 * ib:128 * (ib + 1)],
                    xs[bx][:, 512 * c:512 * (c + 1)],
                    start=(c == 0), stop=(c == ndc - 1)),
                    after=[ldx, dqg[c]] + ([bank_busy[(ib % 2) * 2]] if c == 0 else []))
                mu_ = p.op("tensor", lambda e, c=c, ib=ib, bu=bu, bx=bx: e.matmul(
                    bu[:], UQ[c][:, 128 * ib:128 * (ib + 1)],
                    xs[bx][:, 512 * c:512 * (c + 1)],
                    start=(c == 0), stop=(c == ndc - 1)),
                    after=[ldx, dqu[c]] + ([bank_busy[(ib % 2) * 2 + 1]] if c == 0 else []))
            xs_busy[bx] = mu_
            pe = prev_sw
            q1 = p.op("scalar", lambda e, bg=bg: e.activation(g2[:], bg[:], AF.Square),
                      after=[mg, pe.get("q2")])
            q2 = p.op("vector", lambda e: e.tensor_scalar(
                aat[:], g2[:], 0.0356774081, 0.7978845608, ALU.mult, ALU.add),
                after=[q1, pe.get("q3")])
            q3 = p.op("vector", lambda e, bg=bg: e.tensor_mul(cct[:], aat[:], bg[:]),
                      after=[q2, pe.get("q4")])
            q4 = p.op("scalar", lambda e: e.activation(tht[:], cct[:], AF.Tanh),
                      after=[q3, pe.get("q5")])
            q5 = p.op("vector", lambda e: e.tensor_scalar(
                h1t[:], tht[:], 0.5, 0.5, ALU.mult, ALU.add),
                after=[q4, pe.get("q6")])
            q6 = p.op("vector", lambda e, bg=bg: e.tensor_mul(ggt[:], h1t[:], bg[:]),
                      after=[q5, pe.get("q7")])
            bank_busy[(ib % 2) * 2] = q6
            hb = hdt[ib % 2]
            q7 = p.op("vector", lambda e, bu=bu, hb=hb: e.tensor_mul(
                hb[:], ggt[:], bu[:]), after=[q6, mu_, hst[ib % 2]])
            bank_busy[(ib % 2) * 2 + 1] = q7
            st = p.op("scalar", lambda e, ib=ib, tsl=tsl, hb=hb: e.dma_start(
                out=hdns[128 * ib:128 * (ib + 1), 512 * tsl:512 * (tsl + 1)],
                in_=hb[:]), after=[q7], dma=True, sem=f"sthd{ib % 2}")
            hst[ib % 2] = st
            prev_sw = {"q2": q2, "q3": q3, "q4": q4, "q5": q5, "q6": q6, "q7": q7}
            last_hstore.append(st)

    dq2_toks = []
    ld_prev = [bank_busy[0], bank_busy[1], bank_busy[2], bank_busy[3],
               xs_busy[0], xs_busy[1]]
    for k in range(nwi):
        hwd = min(hw_, d)
        for h in range(max(1, d // hwd)):
            cs, ce = h * hwd, (h + 1) * hwd
            ldp = p.op("sync", lambda e, k=k, cs=cs, ce=ce: e.dma_start(
                out=pk_t[:, 0:ce - cs], in_=dpk[128 * k:128 * (k + 1), cs:ce]),
                after=ld_prev, dma=True, sem="ldpk")
            lds = p.op("sync", lambda e, k=k, cs=cs, ce=ce: e.dma_start(
                out=sc_t[:, 0:ce - cs], in_=dsc[128 * k:128 * (k + 1), cs:ce]),
                after=ld_prev, dma=True, sem="ldsc")
            toks = emit_dequant(p, a, pk_t[:, 0:ce - cs], sc_t[:, 0:ce - cs],
                                GQ[2 * k][:, cs:ce], GQ[2 * k + 1][:, cs:ce],
                                {k2: v[:, 0:ce - cs] for k2, v in tmp.items()},
                                [ldp, lds])
            dq2_toks += toks
            ld_prev = [toks[-1]]

    hl_busy = [None, None]
    st_busy = [None, None]
    for tb in range(ntb):
        bh = tb % 2
        ldh = p.op("sync", lambda e, tb=tb, bh=bh: e.dma_start(
            out=hld[bh][:].rearrange("p (c n) -> p c n", c=nib),
            in_=hdns[:, 128 * tb:128 * (tb + 1)].rearrange("(c p) n -> p c n", p=128)),
            after=[hl_busy[bh]] + (last_hstore if tb < 2 else []),
            dma=True, sem=f"ldhd{bh}")
        lastmm = None
        for dsl in range(nds):
            bank = ps[4 + (dsl % 2)]
            mms = []
            for ib in range(nib):
                mm = p.op("tensor", lambda e, ib=ib, dsl=dsl, bh=bh, bank=bank: e.matmul(
                    bank[:], hld[bh][:, 128 * ib:128 * (ib + 1)],
                    GQ[ib][:, 512 * dsl:512 * (dsl + 1)],
                    start=(ib == 0), stop=(ib == nib - 1)),
                    after=[ldh, dq2_toks[-1]] + ([bank_busy[4 + (dsl % 2)]] if ib == 0 else []))
                mms.append(mm)
            lastmm = mms[-1]
            ev = p.op("vector", lambda e, dsl=dsl, bank=bank: e.tensor_copy(
                ott[dsl % 2][:], bank[:]), after=[mms[-1], st_busy[dsl % 2]])
            bank_busy[4 + (dsl % 2)] = ev
            st = p.op("scalar", lambda e, tb=tb, dsl=dsl: e.dma_start(
                out=part[128 * tb:128 * (tb + 1), 512 * dsl:512 * (dsl + 1)],
                in_=ott[dsl % 2][:]), after=[ev], dma=True, sem=f"stp{dsl % 2}")
            st_busy[dsl % 2] = st
        hl_busy[bh] = lastmm
    p.final_wait([st_busy[0], st_busy[1]])
    p.build(nc)
    a.close()
    return nc


def build_l0attn(t=T, d=D, nsteps=None, debug_stop=None):
    K = 16
    HD = 64
    NHL = 4
    n_chunk = t // K if nsteps is None else nsteps
    tt = n_chunk * K
    ndc = d // 128
    nts = tt // 512
    G = 32
    ngr = n_chunk // G
    nc = bass.Bass("TRN2", target_bir_lowering=False)
    ln1T = nc.dram_tensor("ln1T", [d, t], BF16, kind="ExternalInput").ap()
    wqT = nc.dram_tensor("wqT", [d, 256], F32, kind="ExternalInput").ap()
    wkT = nc.dram_tensor("wkT", [d, 256], F32, kind="ExternalInput").ap()
    wvT = nc.dram_tensor("wvT", [d, 256], F32, kind="ExternalInput").ap()
    lwT4 = nc.dram_tensor("lwT4", [d, 4], F32, kind="ExternalInput").ap()
    lb4 = nc.dram_tensor("lb4", [4, 1], F32, kind="ExternalInput").ap()
    lnw_rep = nc.dram_tensor("lnw_rep", [128, HD], F32, kind="ExternalInput").ap()
    lnb_row = nc.dram_tensor("lnb_row", [1, 256], F32, kind="ExternalInput").ap()
    s0 = nc.dram_tensor("s0", [65, NHL * 64], F32, kind="ExternalInput").ap()
    xcw = nc.dram_tensor("xcw", [tt, 256], BF16, kind="ExternalOutput").ap()
    xbT_s = nc.dram_tensor("xbT_s", [256, tt], BF16, kind="Internal").ap()
    xcT_s = nc.dram_tensor("xcT_s", [256, tt], BF16, kind="Internal").ap()
    sp_s = nc.dram_tensor("sp_s", [tt, 260], F32, kind="Internal").ap()
    xa_s = nc.dram_tensor("xa_s", [tt, 256], F32, kind="Internal").ap()
    xc_s = nc.dram_tensor("xc_s", [tt, 256], F32, kind="Internal").ap()
    cobx_s = nc.dram_tensor("cobx_s", [n_chunk, NHL, K, 65], F32, kind="Internal").ap()
    xa2_s = nc.dram_tensor("xa2_s", [n_chunk, NHL, K, HD], F32, kind="Internal").ap()
    xc2_s = nc.dram_tensor("xc2_s", [n_chunk, NHL, K, HD], F32, kind="Internal").ap()

    p = Prog()
    a = Alloc(nc, p)
    KB = 1024
    SL = min(2048, tt)
    nsl = tt // SL
    LT = [a.sb(f"LT{i}", [128, SL], BF16) for i in range(ndc)]
    wq_sb = a.sb("wq_sb", [128, ndc * 256], BF16)
    wk_sb = a.sb("wk_sb", [128, ndc * 256], BF16)
    wv_sb = a.sb("wv_sb", [128, ndc * 256], BF16)
    lw_sb = a.sb("lw_sb", [128, ndc * 4], BF16)
    lb_sb = a.sb("lb_sb", [4, 1])
    lbr = a.sb("lbr", [1, 4], BF16)
    lnbr = a.sb("lnbr", [1, 256], BF16)
    nones = a.sb("nones", [1, 128], BF16)
    g_t = a.sb("g_t", [128, HD])
    g2_t = a.sb("g2_t", [128, HD])
    evq = [a.sb(f"evq{i}", [128, 512], BF16) for i in range(2)]
    sgq = a.sb("sgq", [128, 512])
    rc_col = a.sb("rc_col", [128, 1])
    onesp = a.sb("onesp", [1, 512], BF16)
    it_col = a.sb("it_col", [128, 1], I32)
    sp_t = [a.sb(f"sp_t{i}", [128, 260]) for i in range(2)]
    xa_t = [a.sb(f"xa_t{i}", [128, 256]) for i in range(2)]
    xc_t = [a.sb(f"xc_t{i}", [128, 256]) for i in range(2)]
    xbts = [a.sb(f"xbts{i}", [65, NHL * G * K], BF16) for i in range(2)]
    xcts = [a.sb(f"xcts{i}", [65, NHL * G * K], BF16) for i in range(2)]
    cbx = [a.sb(f"cbx{i}", [128, G * 65], BF16) for i in range(2)]
    xas = [a.sb(f"xas{i}", [128, G * HD]) for i in range(2)]
    xcs = [a.sb(f"xcs{i}", [128, G * HD]) for i in range(2)]
    outb = [a.sb(f"outb{i}", [128, G * HD], BF16) for i in range(2)]
    S_all = a.sb("S_all", [65, NHL * 65])
    Sb = a.sb("Sb", [65, NHL * 65], BF16)
    maskT = a.sb("maskT", [128, K])
    zcb = a.sb("zcb", [128, HD])
    sqsc = a.sb("sqsc", [128, HD], BF16)
    xhb = a.sb("xhb", [128, HD])
    t5b = a.sb("t5b", [128, HD])
    xagb = a.sb("xagb", [128, HD])
    grb = a.sb("grb", [128, HD])
    t4b = a.sb("t4b", [128, HD], BF16)
    u1b = a.sb("u1b", [128, HD])
    ngb = a.sb("ngb", [128, HD], BF16)
    mco = a.sb("mco", [128, K])
    m2b = a.sb("m2b", [128, K], BF16)
    zc2b = a.sb("zc2b", [128, HD])
    o1b = a.sb("o1b", [128, HD])
    o2b = a.sb("o2b", [128, HD])
    stt = a.sb("stt", [128, 16])
    cobx65 = None
    ps_z = a.ps("ps_z", [128, 512])
    ps_at = a.ps("ps_at", [128, 512])
    ps_zb = a.ps("ps_zb", [128, 512])
    ps_u = a.ps("ps_u", [128, 512])
    ps_p1 = [a.ps(f"ps_p1{i}", [128, 512]) for i in range(2)]
    ps_p2 = [a.ps(f"ps_p2{i}", [128, 512]) for i in range(2)]

    eps_ap, eps_tok = a.const(1e-6)

    ldw = []
    for nm, (sb, dr, w) in {"q": (wq_sb, wqT, 256), "k": (wk_sb, wkT, 256),
                            "v": (wv_sb, wvT, 256), "l": (lw_sb, lwT4, 4)}.items():
        ldw.append(p.op("gpsimd", lambda e, sb=sb, dr=dr, w=w: e.dma_start(
            out=sb[:].rearrange("p (c n) -> p c n", c=ndc),
            in_=dr.rearrange("(c p) n -> p c n", p=128)), dma=True, sem="ldw"))
    ldb = p.op("sync", lambda e: e.dma_start(out=lb_sb[:], in_=lb4[:]),
               dma=True, sem="lds")
    ldbrow = p.op("gpsimd", lambda e: e.dma_start(
        out=lbr[:], in_=lb4.rearrange("h o -> o h")), dma=True, sem="ldbg")
    ldg = p.op("sync", lambda e: e.dma_start(out=g_t[:], in_=lnw_rep[:]),
               dma=True, sem="lds")
    ldbr = p.op("gpsimd", lambda e: e.dma_start(out=lnbr[:], in_=lnb_row[:]),
                dma=True, sem="ldbg")
    lds0 = p.op("sync", lambda e: e.dma_start(
        out=S_all[:].rearrange("p (h n) -> p h n", h=NHL)[:, :, 0:64],
        in_=s0.rearrange("p (h n) -> p h n", h=NHL)), dma=True, sem="lds")

    g2k = p.op("vector", lambda e: e.tensor_mul(g2_t[:], g_t[:], g_t[:]),
               after=[ldg])
    mn1 = p.op("gpsimd", lambda e: e.memset(nones[:], -1.0))
    mp1 = p.op("gpsimd", lambda e: e.memset(onesp[:], 1.0))
    mm0 = p.op("gpsimd", lambda e: e.memset(maskT[:], 1.0))
    mm1 = p.op("gpsimd", lambda e: e.affine_select(
        maskT[:], maskT[:], [[1, K]], ALU.is_ge, 0.0,
        base=0, channel_multiplier=-1), after=[mm0])
    it0 = p.op("gpsimd", lambda e: e.iota(it_col[:], [[0, 1]], base=0,
                                          channel_multiplier=1))
    rc1 = p.op("vector", lambda e: e.tensor_scalar(
        it_col[:], it_col[:], 15, None, ALU.bitwise_and), after=[it0])
    rc2 = p.op("vector", lambda e: e.tensor_copy(rc_col[:], it_col[:]), after=[rc1])
    rc3 = p.op("vector", lambda e: e.tensor_scalar(
        rc_col[:], rc_col[:], 1.0, None, ALU.add), after=[rc2])
    rc4 = p.op("vector", lambda e: e.reciprocal(rc_col[:], rc_col[:]), after=[rc3])
    rc5 = p.op("vector", lambda e: e.tensor_scalar(
        rc_col[:], rc_col[:], 1.0 / 64.0, None, ALU.mult), after=[rc4])
    mk1 = p.op("gpsimd", lambda e: e.iota(maskT[:], [[1, K]], base=0,
               channel_multiplier=0, allow_small_or_imprecise_dtypes=True),
               after=[mm1])
    mk2 = p.op("vector", lambda e: e.tensor_copy(zcb[:, 0:1], it_col[:]),
               after=[rc1, mk1])
    mk3 = p.op("vector", lambda e: e.tensor_scalar(
        maskT[:], maskT[:], zcb[:, 0:1], None, ALU.subtract), after=[mk2])
    mk4 = p.op("vector", lambda e: e.tensor_scalar(
        maskT[:], maskT[:], -0.5, None, ALU.is_gt), after=[mk3])
    mask_tok = mk4

    sr_toks = []
    for hl in range(NHL):
        sr_toks.append(p.op("vector", lambda e, hl=hl: e.reduce_sum(
            S_all[:, 65 * hl + 64:65 * hl + 65],
            S_all[:].rearrange("p (h n) -> p h n", h=NHL)[:, hl, 0:64],
            axis=AX.X), after=[lds0] + sr_toks[-1:]))
    sb0 = p.op("vector", lambda e: e.tensor_copy(Sb[:], S_all[:]),
               after=sr_toks)

    p1_st = []
    prev_p1st = [None, None]
    bank_busy = [None, None, None, None]
    ltp = [None] * ndc
    slice_consumers = []
    for sl in range(nsl):
      t0 = sl * SL
      new_ltp = []
      for c in range(ndc):
        new_ltp.append(p.op("sync", lambda e, c=c, t0=t0: e.dma_start(
            out=LT[c][:], in_=ln1T[128 * c:128 * (c + 1), t0:t0 + SL]),
            after=slice_consumers + [ltp[c]], dma=True, sem=f"ldlt{c % 2}"))
      ltp = new_ltp
      slice_consumers = []
      for half in range(2):
        for tsl_ in range(SL // 512):
            tsl = sl * (SL // 512) + tsl_
            for wi, (wsb, outdr) in enumerate(((wk_sb, xbT_s), (wq_sb, xcT_s))):
                bk = ps_p1[wi]
                mms = []
                for c in range(ndc):
                    mms.append(p.op("tensor", lambda e, wsb=wsb, c=c, half=half,
                               tsl_=tsl_, bk=bk: e.matmul(
                        bk[:], wsb[:, 256 * c + 128 * half:256 * c + 128 * (half + 1)],
                        LT[c][:, 512 * tsl_:512 * (tsl_ + 1)],
                        start=(c == 0), stop=(c == ndc - 1)),
                        after=[ldw[wi], ltp[c]] + ([bank_busy[wi]] if c == 0 else [])))
                slice_consumers.append(mms[-1])
                ev = p.op("vector", lambda e, wi=wi, bk=bk: e.tensor_copy(
                    evq[wi][:], bk[:]), after=[mms[-1], prev_p1st[wi]])
                bank_busy[wi] = ev
                st = p.op("scalar", lambda e, wi=wi, half=half, tsl=tsl,
                          outdr=outdr: e.dma_start(
                    out=outdr[128 * half:128 * (half + 1), 512 * tsl:512 * (tsl + 1)],
                    in_=evq[wi][:]), after=[ev], dma=True, sem=f"stp1{wi}")
                p1_st.append(st)
                prev_p1st[wi] = st

      if sl == 0:
        p2_st = []
        prev_sp = [None, None]
        prev_xa = [None, None]
        prev_xc = [None, None]
      for i_ in range(SL // 128):
        i = sl * (SL // 128) + i_
        b = i % 2
        bkb, bka = ps_p2[0], ps_p2[1]
        bkc, bki = ps_p1[0], ps_p1[1]
        mmsb, mmsa, mmsc, mmsi = [], [], [], []
        for c in range(ndc):
            lhs = LT[c][:, 128 * i_:128 * (i_ + 1)]
            mmsb.append(p.op("tensor", lambda e, c=c, i=i, lhs=lhs: e.matmul(
                bkb[:, 0:256], lhs, wk_sb[:, 256 * c:256 * (c + 1)],
                start=(c == 0), stop=(c == ndc - 1)),
                after=[ldw[1], ltp[c]] + ([prev_sp[b], bank_busy[0]] if c == 0 else [])))
            mmsa.append(p.op("tensor", lambda e, c=c, i=i, lhs=lhs: e.matmul(
                bka[:, 0:256], lhs, wv_sb[:, 256 * c:256 * (c + 1)],
                start=(c == 0), stop=False),
                after=[ldw[2], ltp[c]] + ([prev_xa[b], bank_busy[1]] if c == 0 else [])))
            mmsc.append(p.op("tensor", lambda e, c=c, i=i, lhs=lhs: e.matmul(
                bkc[:, 0:256], lhs, wq_sb[:, 256 * c:256 * (c + 1)],
                start=(c == 0), stop=False),
                after=[ldw[0], ltp[c]] + ([prev_xc[b], bank_busy[2]] if c == 0 else [])))
            mmsi.append(p.op("tensor", lambda e, c=c, i=i, lhs=lhs: e.matmul(
                bki[:, 0:4], lhs, lw_sb[:, 4 * c:4 * (c + 1)],
                start=(c == 0), stop=False),
                after=[ldw[3], ltp[c]] + ([bank_busy[3]] if c == 0 else [])))
        mma2 = p.op("tensor", lambda e: e.matmul(
            bka[:, 0:256], nones[:], lnbr[:], start=False, stop=True),
            after=[mn1, ldbr])
        mmc2 = p.op("tensor", lambda e: e.matmul(
            bkc[:, 0:256], onesp[0:1, 0:128], lnbr[:],
            start=False, stop=True), after=[mp1, ldbr])
        mmib = p.op("tensor", lambda e, bki=bki: e.matmul(
            bki[:, 0:4], onesp[0:1, 0:128], lbr[:], start=False, stop=True),
            after=[mp1, ldbrow, mmsi[-1]])
        slice_consumers.append(mmib)
        evi = p.op("scalar", lambda e, bki=bki: e.activation(
            sgq[:, 0:4], bki[:, 0:4], AF.Sigmoid),
            after=[mmib, bank_busy[3]])
        coi = p.op("vector", lambda e: e.tensor_scalar(
            sgq[:, 4:8], sgq[:, 0:4], rc_col[:, 0:1], None, ALU.mult),
            after=[evi, rc5])
        bank_busy[3] = coi
        evb = p.op("vector", lambda e, b=b, bkb=bkb: e.tensor_copy(
            sp_t[b][:].rearrange("p (h n) -> p h n", h=NHL)[:, :, 0:64],
            bkb[:, 0:256].rearrange("p (h n) -> p h n", h=NHL)),
            after=[mmsb[-1], prev_sp[b]])
        cts = []
        for hl in range(NHL):
            c1_ = p.op("vector", lambda e, b=b, hl=hl: e.tensor_scalar(
                sp_t[b][:, 65 * hl:65 * hl + 64], sp_t[b][:, 65 * hl:65 * hl + 64],
                sgq[:, 4 + hl:5 + hl], None, ALU.mult), after=[evb, coi])
            c2_ = p.op("vector", lambda e, b=b, hl=hl: e.tensor_copy(
                sp_t[b][:, 65 * hl + 64:65 * hl + 65], sgq[:, 4 + hl:5 + hl]),
                after=[evb, coi])
            cts += [c1_, c2_]
        stsp = p.op("scalar", lambda e, i=i, b=b: e.dma_start(
            out=sp_s[128 * i:128 * (i + 1), :], in_=sp_t[b][:]),
            after=cts, dma=True, sem=f"stsp{b}")
        prev_sp[b] = stsp
        eva = p.op("vector", lambda e, b=b, bka=bka: e.tensor_copy(
            xa_t[b][:], bka[:, 0:256]), after=[mma2, prev_xa[b]])
        eva2 = p.op("vector", lambda e, b=b, bkb=bkb: e.tensor_sub(
            xa_t[b][:], xa_t[b][:], bkb[:, 0:256]), after=[eva, mmsb[-1]])
        bank_busy[0] = eva2
        bank_busy[1] = eva2
        sta = p.op("scalar", lambda e, i=i, b=b: e.dma_start(
            out=xa_s[128 * i:128 * (i + 1), :], in_=xa_t[b][:]),
            after=[eva2], dma=True, sem=f"sta{b}")
        prev_xa[b] = sta
        evc = p.op("vector", lambda e, b=b, bkc=bkc: e.tensor_copy(
            xc_t[b][:], bkc[:, 0:256]), after=[mmc2, prev_xc[b]])
        bank_busy[2] = evc
        stc = p.op("scalar", lambda e, i=i, b=b: e.dma_start(
            out=xc_s[128 * i:128 * (i + 1), :], in_=xc_t[b][:]),
            after=[evc], dma=True, sem=f"stc{b}")
        prev_xc[b] = stc
        p2_st += [stsp, sta, stc]

    re_toks = []
    for k in range(K):
        re_toks.append(p.op("gpsimd", lambda e, k=k: e.dma_start(
            out=cobx_s[:, :, k, :],
            in_=sp_s.rearrange("(n k) (h g) -> n k h g", k=K, h=NHL)[:, k, :, :]),
            after=p2_st, dma=True, sem=f"re{k % 4}"))
        re_toks.append(p.op("gpsimd", lambda e, k=k: e.dma_start(
            out=xa2_s[:, :, k, :],
            in_=xa_s.rearrange("(n k) (h g) -> n k h g", k=K, h=NHL)[:, k, :, :]),
            after=p2_st, dma=True, sem=f"re{k % 4}"))
        re_toks.append(p.op("gpsimd", lambda e, k=k: e.dma_start(
            out=xc2_s[:, :, k, :],
            in_=xc_s.rearrange("(n k) (h g) -> n k h g", k=K, h=NHL)[:, k, :, :]),
            after=p2_st, dma=True, sem=f"re{k % 4}"))

    if debug_stop == "p2":
        p.final_wait(p2_st)
        p.build(nc)
        a.close()
        return nc
    if debug_stop == "re":
        p.final_wait(re_toks)
        p.build(nc)
        a.close()
        return nc
    init_toks = []
    for buf in xbts + xcts:
        init_toks.append(p.op("gpsimd", lambda e, buf=buf: e.memset(buf[:], 1.0)))
    for buf in cbx + xas + xcs:
        init_toks.append(p.op("gpsimd", lambda e, buf=buf: e.memset(buf[:], 0.0)))

    GK = G * K
    wr = {}
    from collections import defaultdict as _dd
    rd = _dd(list)

    def OP(engine, fn, reads=(), writes=(), extra=(), dma=False, sem=None):
        after = list(extra)
        for t_ in writes:
            after += rd[t_] + [wr.get(t_)]
        for t_ in reads:
            after.append(wr.get(t_))
        tok = p.op(engine, fn, after=after, dma=dma, sem=sem)
        for t_ in writes:
            rd[t_] = []
            wr[t_] = tok
        for t_ in reads:
            rd[t_].append(tok)
        return tok

    for tok, names in ((sb0, ["Sb", "S_all"]), (mask_tok, ["maskT", "zcb"]),
                       (g2k, ["g2", "g"]), (rc5, ["rc"])):
        for nmx in names:
            wr[nmx] = tok
    wr["stream"] = init_toks[-1]
    for bank, width, names in ((ps_z, 65, ["psz0", "psz1", "psz2", "psz3"]),
                               (ps_at, K, ["psat0", "psat1", "psat2", "psat3"]),
                               (ps_zb, 65, ["pszb0", "pszb1", "pszb2", "pszb3"]),
                               (ps_u, NHL * HD, ["psu0", "psu1", "psu2", "psu3"])):
        wtok = p.op("tensor", lambda e, bank=bank, width=width: e.matmul(
            bank[:, 0:width], onesp[0:1, 0:128], onesp[0:1, 0:width],
            start=True, stop=True), after=[mp1])
        for nmx in names:
            wr[nmx] = wtok

    out_stores = []
    for gr in range(ngr):
        if debug_stop == f"gr{gr}":
            p.final_wait([v for v in wr.values() if v is not None] + out_stores)
            p.build(nc); a.close(); return nc
        pb = gr % 2
        n0 = gr * G
        ld_toks = []
        for hl in range(NHL):
            ld_toks.append(OP("sync", lambda e, hl=hl, pb=pb, n0=n0: e.dma_start(
                out=xbts[pb][0:64, GK * hl:GK * (hl + 1)],
                in_=xbT_s[64 * hl:64 * (hl + 1), K * n0:K * (n0 + G)]),
                reads=[], writes=[f"xbts{pb}_{hl}"], dma=True, sem=f"lsb{pb}",
                extra=[wr["stream"]] + p1_st))
            ld_toks.append(OP("sync", lambda e, hl=hl, pb=pb, n0=n0: e.dma_start(
                out=xcts[pb][0:64, GK * hl:GK * (hl + 1)],
                in_=xcT_s[64 * hl:64 * (hl + 1), K * n0:K * (n0 + G)]),
                reads=[], writes=[f"xcts{pb}_{hl}"], dma=True, sem=f"lsc{pb}",
                extra=[wr["stream"]] + p1_st))
            ld_toks.append(OP("gpsimd", lambda e, hl=hl, pb=pb, n0=n0: e.dma_start(
                out=cbx[pb][32 * hl:32 * hl + 16, :].rearrange(
                    "p (n g) -> p n g", g=65),
                in_=cobx_s[n0:n0 + G, hl].rearrange("n k g -> k n g")),
                reads=[], writes=[f"cbx{pb}_{hl}"], dma=True, sem=f"lcb{pb}",
                extra=[wr["stream"]] + re_toks))
            ld_toks.append(OP("sync", lambda e, hl=hl, pb=pb, n0=n0: e.dma_start(
                out=xas[pb][32 * hl:32 * hl + 16, :].rearrange(
                    "p (n g) -> p n g", g=HD),
                in_=xa2_s[n0:n0 + G, hl].rearrange("n k g -> k n g")),
                reads=[], writes=[f"xas{pb}_{hl}"], dma=True, sem=f"lxa{pb}",
                extra=[wr["stream"]] + re_toks))
            ld_toks.append(OP("sync", lambda e, hl=hl, pb=pb, n0=n0: e.dma_start(
                out=xcs[pb][32 * hl:32 * hl + 16, :].rearrange(
                    "p (n g) -> p n g", g=HD),
                in_=xc2_s[n0:n0 + G, hl].rearrange("n k g -> k n g")),
                reads=[], writes=[f"xcs{pb}_{hl}"], dma=True, sem=f"lxc{pb}",
                extra=[wr["stream"]] + re_toks))
        if debug_stop == "ld0" and gr == 0:
            p.final_wait(ld_toks)
            p.build(nc)
            a.close()
            return nc
        snames = ([f"xbts{pb}_{hl}" for hl in range(NHL)] +
                  [f"xcts{pb}_{hl}" for hl in range(NHL)] +
                  [f"cbx{pb}_{hl}" for hl in range(NHL)] +
                  [f"xas{pb}_{hl}" for hl in range(NHL)] +
                  [f"xcs{pb}_{hl}" for hl in range(NHL)])
        for gs in range(G):
            for hl in range(NHL):
                OP("tensor", lambda e, hl=hl, pb=pb, gs=gs: e.matmul(
                    ps_z[32 * hl:32 * hl + 16, 0:65],
                    xbts[pb][:, GK * hl + K * gs:GK * hl + K * (gs + 1)],
                    Sb[:, 65 * hl:65 * (hl + 1)],
                    start=True, stop=True, tile_position=(0, 32 * hl)),
                    reads=[f"xbts{pb}_{hl}", "Sb"], writes=[f"psz{hl}"])
            if debug_stop == "z1":
                p.final_wait([v for v in wr.values() if v is not None])
                p.build(nc); a.close(); return nc
            OP("scalar", lambda e: e.activation(
                stt[:, 0:1], ps_z[:, 64:65], AF.Identity, scale=-1.0 / 64),
                reads=["psz0", "psz1", "psz2", "psz3"], writes=["nmu"])
            OP("scalar", lambda e: e.activation(
                zcb[:], ps_z[:, 0:64], AF.Identity, bias=stt[:, 0:1]),
                reads=["nmu", "psz0", "psz1", "psz2", "psz3"], writes=["zcb"])
            OP("scalar", lambda e: e.activation(
                sqsc[:], zcb[:], AF.Square, accum_out=stt[:, 1:2]),
                reads=["zcb"], writes=["sqs", "sqsc"])
            OP("scalar", lambda e: e.activation(
                stt[:, 2:3], stt[:, 1:2], AF.Sqrt, scale=1.0 / 64, bias=eps_ap),
                reads=["sqs"], writes=["std"], extra=[eps_tok])
            OP("vector", lambda e: e.reciprocal(stt[:, 3:4], stt[:, 2:3]),
                reads=["std"], writes=["rstd"])
            OP("vector", lambda e: e.tensor_scalar(
                xhb[:], zcb[:], stt[:, 3:4], None, ALU.mult),
                reads=["rstd", "zcb"], writes=["xhb"])
            OP("vector", lambda e: e.tensor_mul(t5b[:], xhb[:], g2_t[:]),
                reads=["xhb", "g2"], writes=["t5b"])
            OP("vector", lambda e, pb=pb, gs=gs: e.tensor_mul(
                xagb[:], xas[pb][:, HD * gs:HD * (gs + 1)], g_t[:]),
                reads=[f"xas{pb}_0", f"xas{pb}_1", f"xas{pb}_2", f"xas{pb}_3", "g"],
                writes=["xagb"])
            OP("vector", lambda e: e.scalar_tensor_tensor(
                grb[:], t5b[:], 1.0, xagb[:], ALU.mult, ALU.subtract,
                accum_out=stt[:, 4:5]),
                reads=["t5b", "xagb"], writes=["grb", "s1"])
            OP("vector", lambda e: e.scalar_tensor_tensor(
                t4b[:], grb[:], 1.0, xhb[:], ALU.mult, ALU.mult,
                accum_out=stt[:, 5:6]),
                reads=["grb", "xhb"], writes=["t4b", "s2"])
            OP("vector", lambda e: e.tensor_scalar(
                stt[:, 6:7], stt[:, 4:5], stt[:, 3:4], 1.0 / 64, ALU.mult, ALU.mult),
                reads=["s1", "rstd"], writes=["c1"])
            OP("vector", lambda e: e.tensor_scalar(
                stt[:, 7:8], stt[:, 5:6], stt[:, 3:4], 1.0 / 64, ALU.mult, ALU.mult),
                reads=["s2", "rstd"], writes=["c2"])
            OP("vector", lambda e: e.tensor_scalar(
                u1b[:], grb[:], stt[:, 3:4], stt[:, 6:7], ALU.mult, ALU.subtract),
                reads=["grb", "rstd", "c1"], writes=["u1b"])
            OP("vector", lambda e: e.scalar_tensor_tensor(
                ngb[:], xhb[:], stt[:, 7:8], u1b[:], ALU.mult, ALU.subtract),
                reads=["xhb", "c2", "u1b"], writes=["ngb"])
            if debug_stop == "ng":
                p.final_wait([v for v in wr.values() if v is not None])
                p.build(nc); a.close(); return nc
            for hl in range(NHL):
                OP("tensor", lambda e, hl=hl, pb=pb, gs=gs: e.matmul(
                    ps_at[32 * hl:32 * hl + 16, 0:K],
                    xbts[pb][0:64, GK * hl + K * gs:GK * hl + K * (gs + 1)],
                    xcts[pb][0:64, GK * hl + K * gs:GK * hl + K * (gs + 1)],
                    start=True, stop=True, tile_position=(0, 32 * hl)),
                    reads=[f"xbts{pb}_{hl}", f"xcts{pb}_{hl}"],
                    writes=[f"psat{hl}"])
            OP("vector", lambda e, pb=pb, gs=gs: e.tensor_copy(
                stt[:, 12:13], cbx[pb][:, 65 * gs + 64:65 * gs + 65]),
                reads=[f"cbx{pb}_0", f"cbx{pb}_1", f"cbx{pb}_2", f"cbx{pb}_3"],
                writes=["cof"])
            OP("vector", lambda e: e.tensor_scalar(
                mco[:], maskT[:], stt[:, 12:13], None, ALU.mult),
                reads=["maskT", "cof"], writes=["mco"])
            OP("vector", lambda e: e.scalar_tensor_tensor(
                m2b[:], ps_at[:, 0:K], 1.0, mco[:], ALU.add, ALU.mult),
                reads=["mco", "psat0", "psat1", "psat2", "psat3"], writes=["m2b"])
            if debug_stop == "m2":
                p.final_wait([v for v in wr.values() if v is not None])
                p.build(nc); a.close(); return nc
            for hl in range(NHL):
                OP("tensor", lambda e, hl=hl, pb=pb, gs=gs: e.matmul(
                    ps_zb[32 * hl:32 * hl + 16, 0:65],
                    xcts[pb][:, GK * hl + K * gs:GK * hl + K * (gs + 1)],
                    Sb[:, 65 * hl:65 * (hl + 1)],
                    start=True, stop=False, tile_position=(0, 32 * hl)),
                    reads=[f"xcts{pb}_{hl}", "Sb"], writes=[f"pszb{hl}"])
                OP("tensor", lambda e, hl=hl: e.matmul(
                    ps_zb[32 * hl:32 * hl + 16, 0:64],
                    m2b[32 * hl:32 * hl + 16, :],
                    ngb[32 * hl:32 * hl + 16, :],
                    start=False, stop=True, tile_position=(32 * hl, 32 * hl)),
                    reads=["m2b", "ngb"], writes=[f"pszb{hl}"])
            if debug_stop == "zb":
                p.final_wait([v for v in wr.values() if v is not None])
                p.build(nc); a.close(); return nc
            for hl in range(NHL):
                OP("tensor", lambda e, hl=hl, pb=pb, gs=gs: e.matmul(
                    ps_u[0:65, HD * hl:HD * (hl + 1)],
                    cbx[pb][32 * hl:32 * hl + 16, 65 * gs:65 * (gs + 1)],
                    ngb[32 * hl:32 * hl + 16, :],
                    start=True, stop=True, tile_position=(32 * hl, 0)),
                    reads=[f"cbx{pb}_{hl}", "ngb"] +
                          ([f"psu{hl - 1}"] if hl > 0 else []),
                    writes=[f"psu{hl}"])
            if debug_stop == "u":
                p.final_wait([v for v in wr.values() if v is not None])
                p.build(nc); a.close(); return nc
            OP("vector", lambda e: e.tensor_tensor(
                S_all[:].rearrange("p (h n) -> p h n", h=NHL)[:, :, 0:64],
                S_all[:].rearrange("p (h n) -> p h n", h=NHL)[:, :, 0:64],
                ps_u[0:65, 0:NHL * HD].rearrange("p (h n) -> p h n", h=NHL),
                ALU.add),
                reads=["psu0", "psu1", "psu2", "psu3"], writes=["S_all"])
            OP("vector", lambda e: e.reduce_sum(
                S_all[:].rearrange("p (h n) -> p h n", h=NHL)[:, :, 64:65],
                S_all[:].rearrange("p (h n) -> p h n", h=NHL)[:, :, 0:64],
                axis=AX.X), reads=["S_all"], writes=["S_all"])
            OP("vector", lambda e: e.tensor_copy(Sb[:], S_all[:]),
               reads=["S_all"], writes=["Sb"])
            if debug_stop == "v26":
                p.final_wait([v for v in wr.values() if v is not None])
                p.build(nc); a.close(); return nc
            OP("scalar", lambda e: e.activation(
                stt[:, 8:9], ps_zb[:, 64:65], AF.Identity, scale=-1.0 / 64),
                reads=["pszb0", "pszb1", "pszb2", "pszb3"], writes=["nmu2"])
            OP("scalar", lambda e: e.activation(
                zc2b[:], ps_zb[:, 0:64], AF.Identity, bias=stt[:, 8:9]),
                reads=["nmu2", "pszb0", "pszb1", "pszb2", "pszb3"],
                writes=["zc2b"])
            OP("scalar", lambda e: e.activation(
                sqsc[:], zc2b[:], AF.Square, accum_out=stt[:, 9:10]),
                reads=["zc2b"], writes=["sqs2", "sqsc"])
            OP("scalar", lambda e: e.activation(
                stt[:, 10:11], stt[:, 9:10], AF.Sqrt, scale=1.0 / 64, bias=eps_ap),
                reads=["sqs2"], writes=["std2"], extra=[eps_tok])
            OP("vector", lambda e: e.reciprocal(stt[:, 11:12], stt[:, 10:11]),
                reads=["std2"], writes=["rstd2"])
            OP("vector", lambda e: e.tensor_scalar(
                o1b[:], zc2b[:], stt[:, 11:12], None, ALU.mult),
                reads=["rstd2", "zc2b"], writes=["o1b"])
            OP("vector", lambda e: e.tensor_mul(o2b[:], o1b[:], g_t[:]),
                reads=["o1b", "g"], writes=["o2b"])
            OP("vector", lambda e, pb=pb, gs=gs: e.tensor_add(
                outb[pb][:, HD * gs:HD * (gs + 1)], o2b[:],
                xcs[pb][:, HD * gs:HD * (gs + 1)]),
                reads=["o2b"] + [f"xcs{pb}_{hl}" for hl in range(NHL)],
                writes=[f"outb{pb}"])
            if debug_stop and debug_stop.startswith("s") and \
                    gr * G + gs + 1 >= int(debug_stop[1:]):
                p.final_wait([v for v in wr.values() if v is not None])
                p.build(nc)
                a.close()
                return nc
        for hl in range(NHL):
            st = OP("scalar", lambda e, hl=hl, pb=pb, n0=n0: e.dma_start(
                out=xcw.rearrange("(n k) (h g) -> k n g h", k=K, h=NHL)
                    [:, n0:n0 + G, :, hl],
                in_=outb[pb][32 * hl:32 * hl + 16, :].rearrange(
                    "p (n g) -> p n g", g=HD)),
                reads=[f"outb{pb}"], writes=[], dma=True, sem=f"stout{pb}")
            out_stores.append(st)
        wr[f"outb{pb}"] = out_stores[-1]

    p.final_wait(out_stores)
    p.build(nc)
    a.close()
    return nc


def build_reduce(tc=TC, d=D, nparts=NCORES):
    nt = tc // 128
    nc = bass.Bass("TRN2", target_bir_lowering=False)
    x3 = nc.dram_tensor("x3", [tc, d], F32, kind="ExternalInput").ap()
    pp = nc.dram_tensor("pp", [nparts, tc, d], F32, kind="ExternalInput").ap()
    y = nc.dram_tensor("y", [tc, d], F32, kind="ExternalOutput").ap()
    p = Prog()
    a = Alloc(nc, p)
    acc = [a.sb(f"acc{i}", [128, d]) for i in range(2)]
    pt = [a.sb(f"pt{i}", [128, d]) for i in range(3)]
    prev_acc = [None, None]
    prev_pt = [None, None, None]
    stores = []
    for i in range(nt):
        b = i % 2
        lda = p.op("sync", lambda e, i=i, b=b: e.dma_start(
            out=acc[b][:], in_=x3[128 * i:128 * (i + 1), :]),
            after=[prev_acc[b]], dma=True, sem=f"la{b}")
        tok = lda
        for j in range(nparts):
            pb = (i * nparts + j) % 3
            ldp = p.op("gpsimd" if j % 2 else "sync",
                       lambda e, i=i, j=j, pb=pb: e.dma_start(
                out=pt[pb][:], in_=pp[j, 128 * i:128 * (i + 1), :]),
                after=[prev_pt[pb]], dma=True, sem=f"lp{pb}")
            tok = p.op("vector", lambda e, b=b, pb=pb: e.tensor_add(
                acc[b][:], acc[b][:], pt[pb][:]), after=[ldp, tok])
            prev_pt[pb] = tok
        st = p.op("scalar", lambda e, i=i, b=b: e.dma_start(
            out=y[128 * i:128 * (i + 1), :], in_=acc[b][:]),
            after=[tok], dma=True, sem=f"st{b}")
        prev_acc[b] = st
        stores.append(st)
    p.final_wait(stores)
    p.build(nc)
    a.close()
    return nc


def pi_perm(n):
    perm = np.empty(n, dtype=np.int64)
    ar = 2 * np.arange(128)
    for k in range(n // 256):
        perm[256 * k:256 * k + 128] = 256 * k + ar
        perm[256 * k + 128:256 * k + 256] = 256 * k + ar + 1
    return perm


def prep_l1ffn_core(c, ncores, gate_pk, gate_sc, up_pk, up_sc, down_pk,
                    down_sc, d, inter):
    ic = inter // ncores
    i_lo = c * ic
    pil = pi_perm(ic)
    ws_d = np.arange(d // 2) // 16
    ws_i = (i_lo // 2 + np.arange(ic // 2)) // 16
    o = {}
    o["gpk"] = np.ascontiguousarray(gate_pk[i_lo:i_lo + ic].T[:, pil]).astype(np.int32)
    o["upk"] = np.ascontiguousarray(up_pk[i_lo:i_lo + ic].T[:, pil]).astype(np.int32)
    o["gsc"] = np.ascontiguousarray(gate_sc[i_lo + pil][:, ws_d].T).astype(np.float32)
    o["usc"] = np.ascontiguousarray(up_sc[i_lo + pil][:, ws_d].T).astype(np.float32)
    o["dpk"] = np.ascontiguousarray(down_pk.T[i_lo // 2:(i_lo + ic) // 2, :]).astype(np.int32)
    o["dsc"] = np.ascontiguousarray(down_sc[:, ws_i].T).astype(np.float32)
    return o


F32R = mybir.dt.float32r


def build_l1attn(mode, tc=TC, d=D, nh=32, NZ=64):
    hd = 64
    dn = 64
    CH = 256
    nzl = tc // CH
    ndc = d // 128
    nts = tc // 512
    ntk = tc // 128
    full = (mode == "main")
    nc = bass.Bass("TRN2", target_bir_lowering=False)
    xn3 = nc.dram_tensor("xn3", [tc, d], BF16, kind="ExternalInput").ap()
    bwT = nc.dram_tensor("bwT", [d, dn], F32, kind="ExternalInput").ap()
    cwT = nc.dram_tensor("cwT", [d, dn], F32, kind="ExternalInput").ap()
    dtwT = nc.dram_tensor("dtwT", [d, nh], F32, kind="ExternalInput").ap()
    bb = nc.dram_tensor("bb", [dn, 1], F32, kind="ExternalInput").ap()
    cb = nc.dram_tensor("cb", [dn, 1], F32, kind="ExternalInput").ap()
    dtb = nc.dram_tensor("dtb", [nh, 1], F32, kind="ExternalInput").ap()
    alog = nc.dram_tensor("alog", [nh, 1], F32, kind="ExternalInput").ap()
    if full:
        x2 = nc.dram_tensor("x2", [tc, d], F32, kind="ExternalInput").ap()
        drow = nc.dram_tensor("drow", [1, nh], F32, kind="ExternalInput").ap()
        prevc = nc.dram_tensor("prevc", [nh, nzl, dn * hd], BF16,
                               kind="ExternalInput").ap()
        owT = nc.dram_tensor("owT", [d, d], F32, kind="ExternalInput").ap()
        ob = nc.dram_tensor("ob", [1, d], F32, kind="ExternalInput").ap()
        x3o = nc.dram_tensor("x3", [tc, d], F32, kind="ExternalOutput").ap()
        xn4o = nc.dram_tensor("xn4", [tc, d], BF16, kind="ExternalOutput").ap()
        yts = nc.dram_tensor("ytok_s", [tc, d], BF16, kind="Internal").ap()
    else:
        stex = nc.dram_tensor("stex", [nzl, nh * dn * hd], F32,
                              kind="ExternalOutput").ap()
        dasum = nc.dram_tensor("dasum", [nh, nzl], F32, kind="ExternalOutput").ap()

    p = Prog()
    a = Alloc(nc, p)
    KB = 1024
    ov = Overlay(nc, a, "arena", 174 * KB)
    XT = [ov.at(f"XT{i}", i * tc * 2, [128, tc], BF16) for i in range(ndc)]
    xdr = [ov.at(f"xdr{i}", i * d * 2, [128, d], BF16) for i in range(4)]
    corow = ov.at("corow", 16 * KB, [128, 8 * CH], F32R)
    firow = ov.at("firow", 24 * KB, [128, 8 * CH], F32R)
    prevt = [ov.at(f"prevt{i}", 32 * KB + i * 4 * KB, [dn, nh * hd], BF16)
             for i in range(2)]
    yt = [ov.at(f"yt{i}", 40 * KB + i * d * 2, [128, d], BF16) for i in range(2)]
    scb = [ov.at(f"scb{i}", 48 * KB + i * CH * 2, [128, CH], BF16) for i in range(2)]
    expl = ov.at("expl", 49 * KB, [128, CH], F32)
    exp3 = ov.at("exp3", 50 * KB, [dn, CH], F32)
    mst = [ov.at(f"mst{i}", 51 * KB + i * CH * 2, [128, CH], BF16) for i in range(2)]
    mst3 = ov.at("mst3", 52 * KB, [dn, CH], BF16)
    mneg = [ov.at(f"mneg{i}", 53 * KB + i * CH * 4, [128, CH], F32R) for i in range(2)]
    YT = [ov.at(f"YT{i}", i * tc * 2, [128, tc], BF16) for i in range(ndc)]
    X16 = [ov.at(f"X16{i}", 64 * KB + i * d * 2, [128, d], BF16) for i in range(ntk)]
    ow = [ov.at(f"ow{i}", 64 * KB + i * d * 2, [128, d], BF16) for i in range(ndc)]
    BT = ov.at("BT", 128 * KB, [dn, tc], F32)
    CT = ov.at("CT", 128 * KB + tc * 4, [dn, tc], F32)
    BTb = ov.at("BTb", 128 * KB + tc * 8, [dn, tc], BF16)
    CTb = ov.at("CTb", 128 * KB + tc * 10, [dn, tc], BF16)
    dt_sb = ov.at("dt_sb", 128 * KB + tc * 12, [nh, tc], F32)
    x2t = [ov.at(f"x2t{i}", 128 * KB + i * d * 4, [128, d], F32) for i in range(2)]
    x3t = [ov.at(f"x3t{i}", 128 * KB + 16 * KB + i * d * 4, [128, d], F32)
           for i in range(2)]
    dt_bf = ov.at("dt_bf", 160 * KB, [nh, tc], BF16)
    spr = ov.at("spr", 160 * KB + tc * 2, [nh, 512], F32)
    spab = ov.at("spab", 160 * KB + tc * 2 + 2 * KB, [nh, 512], F32)
    xnt = [ov.at(f"xnt{i}", 160 * KB + i * d * 2, [128, d], BF16) for i in range(2)]
    bw_sb = ov.at("bw_sb", 168 * KB, [128, ndc * dn], BF16)
    cw_sb = ov.at("cw_sb", 168 * KB + ndc * dn * 2, [128, ndc * dn], BF16)
    dtw_sb = ov.at("dtw_sb", 168 * KB + ndc * dn * 4, [128, ndc * nh], BF16)
    stats = ov.at("stats", 168 * KB, [128, ntk * 6], F32)
    sqscr = ov.at("sqscr", 168 * KB + ntk * 6 * 4, [128, d], BF16)
    bvec = a.sb("bvec", [dn, 2])
    dtb_sb = a.sb("dtb_sb", [nh, 1])
    alog_sb = a.sb("alog_sb", [nh, 1])
    na = a.sb("na", [nh, 1])
    ident = a.sb("ident", [128, 128])
    id_bf = a.sb("id_bf", [128, 128], BF16)
    id_r = a.sb("id_r", [128, 128], F32R)
    ones_r = a.sb("ones_r", [128, 128], F32R)
    ones_bf = a.sb("ones_bf", [1, 128], BF16)
    dttok = [a.sb(f"dttok{i}", [128, nh]) for i in range(ntk)]
    cs_c = a.sb("cs_c", [nh, CH])
    zro = a.sb("zro", [nh, CH])
    cofi = a.sb("cofi", [nh, 2 * CH])
    cofi_r = a.sb("cofi_r", [nh, 2 * CH], F32R)
    ctsb = a.sb("ctsb", [128, 4 * nh])
    negct = a.sb("negct", [128, 2 * nh])
    dasb = a.sb("dasb", [nh, max(nzl, 2)])
    modi = a.sb("modi", [nh, CH], I32)
    dbc = a.sb("dbc", [128, nh])
    dbr = a.sb("dbr", [1, nh], F32R)
    obrow = a.sb("obrow", [1, d], BF16)
    if not full:
        Btok = [a.sb(f"Btok{i}", [128, dn], BF16) for i in range(ntk)]
        dcy = a.sb("dcy", [nh, CH], BF16)
        dcyT = a.sb("dcyT", [128, 2 * nh])
        xw = [a.sb(f"xw{i}", [128, dn], BF16) for i in range(2)]
        stx = a.sb("stx", [dn, nh * hd])
    ps = [a.ps(f"ps{i}", [128, 512]) for i in range(6)]
    pst = a.ps("pst", [128, 512])

    ld0 = p.op("gpsimd", lambda e: e.dma_start(
        out=bw_sb[:].rearrange("p (c n) -> p c n", c=ndc),
        in_=bwT.rearrange("(c p) n -> p c n", p=128)), dma=True, sem="ldw0")
    ld1 = p.op("gpsimd", lambda e: e.dma_start(
        out=cw_sb[:].rearrange("p (c n) -> p c n", c=ndc),
        in_=cwT.rearrange("(c p) n -> p c n", p=128)), dma=True, sem="ldw0")
    ld2 = p.op("gpsimd", lambda e: e.dma_start(
        out=dtw_sb[:].rearrange("p (c n) -> p c n", c=ndc),
        in_=dtwT.rearrange("(c p) n -> p c n", p=128)), dma=True, sem="ldw0")
    ld3 = p.op("sync", lambda e: e.dma_start(out=bvec[:, 0:1], in_=bb[:]),
               dma=True, sem="ldw1")
    ld4 = p.op("sync", lambda e: e.dma_start(out=bvec[:, 1:2], in_=cb[:]),
               dma=True, sem="ldw1")
    ld5 = p.op("sync", lambda e: e.dma_start(out=dtb_sb[:], in_=dtb[:]),
               dma=True, sem="ldw1")
    ld6 = p.op("sync", lambda e: e.dma_start(out=alog_sb[:], in_=alog[:]),
               dma=True, sem="ldw1")
    one_ap, one_tok = a.const(1.0)
    mz = p.op("gpsimd", lambda e: e.memset(ident[:], 1.0))
    idt = p.op("gpsimd", lambda e: e.affine_select(
        ident[:], ident[:], [[-1, 128]], ALU.is_equal, 0.0,
        base=0, channel_multiplier=1), after=[mz])
    idb = p.op("vector", lambda e: e.tensor_copy(id_bf[:], ident[:]), after=[idt])
    idr = p.op("vector", lambda e: e.tensor_copy(id_r[:], ident[:]), after=[idt])
    ones_f = a.sb("ones_f", [128, 128])
    _onf = p.op("gpsimd", lambda e: e.memset(ones_f[:], 1.0))
    onr = p.op("vector", lambda e: e.tensor_copy(ones_r[:], ones_f[:]), after=[_onf])
    onb = p.op("vector", lambda e: e.tensor_copy(ones_bf[:], ones_f[0:1, :]), after=[_onf])
    zr0 = p.op("gpsimd", lambda e: e.memset(zro[:], 0.0))
    ea = p.op("scalar", lambda e: e.activation(na[:], alog_sb[:], AF.Exp),
              after=[ld6])
    ea2 = p.op("vector", lambda e: e.tensor_scalar(na[:], na[:], -1.0, None,
                                                   ALU.mult), after=[ea])
    if full:
        ldd = p.op("sync", lambda e: e.dma_start(out=dbc[0:1, :], in_=drow[:]),
                   dma=True, sem="ldw2")
        ldo = p.op("gpsimd", lambda e: e.dma_start(out=obrow[:], in_=ob[:]),
                   dma=True, sem="ldob")
        c2r = p.op("vector", lambda e: e.tensor_copy(dbr[:], dbc[0:1, :]), after=[ldd])
        mmD = p.op("tensor", lambda e: e.matmul(
            pst[:, 0:nh], ones_r[0:1, :], dbr[:], start=True, stop=True),
            after=[c2r, onr])
        evD = p.op("vector", lambda e: e.tensor_copy(dbc[:], pst[:, 0:nh]),
                   after=[mmD])

    tpx = []
    for c in range(ndc):
        tpx.append(p.op("sync", lambda e, c=c: e.dma_start(
            out=XT[c][:, 0:tc], in_=xn3[:, 128 * c:128 * (c + 1)], transpose=True),
            dma=True, sem="ldtp"))
    ldx16 = []
    for i in range(ntk):
        ldx16.append(p.op("gpsimd", lambda e, i=i: e.dma_start(
            out=X16[i][:], in_=xn3[128 * i:128 * (i + 1), :]),
            dma=True, sem="ldx16"))

    bank_busy = {i: None for i in range(6)}
    evB = evC = None
    sp_prev = None
    for ts in range(nts):
        for wi, (wsb, width) in enumerate(((bw_sb, dn), (cw_sb, dn), (dtw_sb, nh))):
            bank = ps[wi % 3]
            mms = []
            for c in range(ndc):
                mm = p.op("tensor", lambda e, wsb=wsb, c=c, ts=ts, width=width,
                          bank=bank: e.matmul(
                    bank[0:width, :], wsb[:, width * c:width * (c + 1)],
                    XT[c][:, 512 * ts:512 * (ts + 1)],
                    start=(c == 0), stop=(c == ndc - 1)),
                    after=[ld0, ld1, ld2, tpx[c]] + ([bank_busy[wi % 3]] if c == 0 else []))
                mms.append(mm)
            if wi == 0:
                evB = p.op("scalar", lambda e, ts=ts, bank=bank: e.activation(
                    BT[:, 512 * ts:512 * (ts + 1)], bank[0:dn, :], AF.Identity,
                    bias=bvec[:, 0:1]), after=[mms[-1], ld3])
                bank_busy[wi % 3] = evB
            elif wi == 1:
                evC = p.op("scalar", lambda e, ts=ts, bank=bank: e.activation(
                    CT[:, 512 * ts:512 * (ts + 1)], bank[0:dn, :], AF.Identity,
                    bias=bvec[:, 1:2]), after=[mms[-1], ld4])
                bank_busy[wi % 3] = evC
            else:
                s1 = p.op("scalar", lambda e, bank=bank: e.activation(
                    spr[0:nh, :], bank[0:nh, :], AF.Relu, bias=dtb_sb[:]),
                    after=[mms[-1], ld5, sp_prev])
                s2 = p.op("scalar", lambda e, bank=bank: e.activation(
                    spab[0:nh, :], bank[0:nh, :], AF.Abs, bias=dtb_sb[:]),
                    after=[mms[-1], s1])
                s3 = p.op("scalar", lambda e: e.activation(
                    spab[0:nh, :], spab[0:nh, :], AF.Exp, scale=-1.0), after=[s2])
                s4 = p.op("scalar", lambda e: e.activation(
                    spab[0:nh, :], spab[0:nh, :], AF.Ln, bias=one_ap[0:nh, 0:1]),
                    after=[s3, one_tok])
                s5 = p.op("vector", lambda e, ts=ts: e.tensor_add(
                    dt_sb[:, 512 * ts:512 * (ts + 1)], spr[0:nh, :], spab[0:nh, :]),
                    after=[s4])
                sp_prev = s5
                bank_busy[wi % 3] = s5
    cb1 = p.op("vector", lambda e: e.tensor_copy(BTb[:], BT[:]), after=[evB])
    cb2 = p.op("vector", lambda e: e.tensor_copy(CTb[:], CT[:]), after=[evC])
    cb3 = p.op("vector", lambda e: e.tensor_copy(dt_bf[:], dt_sb[:]), after=[sp_prev])

    proj_done = [cb1, cb2, cb3, bank_busy[0], bank_busy[1], bank_busy[2]]
    tk_toks = []
    prev_e1 = prev_e2 = None
    for i in range(ntk):
        if not full:
            m1 = p.op("tensor", lambda e, i=i: e.matmul(
                pst[:, 0:dn], BTb[:, 128 * i:128 * (i + 1)], id_bf[0:dn, 0:dn],
                start=True, stop=True), after=proj_done + [idb, prev_e1])
            prev_e1 = p.op("vector", lambda e, i=i: e.tensor_copy(
                Btok[i][:], pst[:, 0:dn]), after=[m1])
        m2 = p.op("tensor", lambda e, i=i: e.matmul(
            pst[:, 64:64 + nh], dt_bf[:, 128 * i:128 * (i + 1)], id_bf[0:nh, 0:nh],
            start=True, stop=True), after=proj_done + [idb, prev_e2, prev_e1])
        prev_e2 = p.op("vector", lambda e, i=i: e.tensor_copy(
            dttok[i][:], pst[:, 64:64 + nh]), after=[m2])
        tk_toks.append(prev_e2)

    if full:
        mk_toks = []
        mnegf = a.sb("mnegf", [128, CH])
        for half in range(2):
            mz2 = p.op("gpsimd", lambda e: e.memset(mnegf[:], 0.0),
                       after=proj_done + tk_toks + mk_toks)
            mk0 = p.op("gpsimd", lambda e, half=half: e.affine_select(
                mnegf[:], mnegf[:], [[1, CH]], ALU.is_ge, -1e30,
                base=-128 * half, channel_multiplier=-1), after=[mz2])
            mk = p.op("vector", lambda e, half=half: e.tensor_copy(
                mneg[half][:], mnegf[:]), after=[mk0])
            mk_toks.append(mk)

    st_stores = []
    y_stores = []
    prev_stx = None
    prev_da = None
    last_chunk_toks = list(tk_toks) + proj_done
    pv_busy = [None, None]
    yt_busy = [None, None]
    xdr_busy = [None, None, None, None]
    for z in range(nzl):
        deps = last_chunk_toks
        c0 = CH * z
        xa, xb_ = xdr[2 * (z % 2)], xdr[2 * (z % 2) + 1]
        xd_toks = []
        for li, xti in ((0, xa), (1, xb_)):
            for h in range(nh):
                tkx = p.op("vector", lambda e, z=z, li=li, h=h, xti=xti: e.tensor_scalar(
                    xti[:, hd * h:hd * (h + 1)],
                    X16[2 * z + li][:, hd * h:hd * (h + 1)],
                    dttok[2 * z + li][:, h:h + 1], None, ALU.mult),
                    after=[tk_toks[2 * z + li], ldx16[2 * z + li],
                           xdr_busy[2 * (z % 2) + li]] +
                          ([] if z >= 2 else (tpx if full else tpx)) +
                          ([deps[-1]] if z == 0 else []))
                xd_toks.append(tkx)
        xdt_done = xd_toks[-1]
        t1 = p.op("vector", lambda e, c0=c0: e.tensor_scalar(
            cs_c[:], dt_sb[:, c0:c0 + CH], na[0:nh, 0:1], None, ALU.mult),
            after=deps + [ea2])
        t2 = p.op("vector", lambda e: e.tensor_tensor_scan(
            cs_c[:], cs_c[:], zro[:, 0:CH], 0.0, ALU.add, ALU.add),
            after=[t1, zr0])
        if not full:
            t3 = p.op("scalar", lambda e: e.activation(
                dcy[:], cs_c[:], AF.Exp, scale=-1.0, bias=cs_c[:, CH - 1:CH]),
                after=[t2])
            t4 = p.op("vector", lambda e, z=z: e.tensor_copy(
                dasb[:, z:z + 1], cs_c[:, CH - 1:CH]), after=[t2, prev_da])
            prev_da = t4
        else:
            t5a = p.op("vector", lambda e: e.tensor_scalar(
                cofi[:, CH:2 * CH], cs_c[:], 1.0 / 64.0, None, ALU.mult),
                after=[t2])
            t5b = p.op("vector", lambda e: e.tensor_copy(
                modi[:, 0:CH], cofi[:, CH:2 * CH]), after=[t5a])
            t5c = p.op("vector", lambda e: e.tensor_copy(
                cofi[:, CH:2 * CH], modi[:, 0:CH]), after=[t5b])
            t6a = p.op("vector", lambda e: e.tensor_scalar(
                cofi[:, 0:CH], cofi[:, CH:2 * CH], 64.0, None, ALU.mult),
                after=[t5c])
            t6 = p.op("vector", lambda e: e.tensor_sub(
                cofi[:, CH:2 * CH], cs_c[:], cofi[:, 0:CH]), after=[t6a])
            t7 = p.op("vector", lambda e: e.tensor_copy(cofi_r[:], cofi[:]),
                      after=[t6])
            clps = []
            for gi in range(nh // 8):
                for (buf, off) in ((corow, 0), (firow, CH)):
                    clp = p.op("gpsimd", lambda e, gi=gi, buf=buf, off=off: e.dma_start(
                        out=buf[32 * gi:32 * gi + 1, :].rearrange(
                            "o (h n) -> o h n", h=8),
                        in_=cofi_r[8 * gi:8 * (gi + 1), off:off + CH]),
                        after=[t7], dma=True, sem=f"clp{gi % 2}")
                    clps.append(clp)
            tms = []
            for q in range(4):
                part, half = q // 2, q % 2
                tm = p.op("tensor", lambda e, part=part, half=half, q=q: e.matmul(
                    pst[:, 512 - 4 * nh + nh * q:512 - 4 * nh + nh * (q + 1)],
                    cofi_r[:, CH * part + 128 * half:CH * part + 128 * (half + 1)],
                    id_r[0:nh, 0:nh], start=True, stop=True),
                    after=[t7, idr] + tms)
                tms.append(tm)
            tA = p.op("vector", lambda e: e.tensor_copy(
                ctsb[:, 0:4 * nh], pst[:, 512 - 4 * nh:512]), after=tms)
            ncg = []
            for half in range(2):
                ncg.append(p.op("vector", lambda e, half=half: e.scalar_tensor_tensor(
                    negct[:, nh * half:nh * (half + 1)],
                    ctsb[:, nh * half:nh * (half + 1)], -1.0,
                    ctsb[:, 2 * nh + nh * half:2 * nh + nh * (half + 1)],
                    ALU.mult, ALU.subtract), after=[tA]))
        if not full:
            dts = []
            for half in range(2):
                tm = p.op("tensor", lambda e, half=half: e.matmul(
                    pst[:, 256 + nh * half:256 + nh * (half + 1)],
                    dcy[:, 128 * half:128 * (half + 1)], id_bf[0:nh, 0:nh],
                    start=True, stop=True), after=[t3, idb] + dts)
                dts.append(tm)
            dtv = p.op("vector", lambda e: e.tensor_copy(
                dcyT[:], pst[:, 256:256 + 2 * nh]), after=dts)
            evs = []
            for h in range(nh):
                w1 = p.op("vector", lambda e, h=h, xa=xa: e.tensor_scalar(
                    xw[0][:], xa[:, hd * h:hd * (h + 1)],
                    dcyT[:, h:h + 1], None, ALU.mult),
                    after=[dtv, xdt_done] + evs[-1:])
                w2 = p.op("vector", lambda e, h=h, xb_=xb_: e.tensor_scalar(
                    xw[1][:], xb_[:, hd * h:hd * (h + 1)],
                    dcyT[:, nh + h:nh + h + 1], None, ALU.mult),
                    after=[dtv, xdt_done] + evs[-1:])
                m1 = p.op("tensor", lambda e, z=z: e.matmul(
                    pst[0:dn, 0:hd], Btok[2 * z][:], xw[0][:],
                    start=True, stop=False), after=[w1, prev_e1])
                m2 = p.op("tensor", lambda e, z=z: e.matmul(
                    pst[0:dn, 0:hd], Btok[2 * z + 1][:], xw[1][:],
                    start=False, stop=True), after=[w2])
                ev = p.op("vector", lambda e, h=h: e.tensor_copy(
                    stx[:, hd * h:hd * (h + 1)], pst[0:dn, 0:hd]),
                    after=[m2, prev_stx])
                evs.append(ev)
            sst = p.op("scalar", lambda e, z=z: e.dma_start(
                out=stex[z].rearrange("(h n q) -> n h q", h=nh, n=dn),
                in_=stx[:].rearrange("n (h q) -> n h q", h=nh)),
                after=evs, dma=True, sem="stst")
            st_stores.append(sst)
            prev_stx = sst
            xdr_busy[2 * (z % 2)] = evs[-1]
            xdr_busy[2 * (z % 2) + 1] = evs[-1]
            last_chunk_toks = [evs[-1], t2]
            continue
        sc_toks = []
        for half in range(2):
            mm = p.op("tensor", lambda e, half=half, c0=c0: e.matmul(
                ps[4 + half][:, 0:CH], BTb[:, c0 + 128 * half:c0 + 128 * (half + 1)],
                CTb[:, c0:c0 + CH], start=True, stop=True),
                after=[cb1, cb2] + ([bank_busy[4 + half]] if bank_busy[4 + half] else []))
            evv = p.op("vector", lambda e, half=half: e.tensor_copy(
                scb[half][:], ps[4 + half][:, 0:CH]),
                after=[mm] + y_stores[-2:])
            bank_busy[4 + half] = evv
            sc_toks.append(evv)
        bpv = z % 2
        ldpv = p.op("sync", lambda e, z=z, bpv=bpv: e.dma_start(
            out=prevt[bpv][:].rearrange("n (h q) -> n h q", h=nh),
            in_=prevc[:, z, :].rearrange("h (n q) -> n h q", n=dn)),
            after=[pv_busy[bpv]] + (deps if z < 2 else []),
            dma=True, sem=f"ldpv{bpv}")
        ymms_last = None
        prev_ms = None
        prev_e3 = None
        for h in range(nh):
            gi, gc = h // 8, h % 8
            eL = []
            for half in range(2):
                tp_ = (96, 0) if gi == 3 else None
                b1 = p.op("tensor", lambda e, gi=gi, gc=gc, half=half, tp_=tp_: e.matmul(
                    ps[half][:, 0:CH], ones_r[32 * gi:32 * gi + 1, :],
                    corow[32 * gi:32 * gi + 1, CH * gc:CH * (gc + 1)],
                    start=True, stop=False, tile_position=tp_),
                    after=clps + [onr] + ([bank_busy[half]] if bank_busy[half] else []))
                b2 = p.op("tensor", lambda e, gi=gi, gc=gc, half=half, tp_=tp_: e.matmul(
                    ps[half][:, 0:CH], ones_r[32 * gi:32 * gi + 1, :],
                    firow[32 * gi:32 * gi + 1, CH * gc:CH * (gc + 1)],
                    start=False, stop=False, tile_position=tp_), after=[])
                b3 = p.op("tensor", lambda e, half=half: e.matmul(
                    ps[half][:, 0:CH], id_r[:], mneg[half][:],
                    start=False, stop=True), after=[mk_toks[half]])
                ex = p.op("scalar", lambda e, h=h, half=half: e.activation(
                    expl[:], ps[half][:, 0:CH], AF.Exp,
                    bias=negct[:, nh * half + h:nh * half + h + 1]),
                    after=[b3, prev_ms] + ncg)
                ms = p.op("vector", lambda e, half=half: e.tensor_mul(
                    mst[half][:], scb[half][:], expl[:]),
                    after=[ex, sc_toks[half]])
                prev_ms = ms
                bank_busy[half] = ms
                eL.append(ms)
            tp_ = (96, 0) if gi == 3 else None
            b4 = p.op("tensor", lambda e, gi=gi, gc=gc, tp_=tp_: e.matmul(
                ps[2][0:dn, 0:CH], ones_r[32 * gi:32 * gi + 1, 0:dn],
                corow[32 * gi:32 * gi + 1, CH * gc:CH * (gc + 1)],
                start=True, stop=False, tile_position=tp_),
                after=clps + ([bank_busy[2]] if bank_busy[2] else []))
            b5 = p.op("tensor", lambda e, gi=gi, gc=gc, tp_=tp_: e.matmul(
                ps[2][0:dn, 0:CH], ones_r[32 * gi:32 * gi + 1, 0:dn],
                firow[32 * gi:32 * gi + 1, CH * gc:CH * (gc + 1)],
                start=False, stop=True, tile_position=tp_), after=[])
            ex3 = p.op("scalar", lambda e: e.activation(
                exp3[:], ps[2][0:dn, 0:CH], AF.Exp), after=[b5, prev_e3])
            ms3 = p.op("vector", lambda e, c0=c0: e.tensor_mul(
                mst3[:], CT[:, c0:c0 + CH], exp3[:]), after=[ex3, evC])
            prev_e3 = ms3
            bank_busy[2] = ms3
            for lqh in range(2):
                ya = p.op("tensor", lambda e, h=h, lqh=lqh, xa=xa: e.matmul(
                    ps[3][:, 0:hd], mst[0][:, 128 * lqh:128 * (lqh + 1)],
                    xa[:, hd * h:hd * (h + 1)], start=True, stop=False),
                    after=[eL[0], xdt_done] + ([bank_busy[3]] if bank_busy[3] else []))
                yb = p.op("tensor", lambda e, h=h, lqh=lqh, xb_=xb_: e.matmul(
                    ps[3][:, 0:hd], mst[1][:, 128 * lqh:128 * (lqh + 1)],
                    xb_[:, hd * h:hd * (h + 1)], start=False, stop=False),
                    after=[eL[1]])
                yc = p.op("tensor", lambda e, h=h, lqh=lqh, bpv=bpv: e.matmul(
                    ps[3][:, 0:hd], mst3[:, 128 * lqh:128 * (lqh + 1)],
                    prevt[bpv][:, hd * h:hd * (h + 1)], start=False, stop=True),
                    after=[ms3, ldpv])
                yev = p.op("vector", lambda e, z=z, h=h, lqh=lqh: e.scalar_tensor_tensor(
                    yt[lqh][:, hd * h:hd * (h + 1)],
                    X16[2 * z + lqh][:, hd * h:hd * (h + 1)],
                    dbc[0:128, h:h + 1], ps[3][:, 0:hd],
                    ALU.mult, ALU.add),
                    after=[yc, evD] + ([yt_busy[lqh]] if h == 0 else []))
                bank_busy[3] = yev
                ymms_last = yev
                if h == nh - 1:
                    yst = p.op("scalar", lambda e, z=z, lqh=lqh: e.dma_start(
                        out=yts[128 * (2 * z + lqh):128 * (2 * z + lqh + 1), :],
                        in_=yt[lqh][:]), after=[yev], dma=True, sem=f"sty{lqh}")
                    y_stores.append(yst)
                    yt_busy[lqh] = yst
        pv_busy[bpv] = ymms_last
        xdr_busy[2 * (z % 2)] = ymms_last
        xdr_busy[2 * (z % 2) + 1] = ymms_last
        last_chunk_toks = [ymms_last]

    if not full:
        sda = p.op("scalar", lambda e: e.dma_start(
            out=dasum[:, 0:nzl], in_=dasb[:, 0:nzl]),
            after=[prev_da] + st_stores, dma=True, sem="stda")
        p.final_wait(st_stores + [sda])
        p.build(nc)
        a.close()
        return nc

    ldow_toks = []
    for c in range(ndc):
        ldow_toks.append(p.op("gpsimd", lambda e, c=c: e.dma_start(
            out=ow[c][:], in_=owT[128 * c:128 * (c + 1), :]),
            after=y_stores, dma=True, sem="ldow"))
    ytp = []
    for c in range(ndc):
        ytp.append(p.op("sync", lambda e, c=c: e.dma_start(
            out=YT[c][:, 0:tc], in_=yts[:, 128 * c:128 * (c + 1)], transpose=True),
            after=y_stores, dma=True, sem="ldytp"))
    prev_x2 = [None, None]
    prev_x3 = [None, None]
    prev_xn = [None, None]
    prev_sq = None
    out_stores = []
    for i in range(ntk):
        b = i % 2
        ldx2 = p.op("sync", lambda e, i=i, b=b: e.dma_start(
            out=x2t[b][:], in_=x2[128 * i:128 * (i + 1), :]),
            after=[prev_x2[b]] + (proj_done + y_stores if i < 2 else []),
            dma=True, sem=f"ldx2{b}")
        for s in range(d // 512):
            bank = ps[s % 2]
            mms = []
            for c in range(ndc):
                mm = p.op("tensor", lambda e, i=i, s=s, c=c, bank=bank: e.matmul(
                    bank[:, 0:512], YT[c][:, 128 * i:128 * (i + 1)],
                    ow[c][:, 512 * s:512 * (s + 1)],
                    start=(c == 0), stop=False),
                    after=[ytp[c], ldow_toks[c]] + ([bank_busy[s % 2]] if c == 0 else []))
                mms.append(mm)
            mob = p.op("tensor", lambda e, i=i, s=s, bank=bank: e.matmul(
                bank[:, 0:512], ones_bf[:],
                obrow[0:1, 512 * s:512 * (s + 1)], start=False, stop=True),
                after=[onb, ldo])
            ev2 = p.op("vector", lambda e, s=s, b=b, bank=bank: e.tensor_add(
                x3t[b][:, 512 * s:512 * (s + 1)], bank[:, 0:512],
                x2t[b][:, 512 * s:512 * (s + 1)]),
                after=[mob, ldx2, prev_x3[b]])
            bank_busy[s % 2] = ev2
            prev_x3[b] = ev2
        stx3 = p.op("scalar", lambda e, i=i, b=b: e.dma_start(
            out=x3o[128 * i:128 * (i + 1), :], in_=x3t[b][:]),
            after=[prev_x3[b]], dma=True, sem=f"stx3{b}")
        t8, prev_sq = emit_ln(p, a, x3t[b], xnt[b], i, stats, sqscr,
                              [prev_x3[b], prev_xn[b]], prev_sq, d=d)
        stxn = p.op("scalar", lambda e, i=i, b=b: e.dma_start(
            out=xn4o[128 * i:128 * (i + 1), :], in_=xnt[b][:]),
            after=[t8], dma=True, sem=f"stxn4{b}")
        prev_xn[b] = stxn
        prev_x2[b] = ev2
        prev_x3[b] = stx3
        out_stores += [stx3, stxn]
    p.final_wait(out_stores)
    p.build(nc)
    a.close()
    return nc


def build_l1prev(nh=32, NZ=64, dn=64, hd=64):
    nc = bass.Bass("TRN2", target_bir_lowering=False)
    stin = nc.dram_tensor("stin", [nh, NZ, dn * hd], F32, kind="ExternalInput").ap()
    dasT = nc.dram_tensor("dasT", [nh, NZ], F32, kind="ExternalInput").ap()
    prevo = nc.dram_tensor("prev", [nh, NZ, dn * hd], BF16, kind="ExternalOutput").ap()
    p = Prog()
    a = Alloc(nc, p)
    das = a.sb("das", [nh, NZ])
    zro = a.sb("zro", [nh, NZ])
    css = a.sb("css", [nh, NZ + 1])
    cofi = a.sb("cofi", [nh, 2 * (NZ + 1)])
    cofi_r = a.sb("cofi_r", [nh, 2 * (NZ + 1)], F32R)
    corow = a.sb("corow", [1, nh * (NZ + 1)], F32R)
    firow = a.sb("firow", [1, nh * (NZ + 1)], F32R)
    ctsb = a.sb("ctsb", [NZ + 1, 2 * nh])
    negct = a.sb("negct", [NZ + 1, nh])
    mneg = a.sb("mneg", [NZ + 1, NZ], F32R)
    modi = a.sb("modi", [nh, NZ + 1], I32)
    ident = a.sb("ident", [128, 128])
    id_r = a.sb("id_r", [128, 128], F32R)
    ones_r = a.sb("ones_r", [1, 128], F32R)
    wt = a.sb("wt", [NZ + 1, NZ], BF16)
    sext = a.sb("sext", [NZ + 1, dn * hd], BF16)
    pv = a.sb("pv", [NZ, dn * hd], BF16)
    pst = a.ps("pst", [128, 512])
    ps2 = a.ps("ps2", [128, 512])

    ld1 = p.op("gpsimd", lambda e: e.dma_start(out=das[:], in_=dasT[:]),
               dma=True, sem="ld")
    zr = p.op("gpsimd", lambda e: e.memset(zro[:], 0.0))
    z1 = p.op("gpsimd", lambda e: e.memset(css[:, 0:1], 0.0))
    t2 = p.op("vector", lambda e: e.tensor_tensor_scan(
        css[:, 1:NZ + 1], das[:], zro[:], 0.0, ALU.add, ALU.add),
        after=[ld1, zr])
    t5a = p.op("vector", lambda e: e.tensor_scalar(
        cofi[:, NZ + 1:], css[:], 1.0 / 64.0, None, ALU.mult), after=[t2, z1])
    t5b = p.op("vector", lambda e: e.tensor_copy(modi[:], cofi[:, NZ + 1:]),
               after=[t5a])
    t5c = p.op("vector", lambda e: e.tensor_copy(cofi[:, NZ + 1:], modi[:]),
               after=[t5b])
    t6a = p.op("vector", lambda e: e.tensor_scalar(
        cofi[:, 0:NZ + 1], cofi[:, NZ + 1:], 64.0, None, ALU.mult), after=[t5c])
    t6 = p.op("vector", lambda e: e.tensor_sub(
        cofi[:, NZ + 1:], css[:], cofi[:, 0:NZ + 1]), after=[t6a])
    t7 = p.op("vector", lambda e: e.tensor_copy(cofi_r[:], cofi[:]), after=[t6])
    t8 = p.op("gpsimd", lambda e: e.dma_start(
        out=corow[0:1, :].rearrange("o (h n) -> o h n", h=nh),
        in_=cofi_r[:, 0:NZ + 1]), after=[t7], dma=True, sem="clp0")
    t9 = p.op("gpsimd", lambda e: e.dma_start(
        out=firow[0:1, :].rearrange("o (h n) -> o h n", h=nh),
        in_=cofi_r[:, NZ + 1:]), after=[t7], dma=True, sem="clp1")
    mz = p.op("gpsimd", lambda e: e.memset(ident[:], 1.0))
    idt = p.op("gpsimd", lambda e: e.affine_select(
        ident[:], ident[:], [[-1, 128]], ALU.is_equal, 0.0,
        base=0, channel_multiplier=1), after=[mz])
    idr = p.op("vector", lambda e: e.tensor_copy(id_r[:], ident[:]), after=[idt])
    ones_f = a.sb("ones_f", [1, 128])
    mnegf = a.sb("mnegf", [NZ + 1, NZ])
    _onf = p.op("gpsimd", lambda e: e.memset(ones_f[:], 1.0))
    onr = p.op("vector", lambda e: e.tensor_copy(ones_r[:], ones_f[:]), after=[_onf])
    mz2 = p.op("gpsimd", lambda e: e.memset(mnegf[:], 0.0))
    mk0 = p.op("gpsimd", lambda e: e.affine_select(
        mnegf[:], mnegf[:], [[1, NZ]], ALU.is_ge, -1e30,
        base=0, channel_multiplier=-1), after=[mz2])
    mk = p.op("vector", lambda e: e.tensor_copy(mneg[:], mnegf[:]), after=[mk0])
    tm1 = p.op("tensor", lambda e: e.matmul(
        pst[0:NZ + 1, 0:nh], cofi_r[:, 0:NZ + 1], id_r[0:nh, 0:nh],
        start=True, stop=True), after=[t7, idr])
    tm2 = p.op("tensor", lambda e: e.matmul(
        pst[0:NZ + 1, nh:2 * nh], cofi_r[:, NZ + 1:], id_r[0:nh, 0:nh],
        start=True, stop=True), after=[t7, idr, tm1])
    tA = p.op("vector", lambda e: e.tensor_copy(
        ctsb[:], pst[0:NZ + 1, 0:2 * nh]), after=[tm1, tm2])
    ncg = p.op("vector", lambda e: e.scalar_tensor_tensor(
        negct[:], ctsb[:, 0:nh], -1.0, ctsb[:, nh:2 * nh],
        ALU.mult, ALU.subtract), after=[tA])
    stores = []
    prev_ld = None
    prev_wt = None
    for h in range(nh):
        b1 = p.op("tensor", lambda e, h=h: e.matmul(
            ps2[0:NZ + 1, 0:NZ], ones_r[0:1, 0:NZ + 1],
            corow[:, (NZ + 1) * h:(NZ + 1) * h + NZ], start=True, stop=False),
            after=[t8, onr, prev_wt])
        b2 = p.op("tensor", lambda e, h=h: e.matmul(
            ps2[0:NZ + 1, 0:NZ], ones_r[0:1, 0:NZ + 1],
            firow[:, (NZ + 1) * h:(NZ + 1) * h + NZ], start=False, stop=False),
            after=[t9])
        b3 = p.op("tensor", lambda e: e.matmul(
            ps2[0:NZ + 1, 0:NZ], id_r[0:NZ + 1, 0:NZ + 1], mneg[:],
            start=False, stop=True), after=[mk])
        exw = p.op("scalar", lambda e, h=h: e.activation(
            wt[:], ps2[0:NZ + 1, 0:NZ], AF.Exp, bias=negct[:, h:h + 1]),
            after=[b3, ncg, prev_wt])
        prev_wt = exw
        z0m = p.op("gpsimd", lambda e: e.memset(sext[0:1, :], 0.0),
                   after=[prev_ld])
        ldst = p.op("gpsimd", lambda e, h=h: e.dma_start(
            out=sext[1:NZ + 1, :], in_=stin[h]), after=[z0m], dma=True, sem="ldst")
        evs = []
        for s in range(dn * hd // 512):
            mm = p.op("tensor", lambda e, s=s: e.matmul(
                pst[0:NZ, 0:512], wt[:], sext[:, 512 * s:512 * (s + 1)],
                start=True, stop=True), after=[exw, ldst] + evs)
            ev = p.op("vector", lambda e, s=s: e.tensor_copy(
                pv[:, 512 * s:512 * (s + 1)], pst[0:NZ, 0:512]),
                after=[mm] + (stores[-1:] if stores else []))
            evs.append(ev)
        st = p.op("scalar", lambda e, h=h: e.dma_start(
            out=prevo[h], in_=pv[:]), after=evs, dma=True, sem="stpv")
        stores.append(st)
        prev_ld = evs[-1]
    p.final_wait(stores)
    p.build(nc)
    a.close()
    return nc


_BUILD_CACHE = {}


def _cached(name, fn):
    if name not in _BUILD_CACHE:
        _BUILD_CACHE[name] = fn()
    return _BUILD_CACHE[name]


def kernel(**inputs):
    ii = {k: np.asarray(v) for k, v in inputs.items()}
    x = ii["L0_attn_hidden_states"][0].astype(np.float32)
    cores = list(range(NCORES))

    ncA = _cached("A", build_ln_launch)
    res = run_bass_kernel_spmd(ncA, [
        {"x": np.ascontiguousarray(x[c * TC:(c + 1) * TC])} for c in cores],
        core_ids=cores)
    ln1 = np.concatenate([res.results[c]["y"] for c in cores], 0)

    ln1T = np.ascontiguousarray(ln1.T)
    Wq, Wk, Wv = ii["L0_attn_Wq"], ii["L0_attn_Wk"], ii["L0_attn_Wv"]
    lwf = ii["L0_attn_linear_weight"][:, 0, :]
    lbf = ii["L0_attn_linear_bias"][:, 0]
    lnw, lnb = ii["L0_attn_ln_weight"], ii["L0_attn_ln_bias"]
    W1i, b1i = ii["L0_attn_W1_init"], ii["L0_attn_b1_init"]
    ncB = _cached("B", build_l0attn)
    mapsB = []
    for c in cores:
        hs = slice(4 * c, 4 * c + 4)
        lnw_rep = np.zeros((128, 64), np.float32)
        for hl in range(4):
            lnw_rep[32 * hl:32 * hl + 16] = lnw[4 * c + hl]
        s0 = np.zeros((65, 256), np.float32)
        for hl in range(4):
            s0[0:64, 64 * hl:64 * (hl + 1)] = W1i[4 * c + hl]
            s0[64, 64 * hl:64 * (hl + 1)] = b1i[4 * c + hl, 0]
        mapsB.append({
            "ln1T": ln1T,
            "wqT": np.ascontiguousarray(Wq[256 * c:256 * (c + 1)].T).astype(np.float32),
            "wkT": np.ascontiguousarray(Wk[256 * c:256 * (c + 1)].T).astype(np.float32),
            "wvT": np.ascontiguousarray(Wv[256 * c:256 * (c + 1)].T).astype(np.float32),
            "lwT4": np.ascontiguousarray(lwf[hs].T).astype(np.float32),
            "lb4": np.ascontiguousarray(lbf[hs].reshape(4, 1)).astype(np.float32),
            "lnw_rep": lnw_rep,
            "lnb_row": np.ascontiguousarray(lnb[hs].reshape(1, 256)).astype(np.float32),
            "s0": s0,
        })
    res = run_bass_kernel_spmd(ncB, mapsB, core_ids=cores)
    XCW = np.concatenate([np.asarray(res.results[c]["xcw"]) for c in cores], 1)

    ncC = _cached("C", build_l0tail)
    wo_t = np.ascontiguousarray(ii["L0_attn_Wo"].T).astype(np.float32)
    gw_t = np.ascontiguousarray(ii["L0_ffn_gate_w"].T).astype(np.float32)
    uw_t = np.ascontiguousarray(ii["L0_ffn_up_w"].T).astype(np.float32)
    dw_t = np.ascontiguousarray(ii["L0_ffn_down_w"].T).astype(np.float32)
    gs = np.ascontiguousarray(ii["L0_ffn_gate_s"].reshape(-1, 128).T).astype(np.float32)
    us = np.ascontiguousarray(ii["L0_ffn_up_s"].reshape(-1, 128).T).astype(np.float32)
    ds = ii["L0_ffn_down_s"].reshape(1, -1).astype(np.float32)
    mapsC = [{
        "x": np.ascontiguousarray(x[c * TC:(c + 1) * TC]),
        "xcw": np.ascontiguousarray(XCW[c * TC:(c + 1) * TC]),
        "wo_t": wo_t, "gw_t": gw_t, "uw_t": uw_t, "dw_t": dw_t,
        "gs": gs, "us": us, "ds": ds} for c in cores]
    res = run_bass_kernel_spmd(ncC, mapsC, core_ids=cores)
    x2 = np.concatenate([np.asarray(res.results[c]["x2"]) for c in cores], 0)
    xn3 = np.concatenate([np.asarray(res.results[c]["xn3"]) for c in cores], 0)

    bwT = np.ascontiguousarray(ii["L1_attn_B_w"].T).astype(np.float32)
    cwT = np.ascontiguousarray(ii["L1_attn_C_w"].T).astype(np.float32)
    dtwT = np.ascontiguousarray(ii["L1_attn_dt_w"].T).astype(np.float32)
    bb = ii["L1_attn_B_b"].reshape(-1, 1).astype(np.float32)
    cb = ii["L1_attn_C_b"].reshape(-1, 1).astype(np.float32)
    dtb = ii["L1_attn_dt_b"].reshape(-1, 1).astype(np.float32)
    alog = ii["L1_attn_A_log"].reshape(-1, 1).astype(np.float32)
    baseD = {"bwT": bwT, "cwT": cwT, "dtwT": dtwT, "bb": bb, "cb": cb,
             "dtb": dtb, "alog": alog}
    ncD1 = _cached("D1", lambda: build_l1attn("states"))
    mapsD1 = [dict(baseD, xn3=np.ascontiguousarray(xn3[c * TC:(c + 1) * TC]))
              for c in cores]
    res = run_bass_kernel_spmd(ncD1, mapsD1, core_ids=cores)
    nh, dn, hd, nzl = 32, 64, 64, TC // 256
    NZ = NCORES * nzl
    states = np.zeros((nh, NZ, dn * hd), np.float32)
    dasT = np.zeros((nh, NZ), np.float32)
    for c in cores:
        st = np.asarray(res.results[c]["stex"]).reshape(nzl, nh, dn * hd)
        states[:, nzl * c:nzl * (c + 1), :] = st.transpose(1, 0, 2)
        dasT[:, nzl * c:nzl * (c + 1)] = np.asarray(res.results[c]["dasum"])

    ncD2a = _cached("D2a", lambda: build_l1prev(nh=nh, NZ=NZ))
    res = run_bass_kernel_spmd(ncD2a, [{"stin": states, "dasT": dasT}] * NCORES,
                               core_ids=cores)
    prev = np.asarray(res.results[0]["prev"])

    owT = np.ascontiguousarray(ii["L1_attn_out_w"].T).astype(np.float32)
    ob = ii["L1_attn_out_b"].reshape(1, -1).astype(np.float32)
    drow = ii["L1_attn_D"].reshape(1, -1).astype(np.float32)
    ncD2b = _cached("D2b", lambda: build_l1attn("main"))
    mapsD2 = [dict(baseD,
                   xn3=np.ascontiguousarray(xn3[c * TC:(c + 1) * TC]),
                   x2=np.ascontiguousarray(x2[c * TC:(c + 1) * TC]),
                   drow=drow, owT=owT, ob=ob,
                   prevc=np.ascontiguousarray(prev[:, nzl * c:nzl * (c + 1), :]))
              for c in cores]
    res = run_bass_kernel_spmd(ncD2b, mapsD2, core_ids=cores)
    x3 = np.concatenate([np.asarray(res.results[c]["x3"]) for c in cores], 0)
    xn4 = np.concatenate([np.asarray(res.results[c]["xn4"]) for c in cores], 0)

    pid = pi_perm(D)
    xn4T = np.ascontiguousarray(xn4.T[pid, :])
    ncE = _cached("E", lambda: build_l1ffn())
    mapsE = []
    for c in cores:
        m = {"xn4T": xn4T}
        m.update(prep_l1ffn_core(c, NCORES, ii["L1_ffn_gate_W_packed"],
                                 ii["L1_ffn_gate_scales"],
                                 ii["L1_ffn_up_W_packed"], ii["L1_ffn_up_scales"],
                                 ii["L1_ffn_down_W_packed"],
                                 ii["L1_ffn_down_scales"], D, 16384))
        mapsE.append(m)
    res = run_bass_kernel_spmd(ncE, mapsE, core_ids=cores)
    parts = [np.asarray(res.results[c]["part"]) for c in cores]

    ncF = _cached("F", build_reduce)
    mapsF = []
    for c in cores:
        sl = slice(c * TC, (c + 1) * TC)
        pp = np.stack([pt[sl] for pt in parts], 0)
        mapsF.append({"x3": np.ascontiguousarray(x3[sl]),
                      "pp": np.ascontiguousarray(pp)})
    res = run_bass_kernel_spmd(ncF, mapsF, core_ids=cores)
    x4 = np.concatenate([np.asarray(res.results[c]["y"]) for c in cores], 0)
    return x4[None].astype(np.float32)


# version 04 (final)
# speedup vs baseline: 1.0323x; 1.0140x over previous
import numpy as np
import ml_dtypes
from collections import defaultdict
import concourse.bass as bass
import concourse.mybir as mybir
from concourse.bass_utils import run_bass_kernel_spmd

F32 = mybir.dt.float32
BF16 = mybir.dt.bfloat16
I32 = mybir.dt.int32
U16 = mybir.dt.uint16
AF = mybir.ActivationFunctionType
ALU = mybir.AluOpType
AX = mybir.AxisListType
BF16NP = ml_dtypes.bfloat16

NCORES = 8
D = 2048
T = 16384
TC = T // NCORES
NTILE = TC // 128

ENGINES = ("sync", "tensor", "vector", "scalar", "gpsimd")


class Prog:

    def __init__(self):
        self.ops = []
        self.sem_counts = defaultdict(int)
        self._sem_cms = []
        self.sem_handles = {}
        self._dma_sems = set()

    def op(self, engine, fn, after=(), dma=False, sem=None):
        assert engine in ENGINES, engine
        if sem is None:
            sem = engine + ("_dma" if dma else "")
        inc = 16 if dma else 1
        if dma:
            self._dma_sems.add(sem)
        aft = []
        for a in after:
            if a is None:
                continue
            asem, aval = a
            if asem in self._dma_sems:
                aval = self.sem_counts[asem]
            aft.append((asem, aval))
        self.sem_counts[sem] += inc
        tokval = self.sem_counts[sem]
        self.ops.append((engine, fn, tuple(aft), sem, inc, tokval))
        return (sem, tokval)

    def final_wait(self, toks, engine="sync"):
        return self.op(engine, lambda e: e.nop(), after=toks)

    def alloc_sems(self, nc):
        for name in self.sem_counts:
            cm = nc.semaphore(name)
            h = cm.__enter__()
            self._sem_cms.append(cm)
            self.sem_handles[name] = h

    def emit(self, block):
        per_engine = defaultdict(list)
        for rec in self.ops:
            per_engine[rec[0]].append(rec)
        hwm = {e: defaultdict(int) for e in ENGINES}
        sh = self.sem_handles

        def section_for(engine):
            def _(eng):
                h = hwm[engine]
                for (_, fn, after, sem, inc, tokval) in per_engine[engine]:
                    for (wsem, wval) in after:
                        if h[wsem] < wval:
                            eng.wait_ge(sh[wsem], wval)
                            h[wsem] = wval
                    inst = fn(eng)
                    inst.then_inc(sh[sem], inc)
            return _

        for engine in ENGINES:
            if per_engine[engine]:
                getattr(block, engine)(section_for(engine))

    def build(self, nc):
        self.alloc_sems(nc)
        with nc.Block() as block:
            self.emit(block)


class Alloc:

    def __init__(self, nc, prog=None):
        import contextlib
        self.nc = nc
        self.es = contextlib.ExitStack()
        self.prog = prog
        self._const_tile = None
        self._const_cols = {}
        self._const_tok = None

    def sb(self, name, shape, dt=F32):
        return self.es.enter_context(self.nc.sbuf_tensor(name, list(shape), dt))

    def ps(self, name, shape, dt=F32):
        return self.es.enter_context(self.nc.psum_tensor(name, list(shape), dt))

    def const(self, value):
        if self._const_tile is None:
            self._const_tile = self.sb("consts", [128, 16])
        if value not in self._const_cols:
            c = len(self._const_cols)
            assert c < 16
            self._const_cols[value] = c
            ap = self._const_tile[:, c:c + 1]
            tok = self.prog.op("gpsimd", lambda e, ap=ap, v=value: e.memset(ap, v),
                               after=[self._const_tok])
            self._const_tok = tok
            self._const_cols[value] = (c, tok)
        c, tok = self._const_cols[value]
        return self._const_tile[:, c:c + 1], tok

    def close(self):
        self.es.close()


class Overlay:

    def __init__(self, nc, a, name, size_bytes):
        self.nc = nc
        self.handle = a.sb(name, [128, size_bytes], mybir.dt.uint8)
        self.base = nc.lookup_mloc(self.handle).addr

    def at(self, name, offset, shape, dtype):
        h = self.nc.virtual_sbuf_tensor(name, list(shape), dtype)
        self.nc._tensor(h.name, h.shape, h.dtype, type="SB")
        mloc = self.nc.lookup_mloc(h)
        mloc.allocated = True
        mloc.addr = self.base + offset
        return h


def emit_ln(p, a, xt, xn_out, col, stats, sqscr, deps, prev_sq, eps=1e-5,
            d=D):
    eps_ap, eps_tok = a.const(eps)
    """LayerNorm of one token-major tile xt [128, d] fp32 -> xn_out (bf16 or
    f32). stats: sbuf [128, ncols*6] scratch; col: which stat column group.
    Returns (token_of_xn, token_of_sq) ."""
    c = col * 6
    red = stats[:, c + 0:c + 1]
    mu = stats[:, c + 1:c + 2]
    sqs = stats[:, c + 2:c + 3]
    e2 = stats[:, c + 3:c + 4]
    nvar = stats[:, c + 4:c + 5]
    rstd = stats[:, c + 5:c + 6]
    t1 = p.op("vector", lambda e: e.reduce_sum(red, xt[:], axis=AX.X), after=deps)
    t2 = p.op("scalar", lambda e: e.activation(sqscr[:], xt[:], AF.Square,
                                               accum_out=sqs),
              after=list(deps) + ([prev_sq] if prev_sq else []))
    t3 = p.op("vector", lambda e: e.tensor_scalar(mu, red, 1.0 / d, None, ALU.mult),
              after=[t1])
    t4 = p.op("vector", lambda e: e.tensor_scalar(e2, sqs, 1.0 / d, None, ALU.mult),
              after=[t2])
    t5 = p.op("vector", lambda e: e.scalar_tensor_tensor(nvar, mu, mu, e2,
                                                         ALU.mult, ALU.subtract),
              after=[t3, t4])
    t6 = p.op("scalar", lambda e: e.activation(stats[:, c + 4:c + 5], nvar, AF.Sqrt,
                                               scale=-1.0, bias=eps_ap),
              after=[t5, eps_tok])
    t7 = p.op("vector", lambda e: e.reciprocal(rstd, stats[:, c + 4:c + 5]),
              after=[t6])
    t8 = p.op("vector", lambda e: e.tensor_scalar(xn_out[:], xt[:], mu, rstd,
                                                  ALU.subtract, ALU.mult),
              after=[t3, t7] + list(deps))
    return t8, t2


def build_ln_launch():
    nc = bass.Bass("TRN2", target_bir_lowering=False)
    x = nc.dram_tensor("x", [TC, D], F32, kind="ExternalInput").ap()
    y = nc.dram_tensor("y", [TC, D], BF16, kind="ExternalOutput").ap()
    p = Prog()
    a = Alloc(nc, p)
    xt = [a.sb(f"xt{i}", [128, D]) for i in range(2)]
    xn = [a.sb(f"xn{i}", [128, D], BF16) for i in range(2)]
    stats = a.sb("stats", [128, NTILE * 6])
    sqscr = a.sb("sqscr", [128, D])
    prev_use = [None, None]
    prev_store = [None, None]
    prev_sq = None
    stores = []
    for i in range(NTILE):
        b = i % 2
        ld = p.op("sync", lambda e, i=i, b=b: e.dma_start(
            out=xt[b][:], in_=x[128 * i:128 * (i + 1), :]),
            after=[prev_use[b]], dma=True, sem=f"ld{b}")
        t8, prev_sq = emit_ln(p, a, xt[b], xn[b], i, stats, sqscr,
                              [ld, prev_store[b]], prev_sq)
        prev_use[b] = t8
        st = p.op("scalar", lambda e, i=i, b=b: e.dma_start(
            out=y[128 * i:128 * (i + 1), :], in_=xn[b][:]),
            after=[t8], dma=True, sem=f"st{b}")
        prev_store[b] = st
        stores.append(st)
    p.final_wait(stores)
    p.build(nc)
    a.close()
    return nc


def host_ln(x_full):
    nc = build_ln_launch()
    in_maps = [{"x": np.ascontiguousarray(x_full[c * TC:(c + 1) * TC])}
               for c in range(NCORES)]
    res = run_bass_kernel_spmd(nc, in_maps, core_ids=list(range(NCORES)))
    return np.concatenate([res.results[c]["y"] for c in range(NCORES)], axis=0)


def build_l0tail(tc=TC, d=D, inter=5632):
    nt = tc // 128
    ndc = d // 128
    nts = tc // 512
    nds = d // 512
    nis = inter // 512
    nib = inter // 128
    nc = bass.Bass("TRN2", target_bir_lowering=False)
    x = nc.dram_tensor("x", [tc, d], F32, kind="ExternalInput").ap()
    xcw = nc.dram_tensor("xcw", [tc, d], BF16, kind="ExternalInput").ap()
    wo_t = nc.dram_tensor("wo_t", [d, d], F32, kind="ExternalInput").ap()
    gw_t = nc.dram_tensor("gw_t", [d, inter], F32, kind="ExternalInput").ap()
    uw_t = nc.dram_tensor("uw_t", [d, inter], F32, kind="ExternalInput").ap()
    dw_t = nc.dram_tensor("dw_t", [inter, d], F32, kind="ExternalInput").ap()
    gs = nc.dram_tensor("gs", [128, nib], F32, kind="ExternalInput").ap()
    us = nc.dram_tensor("us", [128, nib], F32, kind="ExternalInput").ap()
    ds = nc.dram_tensor("ds", [1, d], F32, kind="ExternalInput").ap()
    x2o = nc.dram_tensor("x2", [tc, d], F32, kind="ExternalOutput").ap()
    xn3o = nc.dram_tensor("xn3", [tc, d], BF16, kind="ExternalOutput").ap()
    x1s = nc.dram_tensor("x1s", [tc, d], F32, kind="Internal").ap()
    xn2s = nc.dram_tensor("xn2s", [tc, d], BF16, kind="Internal").ap()
    hdns = nc.dram_tensor("hdns", [inter, tc], BF16, kind="Internal").ap()
    dfms = nc.dram_tensor("dfms", [d, tc], BF16, kind="Internal").ap()

    p = Prog()
    a = Alloc(nc, p)
    A = [a.sb(f"A{i}", [128, d], BF16) for i in range(ndc)]
    B = a.sb("B", [128, ndc * d], BF16)
    _dw0 = a.sb("DW0", [128, nib * 128], BF16)
    DW = [_dw0, None]
    xt = [a.sb(f"xt{i}", [128, d]) for i in range(2)]
    x1t = [a.sb(f"x1t{i}", [128, d]) for i in range(2)]
    x2t = xt
    xnt = [a.sb(f"xnt{i}", [128, d], BF16) for i in range(2)]
    dnt = [a.sb(f"dnt{i}", [128, d], BF16) for i in range(2)]
    gst = a.sb("gst", [128, nib])
    ust = a.sb("ust", [128, nib])
    dsrow = a.sb("dsrow", [1, d], BF16)
    ones1 = a.sb("ones1", [1, 128], BF16)
    dsb = a.sb("dsb", [128, d], BF16)
    stats = a.sb("stats", [128, 2 * nt * 6])
    sqscr = a.sb("sqscr", [128, d], BF16)
    hout = [a.sb(f"hout{i}", [128, 512], BF16) for i in range(2)]
    gtmp = a.sb("gtmp", [128, 512])
    stmp = a.sb("stmp", [128, 512], BF16)
    utmp = a.sb("utmp", [128, 512], BF16)
    dftile = hout
    ps = [a.ps(f"ps{i}", [128, 512]) for i in range(6)]
    DW[1] = B[:, 0:nib * 128]

    ld_gs = p.op("sync", lambda e: e.dma_start(out=gst[:], in_=gs[:]), dma=True, sem="lds")
    ld_us = p.op("sync", lambda e: e.dma_start(out=ust[:], in_=us[:]), dma=True, sem="lds")
    ld_ds = p.op("gpsimd", lambda e: e.dma_start(out=dsrow[:], in_=ds[:]), dma=True, sem="ldds")
    on1 = p.op("gpsimd", lambda e: e.memset(ones1[:], 1.0))
    dsb_toks = []
    for s in range(nds):
        mm = p.op("tensor", lambda e, s=s: e.matmul(
            ps[5][:], ones1[:], dsrow[0:1, 512 * s:512 * (s + 1)],
            start=True, stop=True), after=[ld_ds, on1] + dsb_toks)
        ev = p.op("vector", lambda e, s=s: e.tensor_copy(
            dsb[:, 512 * s:512 * (s + 1)], ps[5][:]), after=[mm])
        dsb_toks = [ev]
    dsb_tok = dsb_toks[0]
    ld_wo = p.op("gpsimd", lambda e: e.dma_start(
        out=B[:].rearrange("p (c n) -> p c n", c=ndc),
        in_=wo_t.rearrange("(c p) n -> p c n", p=128)), dma=True, sem="ldwo")
    tp_toks = []
    for c in range(ndc):
        tp = p.op("sync", lambda e, c=c: e.dma_start(
            out=A[c][:, 0:tc], in_=xcw[:, 128 * c:128 * (c + 1)], transpose=True),
            dma=True, sem="ldtp")
        tp_toks.append(tp)

    prev_x = [None, None]
    prev_x1 = [None, None]
    prev_xn = [None, None]
    prev_sq = None
    bank_busy = {i: None for i in range(6)}
    bank_busy[5] = dsb_tok
    lastA = []
    lastB = []
    p1_stores = []
    for i in range(nt):
        b = i % 2
        ldx = p.op("sync", lambda e, i=i, b=b: e.dma_start(
            out=xt[b][:], in_=x[128 * i:128 * (i + 1), :]),
            after=[prev_x[b]], dma=True, sem=f"ldx{b}")
        for s in range(nds):
            bank = ps[s % 4]
            mms = []
            for c in range(ndc):
                mm = p.op("tensor", lambda e, i=i, s=s, c=c, bank=bank: e.matmul(
                    bank[:], A[c][:, 128 * i:128 * (i + 1)],
                    B[:, (c * d) + 512 * s:(c * d) + 512 * (s + 1)],
                    start=(c == 0), stop=(c == ndc - 1)),
                    after=[tp_toks[c], ld_wo] + ([bank_busy[s % 4]] if c == 0 else []))
                mms.append(mm)
            lastA.append(mms[-1])
            lastB.append(mms[-1])
            ev = p.op("vector", lambda e, i=i, s=s, b=b, bank=bank: e.tensor_add(
                x1t[b][:, 512 * s:512 * (s + 1)], bank[:],
                xt[b][:, 512 * s:512 * (s + 1)]), after=[mms[-1], ldx, prev_x1[b]])
            bank_busy[s % 4] = ev
            prev_x1[b] = ev
        t8, prev_sq = emit_ln(p, a, x1t[b], xnt[b], i, stats, sqscr,
                              [prev_x1[b], prev_xn[b]], prev_sq, d=d)
        prev_xn[b] = t8
        st1 = p.op("scalar", lambda e, i=i, b=b: e.dma_start(
            out=x1s[128 * i:128 * (i + 1), :], in_=x1t[b][:]),
            after=[prev_x1[b]], dma=True, sem=f"stx1{b}")
        st2 = p.op("scalar", lambda e, i=i, b=b: e.dma_start(
            out=xn2s[128 * i:128 * (i + 1), :], in_=xnt[b][:]),
            after=[t8], dma=True, sem=f"stxn{b}")
        prev_x[b] = ev
        prev_x1[b] = st1
        prev_xn[b] = st2
        p1_stores += [st1, st2]

    tp2_toks = []
    for c in range(ndc):
        tp = p.op("sync", lambda e, c=c: e.dma_start(
            out=A[c][:, 0:tc], in_=xn2s[:, 128 * c:128 * (c + 1)], transpose=True),
            after=lastA + p1_stores, dma=True, sem="ldtp2")
        tp2_toks.append(tp)
    nseg = (d // 512)
    seg_busy = [lastB + p1_stores] * 4
    hstore_toks = []
    hd_idx = 0
    prev_e2 = prev_e4 = None
    last_p2_mm = None
    for isl in range(nis):
        for mi, (wt, sct) in enumerate(((gw_t, gst), (uw_t, ust))):
            seg = (isl * 2 + mi) % 4
            ldw = p.op("gpsimd", lambda e, isl=isl, wt=wt, seg=seg: e.dma_start(
                out=B[:].rearrange("p (c n) -> p c n", c=ndc)[:, :, 512 * seg:512 * (seg + 1)],
                in_=wt[:, 512 * isl:512 * (isl + 1)].rearrange("(c p) n -> p c n", p=128)),
                after=seg_busy[seg], dma=True, sem=f"ldw{seg}")
            if mi == 0:
                ldg_tok = ldw
            else:
                ldu_tok = ldw
        for ib in range(4):
            iblk = isl * 4 + ib
            for ts in range(nts):
                mmsg, mmsu = [], []
                bg, bu = ps[(ts % 2) * 2], ps[(ts % 2) * 2 + 1]
                gi, ui = (ts % 2) * 2, (ts % 2) * 2 + 1
                for c in range(ndc):
                    lg = B[:, c * d + 512 * ((isl * 2 + 0) % 4) + 128 * ib:
                           c * d + 512 * ((isl * 2 + 0) % 4) + 128 * (ib + 1)]
                    lu = B[:, c * d + 512 * ((isl * 2 + 1) % 4) + 128 * ib:
                           c * d + 512 * ((isl * 2 + 1) % 4) + 128 * (ib + 1)]
                    mg = p.op("tensor", lambda e, lg=lg, c=c, ts=ts, bg=bg: e.matmul(
                        bg[:], lg, A[c][:, 512 * ts:512 * (ts + 1)],
                        start=(c == 0), stop=(c == ndc - 1)),
                        after=[ldg_tok, tp2_toks[c]] + ([bank_busy[gi]] if c == 0 else []))
                    mu = p.op("tensor", lambda e, lu=lu, c=c, ts=ts, bu=bu: e.matmul(
                        bu[:], lu, A[c][:, 512 * ts:512 * (ts + 1)],
                        start=(c == 0), stop=(c == ndc - 1)),
                        after=[ldu_tok, tp2_toks[c]] + ([bank_busy[ui]] if c == 0 else []))
                    mmsg.append(mg)
                    mmsu.append(mu)
                seg_busy[(isl * 2 + 0) % 4] = [mmsg[-1]]
                seg_busy[(isl * 2 + 1) % 4] = [mmsu[-1]]
                last_p2_mm = mmsu[-1]
                hb = hout[hd_idx % 2]
                e1 = p.op("vector", lambda e, bg=bg, iblk=iblk: e.tensor_scalar(
                    gtmp[:], bg[:], gst[:, iblk:iblk + 1], None, ALU.mult),
                    after=[mmsg[-1], ld_gs] + ([prev_e2] if prev_e2 else []))
                e2 = p.op("scalar", lambda e: e.activation(stmp[:], gtmp[:], AF.Silu),
                          after=[e1] + ([prev_e4] if prev_e4 else []))
                e3 = p.op("vector", lambda e, bu=bu, iblk=iblk: e.tensor_scalar(
                    utmp[:], bu[:], ust[:, iblk:iblk + 1], None, ALU.mult),
                    after=[mmsu[-1], ld_us, e1] + ([prev_e4] if prev_e4 else []))
                e4 = p.op("vector", lambda e, hb=hb: e.tensor_mul(hb[:], stmp[:], utmp[:]),
                          after=[e2, e3] + ([hstore_toks[-2]] if len(hstore_toks) >= 2 else []))
                bank_busy[gi] = e1
                bank_busy[ui] = e3
                prev_e2, prev_e4 = e2, e4
                sth = p.op("scalar", lambda e, hb=hb, iblk=iblk, ts=ts: e.dma_start(
                    out=hdns[128 * iblk:128 * (iblk + 1), 512 * ts:512 * (ts + 1)],
                    in_=hb[:]), after=[e4], dma=True, sem=f"sth{hd_idx % 2}")
                hstore_toks.append(sth)
                hd_idx += 1

    def hslice_tile(ibk):
        return A[ibk // 4][:, 512 * (ibk % 4):512 * (ibk % 4 + 1)]
    last_dfm = [None, None]
    dw_busy = [None, None]
    first_dw = [True]
    df_stores = []
    for ts in range(nts):
        ldh_toks = []
        for c4 in range((nib + 3) // 4):
            n_ib = min(4, nib - c4 * 4)
            dst = A[c4][:].rearrange("p (s n) -> p s n", s=d // 512)[:, 0:n_ib, :]
            src = hdns[128 * 4 * c4:128 * (4 * c4 + n_ib), 512 * ts:512 * (ts + 1)]
            ldh = p.op("sync", lambda e, dst=dst, src=src: e.dma_start(
                out=dst, in_=src.rearrange("(s p) n -> p s n", p=128)),
                after=([last_p2_mm] + hstore_toks) if ts == 0 else [df_stores[-1]],
                dma=True, sem=f"ldh{c4 % 2}")
            ldh_toks.append(ldh)
        for db in range(ndc):
            wb = DW[db % 2]
            ldw = p.op("gpsimd", lambda e, db=db, ts=ts, wb=wb: e.dma_start(
                out=wb.rearrange("p (c n) -> p c n", c=nib)
                if wb is DW[1] else wb[:].rearrange("p (c n) -> p c n", c=nib),
                in_=dw_t[:, 128 * db:128 * (db + 1)].rearrange("(c p) n -> p c n", p=128)),
                after=[dw_busy[db % 2]] +
                      (sum(seg_busy, []) if first_dw[0] else []),
                dma=True, sem=f"lddw{db % 2}")
            first_dw[0] = False
            bank = ps[4 + (db % 2)]
            mms = []
            for ib in range(nib):
                mm = p.op("tensor", lambda e, ib=ib, wb=wb, bank=bank: e.matmul(
                    bank[:], wb[:, 128 * ib:128 * (ib + 1)], hslice_tile(ib),
                    start=(ib == 0), stop=(ib == nib - 1)),
                    after=[ldw] + ldh_toks + ([bank_busy[4 + (db % 2)]] if ib == 0 else []))
                mms.append(mm)
            dw_busy[db % 2] = mms[-1]
            dft = dftile[db % 2]
            ev = p.op("scalar", lambda e, dft=dft, bank=bank: e.activation(
                dft[:], bank[:], AF.Identity), after=[mms[-1], last_dfm[db % 2]])
            bank_busy[4 + (db % 2)] = ev
            stf = p.op("scalar", lambda e, db=db, ts=ts, dft=dft: e.dma_start(
                out=dfms[128 * db:128 * (db + 1), 512 * ts:512 * (ts + 1)],
                in_=dft[:]), after=[ev], dma=True, sem=f"stdf{db % 2}")
            last_dfm[db % 2] = stf
            df_stores.append(stf)

    prev2_x1 = [None, None]
    prev2_dn = [None, None]
    prev2_x2 = [None, None]
    prev2_st = [None, None]
    prev_sq = None
    out_stores = []
    for i in range(nt):
        b = i % 2
        ldx1 = p.op("sync", lambda e, i=i, b=b: e.dma_start(
            out=x1t[b][:], in_=x1s[128 * i:128 * (i + 1), :]),
            after=[prev2_x1[b]] + (df_stores if i == 0 else []), dma=True, sem=f"ldx1{b}")
        tpd = p.op("sync", lambda e, i=i, b=b: e.dma_start(
            out=dnt[b][:], in_=dfms[:, 128 * i:128 * (i + 1)], transpose=True),
            after=[prev2_dn[b]] + (df_stores if i == 0 else []), dma=True, sem=f"tpd{b}")
        t1 = p.op("vector", lambda e, b=b: e.tensor_mul(sqscr[:], dnt[b][:], dsb[:]),
                  after=[tpd, dsb_tok, prev_sq])
        t2 = p.op("vector", lambda e, b=b: e.tensor_add(x2t[b][:], sqscr[:], x1t[b][:]),
                  after=[t1, ldx1, prev2_x2[b], prev2_st[b]])
        stx2 = p.op("scalar", lambda e, i=i, b=b: e.dma_start(
            out=x2o[128 * i:128 * (i + 1), :], in_=x2t[b][:]),
            after=[t2], dma=True, sem=f"stx2{b}")
        t8, prev_sq = emit_ln(p, a, x2t[b], xnt[b], nt + i, stats, sqscr,
                              [t2, prev_xn[b]], prev_sq, d=d)
        prev_xn[b] = t8
        stxn = p.op("scalar", lambda e, i=i, b=b: e.dma_start(
            out=xn3o[128 * i:128 * (i + 1), :], in_=xnt[b][:]),
            after=[t8], dma=True, sem=f"stxn3{b}")
        prev_xn[b] = stxn
        prev2_x1[b] = t2
        prev2_dn[b] = t1
        prev2_x2[b] = t8
        prev2_st[b] = stx2
        out_stores += [stx2, stxn]
    p.final_wait(out_stores)
    p.build(nc)
    a.close()
    return nc


def emit_dequant(p, a, pk_tile, sc_tile, out_lo, out_hi, tmp, deps):
    toks = []
    for half, shift in ((0, None), (1, 4)):
        A_, B_, C_, U_ = tmp["a"], tmp["b"], tmp["c"], tmp["u16"]
        if half == 0:
            d0 = p.op("vector", lambda e: e.tensor_scalar(
                A_, pk_tile, 15, None, ALU.bitwise_and), after=deps)
        else:
            d0 = p.op("vector", lambda e: e.tensor_scalar(
                A_, pk_tile, 4, None, ALU.logical_shift_right), after=deps)
        d1 = p.op("vector", lambda e: e.tensor_scalar(
            B_, A_, 7, None, ALU.bitwise_and), after=[d0])
        d2 = p.op("vector", lambda e: e.tensor_scalar(
            C_, A_, 8, 12, ALU.bitwise_and, ALU.logical_shift_left), after=[d0])
        d3 = p.op("vector", lambda e: e.tensor_scalar(
            A_, B_, 64, 16128, ALU.mult, ALU.add), after=[d1, d2])
        d4 = p.op("vector", lambda e: e.tensor_add(A_, A_, C_), after=[d3])
        d5 = p.op("vector", lambda e: e.tensor_scalar(
            C_, B_, 1, None, ALU.is_le), after=[d4, d1])
        d6 = p.op("vector", lambda e: e.tensor_scalar(
            B_, B_, -16064, 16128, ALU.mult, ALU.add), after=[d5])
        d7 = p.op("vector", lambda e: e.tensor_mul(B_, B_, C_), after=[d6, d5])
        d8 = p.op("vector", lambda e: e.tensor_sub(A_, A_, B_), after=[d7, d4])
        d9 = p.op("vector", lambda e: e.tensor_copy(U_, A_), after=[d8])
        out = out_lo if half == 0 else out_hi
        d10 = p.op("vector", lambda e, out=out: e.tensor_mul(
            out, U_.bitcast(BF16), sc_tile), after=[d9] + deps)
        toks.append(d10)
    return toks


def build_l1ffn(t=T, d=D, ic=2048):
    nt512 = t // 512
    ndc = d // 128
    nwd = d // 256
    nwi = ic // 256
    nib = ic // 128
    nds = d // 512
    ntb = t // 128
    wd = d // 2 // 128
    nc = bass.Bass("TRN2", target_bir_lowering=False)
    xn4T = nc.dram_tensor("xn4T", [d, t], BF16, kind="ExternalInput").ap()
    gpk = nc.dram_tensor("gpk", [d // 2, ic], I32, kind="ExternalInput").ap()
    upk = nc.dram_tensor("upk", [d // 2, ic], I32, kind="ExternalInput").ap()
    gsc = nc.dram_tensor("gsc", [d // 2, ic], F32, kind="ExternalInput").ap()
    usc = nc.dram_tensor("usc", [d // 2, ic], F32, kind="ExternalInput").ap()
    dpk = nc.dram_tensor("dpk", [ic // 2, d], I32, kind="ExternalInput").ap()
    dsc = nc.dram_tensor("dsc", [ic // 2, d], F32, kind="ExternalInput").ap()
    part = nc.dram_tensor("part", [t, d], F32, kind="ExternalOutput").ap()
    hdns = nc.dram_tensor("hdns", [ic, t], BF16, kind="Internal").ap()

    p = Prog()
    a = Alloc(nc, p)
    wdq = max(ic, d)
    hw_ = wdq // 2
    GQ = [a.sb(f"GQ{i}", [128, wdq], BF16) for i in range(ndc)]
    UQ = [a.sb(f"UQ{i}", [128, ic], BF16) for i in range(ndc)]
    pk_t = a.sb("pk_t", [128, hw_], I32)
    sc_t = a.sb("sc_t", [128, hw_])
    tmp = {"a": a.sb("tmpa", [128, hw_], I32), "b": a.sb("tmpb", [128, hw_], I32),
           "c": a.sb("tmpc", [128, hw_], I32), "u16": a.sb("tmpu", [128, hw_], U16)}
    _xs0 = a.sb("xs0", [128, ndc * 512], BF16)
    xs = [_xs0, _xs0]
    g2 = a.sb("g2", [128, 512])
    aat = a.sb("aat", [128, 512])
    cct = a.sb("cct", [128, 512])
    tht = a.sb("tht", [128, 512], BF16)
    h1t = a.sb("h1t", [128, 512], BF16)
    ggt = a.sb("ggt", [128, 512])
    hdt = [a.sb(f"hdt{i}", [128, 512], BF16) for i in range(2)]
    hld = [a.sb(f"hld{i}", [128, nib * 128], BF16) for i in range(2)]
    ott = [a.sb(f"ott{i}", [128, 512]) for i in range(2)]
    ps = [a.ps(f"ps{i}", [128, 512]) for i in range(6)]

    dq_toks = []
    ld_prev = None
    hwc = min(hw_, ic)
    nh = max(1, ic // hwc)
    dqg = [None] * ndc
    dqu = [None] * ndc
    for mat, (pk, sc, out_arena) in enumerate(((gpk, gsc, GQ), (upk, usc, UQ))):
        for k in range(nwd):
            for h in range(nh):
                cs, ce = h * hwc, (h + 1) * hwc
                ldp = p.op("sync", lambda e, pk=pk, k=k, cs=cs, ce=ce: e.dma_start(
                    out=pk_t[:, 0:ce - cs], in_=pk[128 * k:128 * (k + 1), cs:ce]),
                    after=[ld_prev], dma=True, sem="ldpk")
                lds = p.op("sync", lambda e, sc=sc, k=k, cs=cs, ce=ce: e.dma_start(
                    out=sc_t[:, 0:ce - cs], in_=sc[128 * k:128 * (k + 1), cs:ce]),
                    after=[ld_prev], dma=True, sem="ldsc")
                toks = emit_dequant(p, a, pk_t[:, 0:ce - cs], sc_t[:, 0:ce - cs],
                                    out_arena[2 * k][:, cs:ce],
                                    out_arena[2 * k + 1][:, cs:ce],
                                    {k2: v[:, 0:ce - cs] for k2, v in tmp.items()},
                                    [ldp, lds])
                dq_toks += toks
                ld_prev = toks[-1]
                tgt = dqg if mat == 0 else dqu
                tgt[2 * k] = ld_prev
                tgt[2 * k + 1] = ld_prev

    bank_busy = {i: None for i in range(6)}
    hst = [None, None]
    prev_sw = {}
    xs_busy = [None, None]
    last_hstore = []
    for tsl in range(nt512):
        bx = 0
        ldx = p.op("sync", lambda e, tsl=tsl, bx=bx: e.dma_start(
            out=xs[bx][:].rearrange("p (c n) -> p c n", c=ndc),
            in_=xn4T[:, 512 * tsl:512 * (tsl + 1)].rearrange("(c p) n -> p c n", p=128)),
            after=[xs_busy[bx]], dma=True, sem=f"ldxs{bx}")
        for ib in range(nib):
            bg, bu = ps[(ib % 2) * 2], ps[(ib % 2) * 2 + 1]
            mg = mu_ = None
            for c in range(ndc):
                mg = p.op("tensor", lambda e, c=c, ib=ib, bg=bg, bx=bx: e.matmul(
                    bg[:], GQ[c][:, 128 * ib:128 * (ib + 1)],
                    xs[bx][:, 512 * c:512 * (c + 1)],
                    start=(c == 0), stop=(c == ndc - 1)),
                    after=[ldx, dqg[c]] + ([bank_busy[(ib % 2) * 2]] if c == 0 else []))
                mu_ = p.op("tensor", lambda e, c=c, ib=ib, bu=bu, bx=bx: e.matmul(
                    bu[:], UQ[c][:, 128 * ib:128 * (ib + 1)],
                    xs[bx][:, 512 * c:512 * (c + 1)],
                    start=(c == 0), stop=(c == ndc - 1)),
                    after=[ldx, dqu[c]] + ([bank_busy[(ib % 2) * 2 + 1]] if c == 0 else []))
            xs_busy[bx] = mu_
            pe = prev_sw
            q1 = p.op("scalar", lambda e, bg=bg: e.activation(g2[:], bg[:], AF.Square),
                      after=[mg, pe.get("q2")])
            q2 = p.op("vector", lambda e: e.tensor_scalar(
                aat[:], g2[:], 0.0356774081, 0.7978845608, ALU.mult, ALU.add),
                after=[q1, pe.get("q3")])
            q3 = p.op("vector", lambda e, bg=bg: e.tensor_mul(cct[:], aat[:], bg[:]),
                      after=[q2, pe.get("q4")])
            q4 = p.op("scalar", lambda e: e.activation(tht[:], cct[:], AF.Tanh),
                      after=[q3, pe.get("q5")])
            q5 = p.op("vector", lambda e: e.tensor_scalar(
                h1t[:], tht[:], 0.5, 0.5, ALU.mult, ALU.add),
                after=[q4, pe.get("q6")])
            q6 = p.op("vector", lambda e, bg=bg: e.tensor_mul(ggt[:], h1t[:], bg[:]),
                      after=[q5, pe.get("q7")])
            bank_busy[(ib % 2) * 2] = q6
            hb = hdt[ib % 2]
            q7 = p.op("vector", lambda e, bu=bu, hb=hb: e.tensor_mul(
                hb[:], ggt[:], bu[:]), after=[q6, mu_, hst[ib % 2]])
            bank_busy[(ib % 2) * 2 + 1] = q7
            st = p.op("scalar", lambda e, ib=ib, tsl=tsl, hb=hb: e.dma_start(
                out=hdns[128 * ib:128 * (ib + 1), 512 * tsl:512 * (tsl + 1)],
                in_=hb[:]), after=[q7], dma=True, sem=f"sthd{ib % 2}")
            hst[ib % 2] = st
            prev_sw = {"q2": q2, "q3": q3, "q4": q4, "q5": q5, "q6": q6, "q7": q7}
            last_hstore.append(st)

    dq2_toks = []
    ld_prev = [bank_busy[0], bank_busy[1], bank_busy[2], bank_busy[3],
               xs_busy[0], xs_busy[1]]
    for k in range(nwi):
        hwd = min(hw_, d)
        for h in range(max(1, d // hwd)):
            cs, ce = h * hwd, (h + 1) * hwd
            ldp = p.op("sync", lambda e, k=k, cs=cs, ce=ce: e.dma_start(
                out=pk_t[:, 0:ce - cs], in_=dpk[128 * k:128 * (k + 1), cs:ce]),
                after=ld_prev, dma=True, sem="ldpk")
            lds = p.op("sync", lambda e, k=k, cs=cs, ce=ce: e.dma_start(
                out=sc_t[:, 0:ce - cs], in_=dsc[128 * k:128 * (k + 1), cs:ce]),
                after=ld_prev, dma=True, sem="ldsc")
            toks = emit_dequant(p, a, pk_t[:, 0:ce - cs], sc_t[:, 0:ce - cs],
                                GQ[2 * k][:, cs:ce], GQ[2 * k + 1][:, cs:ce],
                                {k2: v[:, 0:ce - cs] for k2, v in tmp.items()},
                                [ldp, lds])
            dq2_toks += toks
            ld_prev = [toks[-1]]

    hl_busy = [None, None]
    st_busy = [None, None]
    for tb in range(ntb):
        bh = tb % 2
        ldh = p.op("sync", lambda e, tb=tb, bh=bh: e.dma_start(
            out=hld[bh][:].rearrange("p (c n) -> p c n", c=nib),
            in_=hdns[:, 128 * tb:128 * (tb + 1)].rearrange("(c p) n -> p c n", p=128)),
            after=[hl_busy[bh]] + (last_hstore if tb < 2 else []),
            dma=True, sem=f"ldhd{bh}")
        lastmm = None
        for dsl in range(nds):
            bank = ps[4 + (dsl % 2)]
            mms = []
            for ib in range(nib):
                mm = p.op("tensor", lambda e, ib=ib, dsl=dsl, bh=bh, bank=bank: e.matmul(
                    bank[:], hld[bh][:, 128 * ib:128 * (ib + 1)],
                    GQ[ib][:, 512 * dsl:512 * (dsl + 1)],
                    start=(ib == 0), stop=(ib == nib - 1)),
                    after=[ldh, dq2_toks[-1]] + ([bank_busy[4 + (dsl % 2)]] if ib == 0 else []))
                mms.append(mm)
            lastmm = mms[-1]
            ev = p.op("vector", lambda e, dsl=dsl, bank=bank: e.tensor_copy(
                ott[dsl % 2][:], bank[:]), after=[mms[-1], st_busy[dsl % 2]])
            bank_busy[4 + (dsl % 2)] = ev
            st = p.op("scalar", lambda e, tb=tb, dsl=dsl: e.dma_start(
                out=part[128 * tb:128 * (tb + 1), 512 * dsl:512 * (dsl + 1)],
                in_=ott[dsl % 2][:]), after=[ev], dma=True, sem=f"stp{dsl % 2}")
            st_busy[dsl % 2] = st
        hl_busy[bh] = lastmm
    p.final_wait([st_busy[0], st_busy[1]])
    p.build(nc)
    a.close()
    return nc


def build_l0attn(t=T, d=D, nsteps=None, debug_stop=None):
    K = 16
    HD = 64
    NHL = 4
    n_chunk = t // K if nsteps is None else nsteps
    tt = n_chunk * K
    ndc = d // 128
    nts = tt // 512
    G = 32
    ngr = n_chunk // G
    nc = bass.Bass("TRN2", target_bir_lowering=False)
    ln1T = nc.dram_tensor("ln1T", [d, t], BF16, kind="ExternalInput").ap()
    wqT = nc.dram_tensor("wqT", [d, 256], F32, kind="ExternalInput").ap()
    wkT = nc.dram_tensor("wkT", [d, 256], F32, kind="ExternalInput").ap()
    wvT = nc.dram_tensor("wvT", [d, 256], F32, kind="ExternalInput").ap()
    lwT4 = nc.dram_tensor("lwT4", [d, 4], F32, kind="ExternalInput").ap()
    lb4 = nc.dram_tensor("lb4", [4, 1], F32, kind="ExternalInput").ap()
    lnw_rep = nc.dram_tensor("lnw_rep", [128, HD], F32, kind="ExternalInput").ap()
    lnb_row = nc.dram_tensor("lnb_row", [1, 256], F32, kind="ExternalInput").ap()
    s0 = nc.dram_tensor("s0", [65, NHL * 64], F32, kind="ExternalInput").ap()
    xcw = nc.dram_tensor("xcw", [tt, 256], BF16, kind="ExternalOutput").ap()
    xbT_s = nc.dram_tensor("xbT_s", [256, tt], BF16, kind="Internal").ap()
    xcT_s = nc.dram_tensor("xcT_s", [256, tt], BF16, kind="Internal").ap()
    sp_s = nc.dram_tensor("sp_s", [tt, 260], F32, kind="Internal").ap()
    xa_s = nc.dram_tensor("xa_s", [tt, 256], F32, kind="Internal").ap()
    xc_s = nc.dram_tensor("xc_s", [tt, 256], F32, kind="Internal").ap()
    cobx_s = nc.dram_tensor("cobx_s", [n_chunk, NHL, K, 65], F32, kind="Internal").ap()
    xa2_s = nc.dram_tensor("xa2_s", [n_chunk, NHL, K, HD], F32, kind="Internal").ap()
    xc2_s = nc.dram_tensor("xc2_s", [n_chunk, NHL, K, HD], F32, kind="Internal").ap()

    p = Prog()
    a = Alloc(nc, p)
    KB = 1024
    SL = min(2048, tt)
    nsl = tt // SL
    LT = [a.sb(f"LT{i}", [128, SL], BF16) for i in range(ndc)]
    wq_sb = a.sb("wq_sb", [128, ndc * 256], BF16)
    wk_sb = a.sb("wk_sb", [128, ndc * 256], BF16)
    wv_sb = a.sb("wv_sb", [128, ndc * 256], BF16)
    lw_sb = a.sb("lw_sb", [128, ndc * 4], BF16)
    lb_sb = a.sb("lb_sb", [4, 1])
    lbr = a.sb("lbr", [1, 4], BF16)
    lnbr = a.sb("lnbr", [1, 256], BF16)
    nones = a.sb("nones", [1, 128], BF16)
    g_t = a.sb("g_t", [128, HD])
    g2_t = a.sb("g2_t", [128, HD])
    evq = [a.sb(f"evq{i}", [128, 512], BF16) for i in range(2)]
    sgq = a.sb("sgq", [128, 512])
    rc_col = a.sb("rc_col", [128, 1])
    onesp = a.sb("onesp", [1, 512], BF16)
    it_col = a.sb("it_col", [128, 1], I32)
    sp_t = [a.sb(f"sp_t{i}", [128, 260]) for i in range(2)]
    xa_t = [a.sb(f"xa_t{i}", [128, 256]) for i in range(2)]
    xc_t = [a.sb(f"xc_t{i}", [128, 256]) for i in range(2)]
    xbts = [a.sb(f"xbts{i}", [65, NHL * G * K], BF16) for i in range(2)]
    xcts = [a.sb(f"xcts{i}", [65, NHL * G * K], BF16) for i in range(2)]
    cbx = [a.sb(f"cbx{i}", [128, G * 65], BF16) for i in range(2)]
    xas = [a.sb(f"xas{i}", [128, G * HD]) for i in range(2)]
    xcs = [a.sb(f"xcs{i}", [128, G * HD]) for i in range(2)]
    outb = [a.sb(f"outb{i}", [128, G * HD], BF16) for i in range(2)]
    S_all = a.sb("S_all", [65, NHL * 65])
    Sb = a.sb("Sb", [65, NHL * 65], BF16)
    maskT = a.sb("maskT", [128, K])
    zcb = a.sb("zcb", [128, HD])
    sqsc = a.sb("sqsc", [128, HD], BF16)
    xhb = a.sb("xhb", [128, HD])
    t5b = a.sb("t5b", [128, HD])
    xagb = a.sb("xagb", [128, HD])
    grb = a.sb("grb", [128, HD])
    t4b = a.sb("t4b", [128, HD], BF16)
    u1b = a.sb("u1b", [128, HD])
    ngb = a.sb("ngb", [128, HD], BF16)
    mco = a.sb("mco", [128, K])
    m2b = a.sb("m2b", [128, K], BF16)
    zc2b = a.sb("zc2b", [128, HD])
    o1b = a.sb("o1b", [128, HD])
    o2b = a.sb("o2b", [128, HD])
    stt = a.sb("stt", [128, 16])
    cobx65 = None
    ps_z = a.ps("ps_z", [128, 512])
    ps_at = a.ps("ps_at", [128, 512])
    ps_zb = a.ps("ps_zb", [128, 512])
    ps_u = a.ps("ps_u", [128, 512])
    ps_p1 = [a.ps(f"ps_p1{i}", [128, 512]) for i in range(2)]
    ps_p2 = [a.ps(f"ps_p2{i}", [128, 512]) for i in range(2)]

    eps_ap, eps_tok = a.const(1e-6)

    ldw = []
    for nm, (sb, dr, w) in {"q": (wq_sb, wqT, 256), "k": (wk_sb, wkT, 256),
                            "v": (wv_sb, wvT, 256), "l": (lw_sb, lwT4, 4)}.items():
        ldw.append(p.op("gpsimd", lambda e, sb=sb, dr=dr, w=w: e.dma_start(
            out=sb[:].rearrange("p (c n) -> p c n", c=ndc),
            in_=dr.rearrange("(c p) n -> p c n", p=128)), dma=True, sem="ldw"))
    ldb = p.op("sync", lambda e: e.dma_start(out=lb_sb[:], in_=lb4[:]),
               dma=True, sem="lds")
    ldbrow = p.op("gpsimd", lambda e: e.dma_start(
        out=lbr[:], in_=lb4.rearrange("h o -> o h")), dma=True, sem="ldbg")
    ldg = p.op("sync", lambda e: e.dma_start(out=g_t[:], in_=lnw_rep[:]),
               dma=True, sem="lds")
    ldbr = p.op("gpsimd", lambda e: e.dma_start(out=lnbr[:], in_=lnb_row[:]),
                dma=True, sem="ldbg")
    lds0 = p.op("sync", lambda e: e.dma_start(
        out=S_all[:].rearrange("p (h n) -> p h n", h=NHL)[:, :, 0:64],
        in_=s0.rearrange("p (h n) -> p h n", h=NHL)), dma=True, sem="lds")

    g2k = p.op("vector", lambda e: e.tensor_mul(g2_t[:], g_t[:], g_t[:]),
               after=[ldg])
    mn1 = p.op("gpsimd", lambda e: e.memset(nones[:], -1.0))
    mp1 = p.op("gpsimd", lambda e: e.memset(onesp[:], 1.0))
    mm0 = p.op("gpsimd", lambda e: e.memset(maskT[:], 1.0))
    mm1 = p.op("gpsimd", lambda e: e.affine_select(
        maskT[:], maskT[:], [[1, K]], ALU.is_ge, 0.0,
        base=0, channel_multiplier=-1), after=[mm0])
    it0 = p.op("gpsimd", lambda e: e.iota(it_col[:], [[0, 1]], base=0,
                                          channel_multiplier=1))
    rc1 = p.op("vector", lambda e: e.tensor_scalar(
        it_col[:], it_col[:], 15, None, ALU.bitwise_and), after=[it0])
    rc2 = p.op("vector", lambda e: e.tensor_copy(rc_col[:], it_col[:]), after=[rc1])
    rc3 = p.op("vector", lambda e: e.tensor_scalar(
        rc_col[:], rc_col[:], 1.0, None, ALU.add), after=[rc2])
    rc4 = p.op("vector", lambda e: e.reciprocal(rc_col[:], rc_col[:]), after=[rc3])
    rc5 = p.op("vector", lambda e: e.tensor_scalar(
        rc_col[:], rc_col[:], 1.0 / 64.0, None, ALU.mult), after=[rc4])
    mk1 = p.op("gpsimd", lambda e: e.iota(maskT[:], [[1, K]], base=0,
               channel_multiplier=0, allow_small_or_imprecise_dtypes=True),
               after=[mm1])
    mk2 = p.op("vector", lambda e: e.tensor_copy(zcb[:, 0:1], it_col[:]),
               after=[rc1, mk1])
    mk3 = p.op("vector", lambda e: e.tensor_scalar(
        maskT[:], maskT[:], zcb[:, 0:1], None, ALU.subtract), after=[mk2])
    mk4 = p.op("vector", lambda e: e.tensor_scalar(
        maskT[:], maskT[:], -0.5, None, ALU.is_gt), after=[mk3])
    mask_tok = mk4

    sr_toks = []
    for hl in range(NHL):
        sr_toks.append(p.op("vector", lambda e, hl=hl: e.reduce_sum(
            S_all[:, 65 * hl + 64:65 * hl + 65],
            S_all[:].rearrange("p (h n) -> p h n", h=NHL)[:, hl, 0:64],
            axis=AX.X), after=[lds0] + sr_toks[-1:]))
    sb0 = p.op("vector", lambda e: e.tensor_copy(Sb[:], S_all[:]),
               after=sr_toks)

    p1_st = []
    prev_p1st = [None, None]
    bank_busy = [None, None, None, None]
    ltp = [None] * ndc
    slice_consumers = []
    for sl in range(nsl):
      t0 = sl * SL
      new_ltp = []
      for c in range(ndc):
        new_ltp.append(p.op("sync", lambda e, c=c, t0=t0: e.dma_start(
            out=LT[c][:], in_=ln1T[128 * c:128 * (c + 1), t0:t0 + SL]),
            after=slice_consumers + [ltp[c]], dma=True, sem=f"ldlt{c % 2}"))
      ltp = new_ltp
      slice_consumers = []
      for half in range(2):
        for tsl_ in range(SL // 512):
            tsl = sl * (SL // 512) + tsl_
            for wi, (wsb, outdr) in enumerate(((wk_sb, xbT_s), (wq_sb, xcT_s))):
                bk = ps_p1[wi]
                mms = []
                for c in range(ndc):
                    mms.append(p.op("tensor", lambda e, wsb=wsb, c=c, half=half,
                               tsl_=tsl_, bk=bk: e.matmul(
                        bk[:], wsb[:, 256 * c + 128 * half:256 * c + 128 * (half + 1)],
                        LT[c][:, 512 * tsl_:512 * (tsl_ + 1)],
                        start=(c == 0), stop=(c == ndc - 1)),
                        after=[ldw[wi], ltp[c]] + ([bank_busy[wi]] if c == 0 else [])))
                slice_consumers.append(mms[-1])
                ev = p.op("vector", lambda e, wi=wi, bk=bk: e.tensor_copy(
                    evq[wi][:], bk[:]), after=[mms[-1], prev_p1st[wi]])
                bank_busy[wi] = ev
                st = p.op("scalar", lambda e, wi=wi, half=half, tsl=tsl,
                          outdr=outdr: e.dma_start(
                    out=outdr[128 * half:128 * (half + 1), 512 * tsl:512 * (tsl + 1)],
                    in_=evq[wi][:]), after=[ev], dma=True, sem=f"stp1{wi}")
                p1_st.append(st)
                prev_p1st[wi] = st

      if sl == 0:
        p2_st = []
        prev_sp = [None, None]
        prev_xa = [None, None]
        prev_xc = [None, None]
      for i_ in range(SL // 128):
        i = sl * (SL // 128) + i_
        b = i % 2
        bkb, bka = ps_p2[0], ps_p2[1]
        bkc, bki = ps_p1[0], ps_p1[1]
        mmsb, mmsa, mmsc, mmsi = [], [], [], []
        for c in range(ndc):
            lhs = LT[c][:, 128 * i_:128 * (i_ + 1)]
            mmsb.append(p.op("tensor", lambda e, c=c, i=i, lhs=lhs: e.matmul(
                bkb[:, 0:256], lhs, wk_sb[:, 256 * c:256 * (c + 1)],
                start=(c == 0), stop=(c == ndc - 1)),
                after=[ldw[1], ltp[c]] + ([prev_sp[b], bank_busy[2]] if c == 0 else [])))
            mmsa.append(p.op("tensor", lambda e, c=c, i=i, lhs=lhs: e.matmul(
                bka[:, 0:256], lhs, wv_sb[:, 256 * c:256 * (c + 1)],
                start=(c == 0), stop=False),
                after=[ldw[2], ltp[c]] + ([prev_xa[b], bank_busy[3]] if c == 0 else [])))
            mmsc.append(p.op("tensor", lambda e, c=c, i=i, lhs=lhs: e.matmul(
                bkc[:, 0:256], lhs, wq_sb[:, 256 * c:256 * (c + 1)],
                start=(c == 0), stop=False),
                after=[ldw[0], ltp[c]] + ([prev_xc[b], bank_busy[0]] if c == 0 else [])))
            mmsi.append(p.op("tensor", lambda e, c=c, i=i, lhs=lhs: e.matmul(
                bki[:, 0:4], lhs, lw_sb[:, 4 * c:4 * (c + 1)],
                start=(c == 0), stop=False),
                after=[ldw[3], ltp[c]] + ([bank_busy[1]] if c == 0 else [])))
        mma2 = p.op("tensor", lambda e: e.matmul(
            bka[:, 0:256], nones[:], lnbr[:], start=False, stop=True),
            after=[mn1, ldbr])
        mmc2 = p.op("tensor", lambda e: e.matmul(
            bkc[:, 0:256], onesp[0:1, 0:128], lnbr[:],
            start=False, stop=True), after=[mp1, ldbr])
        mmib = p.op("tensor", lambda e, bki=bki: e.matmul(
            bki[:, 0:4], onesp[0:1, 0:128], lbr[:], start=False, stop=True),
            after=[mp1, ldbrow, mmsi[-1]])
        slice_consumers.append(mmib)
        evi = p.op("scalar", lambda e, bki=bki: e.activation(
            sgq[:, 0:4], bki[:, 0:4], AF.Sigmoid),
            after=[mmib, bank_busy[1]])
        coi = p.op("vector", lambda e: e.tensor_scalar(
            sgq[:, 4:8], sgq[:, 0:4], rc_col[:, 0:1], None, ALU.mult),
            after=[evi, rc5])
        bank_busy[1] = coi
        evb = p.op("vector", lambda e, b=b, bkb=bkb: e.tensor_copy(
            sp_t[b][:].rearrange("p (h n) -> p h n", h=NHL)[:, :, 0:64],
            bkb[:, 0:256].rearrange("p (h n) -> p h n", h=NHL)),
            after=[mmsb[-1], prev_sp[b]])
        cts = []
        for hl in range(NHL):
            c1_ = p.op("vector", lambda e, b=b, hl=hl: e.tensor_scalar(
                sp_t[b][:, 65 * hl:65 * hl + 64], sp_t[b][:, 65 * hl:65 * hl + 64],
                sgq[:, 4 + hl:5 + hl], None, ALU.mult), after=[evb, coi])
            c2_ = p.op("vector", lambda e, b=b, hl=hl: e.tensor_copy(
                sp_t[b][:, 65 * hl + 64:65 * hl + 65], sgq[:, 4 + hl:5 + hl]),
                after=[evb, coi])
            cts += [c1_, c2_]
        stsp = p.op("scalar", lambda e, i=i, b=b: e.dma_start(
            out=sp_s[128 * i:128 * (i + 1), :], in_=sp_t[b][:]),
            after=cts, dma=True, sem=f"stsp{b}")
        prev_sp[b] = stsp
        eva = p.op("vector", lambda e, b=b, bka=bka: e.tensor_copy(
            xa_t[b][:], bka[:, 0:256]), after=[mma2, prev_xa[b]])
        eva2 = p.op("vector", lambda e, b=b, bkb=bkb: e.tensor_sub(
            xa_t[b][:], xa_t[b][:], bkb[:, 0:256]), after=[eva, mmsb[-1]])
        bank_busy[2] = eva2
        bank_busy[3] = eva2
        sta = p.op("scalar", lambda e, i=i, b=b: e.dma_start(
            out=xa_s[128 * i:128 * (i + 1), :], in_=xa_t[b][:]),
            after=[eva2], dma=True, sem=f"sta{b}")
        prev_xa[b] = sta
        evc = p.op("vector", lambda e, b=b, bkc=bkc: e.tensor_copy(
            xc_t[b][:], bkc[:, 0:256]), after=[mmc2, prev_xc[b]])
        bank_busy[0] = evc
        stc = p.op("scalar", lambda e, i=i, b=b: e.dma_start(
            out=xc_s[128 * i:128 * (i + 1), :], in_=xc_t[b][:]),
            after=[evc], dma=True, sem=f"stc{b}")
        prev_xc[b] = stc
        p2_st += [stsp, sta, stc]

    re_toks = []
    for k in range(K):
        re_toks.append(p.op("gpsimd", lambda e, k=k: e.dma_start(
            out=cobx_s[:, :, k, :],
            in_=sp_s.rearrange("(n k) (h g) -> n k h g", k=K, h=NHL)[:, k, :, :]),
            after=p2_st, dma=True, sem=f"re{k % 4}"))
        re_toks.append(p.op("gpsimd", lambda e, k=k: e.dma_start(
            out=xa2_s[:, :, k, :],
            in_=xa_s.rearrange("(n k) (h g) -> n k h g", k=K, h=NHL)[:, k, :, :]),
            after=p2_st, dma=True, sem=f"re{k % 4}"))
        re_toks.append(p.op("gpsimd", lambda e, k=k: e.dma_start(
            out=xc2_s[:, :, k, :],
            in_=xc_s.rearrange("(n k) (h g) -> n k h g", k=K, h=NHL)[:, k, :, :]),
            after=p2_st, dma=True, sem=f"re{k % 4}"))

    if debug_stop == "p2":
        p.final_wait(p2_st)
        p.build(nc)
        a.close()
        return nc
    if debug_stop == "re":
        p.final_wait(re_toks)
        p.build(nc)
        a.close()
        return nc
    init_toks = []
    for buf in xbts + xcts:
        init_toks.append(p.op("gpsimd", lambda e, buf=buf: e.memset(buf[:], 1.0)))
    for buf in cbx + xas + xcs:
        init_toks.append(p.op("gpsimd", lambda e, buf=buf: e.memset(buf[:], 0.0)))

    GK = G * K
    wr = {}
    from collections import defaultdict as _dd
    rd = _dd(list)

    def OP(engine, fn, reads=(), writes=(), extra=(), dma=False, sem=None):
        after = list(extra)
        for t_ in writes:
            after += rd[t_] + [wr.get(t_)]
        for t_ in reads:
            after.append(wr.get(t_))
        tok = p.op(engine, fn, after=after, dma=dma, sem=sem)
        for t_ in writes:
            rd[t_] = []
            wr[t_] = tok
        for t_ in reads:
            rd[t_].append(tok)
        return tok

    for tok, names in ((sb0, ["Sb", "S_all"]), (mask_tok, ["maskT", "zcb"]),
                       (g2k, ["g2", "g"]), (rc5, ["rc"])):
        for nmx in names:
            wr[nmx] = tok
    wr["stream"] = init_toks[-1]
    for bank, width, names in ((ps_z, 65, ["psz0", "psz1", "psz2", "psz3"]),
                               (ps_at, K, ["psat0", "psat1", "psat2", "psat3"]),
                               (ps_zb, 65, ["pszb0", "pszb1", "pszb2", "pszb3"]),
                               (ps_u, NHL * HD, ["psu0", "psu1", "psu2", "psu3"])):
        wtok = p.op("tensor", lambda e, bank=bank, width=width: e.matmul(
            bank[:, 0:width], onesp[0:1, 0:128], onesp[0:1, 0:width],
            start=True, stop=True), after=[mp1])
        for nmx in names:
            wr[nmx] = wtok

    out_stores = []
    for gr in range(ngr):
        if debug_stop == f"gr{gr}":
            p.final_wait([v for v in wr.values() if v is not None] + out_stores)
            p.build(nc); a.close(); return nc
        pb = gr % 2
        n0 = gr * G
        ld_toks = []
        for hl in range(NHL):
            ld_toks.append(OP("sync", lambda e, hl=hl, pb=pb, n0=n0: e.dma_start(
                out=xbts[pb][0:64, GK * hl:GK * (hl + 1)],
                in_=xbT_s[64 * hl:64 * (hl + 1), K * n0:K * (n0 + G)]),
                reads=[], writes=[f"xbts{pb}_{hl}"], dma=True, sem=f"lsb{pb}",
                extra=[wr["stream"]] + p1_st))
            ld_toks.append(OP("sync", lambda e, hl=hl, pb=pb, n0=n0: e.dma_start(
                out=xcts[pb][0:64, GK * hl:GK * (hl + 1)],
                in_=xcT_s[64 * hl:64 * (hl + 1), K * n0:K * (n0 + G)]),
                reads=[], writes=[f"xcts{pb}_{hl}"], dma=True, sem=f"lsc{pb}",
                extra=[wr["stream"]] + p1_st))
            ld_toks.append(OP("gpsimd", lambda e, hl=hl, pb=pb, n0=n0: e.dma_start(
                out=cbx[pb][32 * hl:32 * hl + 16, :].rearrange(
                    "p (n g) -> p n g", g=65),
                in_=cobx_s[n0:n0 + G, hl].rearrange("n k g -> k n g")),
                reads=[], writes=[f"cbx{pb}_{hl}"], dma=True, sem=f"lcb{pb}",
                extra=[wr["stream"]] + re_toks))
            ld_toks.append(OP("sync", lambda e, hl=hl, pb=pb, n0=n0: e.dma_start(
                out=xas[pb][32 * hl:32 * hl + 16, :].rearrange(
                    "p (n g) -> p n g", g=HD),
                in_=xa2_s[n0:n0 + G, hl].rearrange("n k g -> k n g")),
                reads=[], writes=[f"xas{pb}_{hl}"], dma=True, sem=f"lxa{pb}",
                extra=[wr["stream"]] + re_toks))
            ld_toks.append(OP("sync", lambda e, hl=hl, pb=pb, n0=n0: e.dma_start(
                out=xcs[pb][32 * hl:32 * hl + 16, :].rearrange(
                    "p (n g) -> p n g", g=HD),
                in_=xc2_s[n0:n0 + G, hl].rearrange("n k g -> k n g")),
                reads=[], writes=[f"xcs{pb}_{hl}"], dma=True, sem=f"lxc{pb}",
                extra=[wr["stream"]] + re_toks))
        if debug_stop == "ld0" and gr == 0:
            p.final_wait(ld_toks)
            p.build(nc)
            a.close()
            return nc
        snames = ([f"xbts{pb}_{hl}" for hl in range(NHL)] +
                  [f"xcts{pb}_{hl}" for hl in range(NHL)] +
                  [f"cbx{pb}_{hl}" for hl in range(NHL)] +
                  [f"xas{pb}_{hl}" for hl in range(NHL)] +
                  [f"xcs{pb}_{hl}" for hl in range(NHL)])
        for gs in range(G):
            for hl in range(NHL):
                OP("tensor", lambda e, hl=hl, pb=pb, gs=gs: e.matmul(
                    ps_z[32 * hl:32 * hl + 16, 0:65],
                    xbts[pb][:, GK * hl + K * gs:GK * hl + K * (gs + 1)],
                    Sb[:, 65 * hl:65 * (hl + 1)],
                    start=True, stop=True, tile_position=(0, 32 * hl)),
                    reads=[f"xbts{pb}_{hl}", "Sb"], writes=[f"psz{hl}"])
            if debug_stop == "z1":
                p.final_wait([v for v in wr.values() if v is not None])
                p.build(nc); a.close(); return nc
            OP("scalar", lambda e: e.activation(
                stt[:, 0:1], ps_z[:, 64:65], AF.Identity, scale=-1.0 / 64),
                reads=["psz0", "psz1", "psz2", "psz3"], writes=["nmu"])
            OP("scalar", lambda e: e.activation(
                zcb[:], ps_z[:, 0:64], AF.Identity, bias=stt[:, 0:1]),
                reads=["nmu", "psz0", "psz1", "psz2", "psz3"], writes=["zcb"])
            OP("scalar", lambda e: e.activation(
                sqsc[:], zcb[:], AF.Square, accum_out=stt[:, 1:2]),
                reads=["zcb"], writes=["sqs", "sqsc"])
            OP("scalar", lambda e: e.activation(
                stt[:, 2:3], stt[:, 1:2], AF.Sqrt, scale=1.0 / 64, bias=eps_ap),
                reads=["sqs"], writes=["std"], extra=[eps_tok])
            OP("vector", lambda e: e.reciprocal(stt[:, 3:4], stt[:, 2:3]),
                reads=["std"], writes=["rstd"])
            OP("vector", lambda e: e.tensor_scalar(
                xhb[:], zcb[:], stt[:, 3:4], None, ALU.mult),
                reads=["rstd", "zcb"], writes=["xhb"])
            OP("vector", lambda e: e.tensor_mul(t5b[:], xhb[:], g2_t[:]),
                reads=["xhb", "g2"], writes=["t5b"])
            OP("vector", lambda e, pb=pb, gs=gs: e.tensor_mul(
                xagb[:], xas[pb][:, HD * gs:HD * (gs + 1)], g_t[:]),
                reads=[f"xas{pb}_0", f"xas{pb}_1", f"xas{pb}_2", f"xas{pb}_3", "g"],
                writes=["xagb"])
            OP("vector", lambda e: e.scalar_tensor_tensor(
                grb[:], t5b[:], 1.0, xagb[:], ALU.mult, ALU.subtract,
                accum_out=stt[:, 4:5]),
                reads=["t5b", "xagb"], writes=["grb", "s1"])
            OP("vector", lambda e: e.scalar_tensor_tensor(
                t4b[:], grb[:], 1.0, xhb[:], ALU.mult, ALU.mult,
                accum_out=stt[:, 5:6]),
                reads=["grb", "xhb"], writes=["t4b", "s2"])
            OP("vector", lambda e: e.tensor_scalar(
                stt[:, 6:7], stt[:, 4:5], stt[:, 3:4], 1.0 / 64, ALU.mult, ALU.mult),
                reads=["s1", "rstd"], writes=["c1"])
            OP("vector", lambda e: e.tensor_scalar(
                stt[:, 7:8], stt[:, 5:6], stt[:, 3:4], 1.0 / 64, ALU.mult, ALU.mult),
                reads=["s2", "rstd"], writes=["c2"])
            OP("vector", lambda e: e.tensor_scalar(
                u1b[:], grb[:], stt[:, 3:4], stt[:, 6:7], ALU.mult, ALU.subtract),
                reads=["grb", "rstd", "c1"], writes=["u1b"])
            OP("vector", lambda e: e.scalar_tensor_tensor(
                ngb[:], xhb[:], stt[:, 7:8], u1b[:], ALU.mult, ALU.subtract),
                reads=["xhb", "c2", "u1b"], writes=["ngb"])
            if debug_stop == "ng":
                p.final_wait([v for v in wr.values() if v is not None])
                p.build(nc); a.close(); return nc
            for hl in range(NHL):
                OP("tensor", lambda e, hl=hl, pb=pb, gs=gs: e.matmul(
                    ps_at[32 * hl:32 * hl + 16, 0:K],
                    xbts[pb][0:64, GK * hl + K * gs:GK * hl + K * (gs + 1)],
                    xcts[pb][0:64, GK * hl + K * gs:GK * hl + K * (gs + 1)],
                    start=True, stop=True, tile_position=(0, 32 * hl)),
                    reads=[f"xbts{pb}_{hl}", f"xcts{pb}_{hl}"],
                    writes=[f"psat{hl}"])
            OP("vector", lambda e, pb=pb, gs=gs: e.tensor_copy(
                stt[:, 12:13], cbx[pb][:, 65 * gs + 64:65 * gs + 65]),
                reads=[f"cbx{pb}_0", f"cbx{pb}_1", f"cbx{pb}_2", f"cbx{pb}_3"],
                writes=["cof"])
            OP("vector", lambda e: e.tensor_scalar(
                mco[:], maskT[:], stt[:, 12:13], None, ALU.mult),
                reads=["maskT", "cof"], writes=["mco"])
            OP("vector", lambda e: e.scalar_tensor_tensor(
                m2b[:], ps_at[:, 0:K], 1.0, mco[:], ALU.add, ALU.mult),
                reads=["mco", "psat0", "psat1", "psat2", "psat3"], writes=["m2b"])
            if debug_stop == "m2":
                p.final_wait([v for v in wr.values() if v is not None])
                p.build(nc); a.close(); return nc
            for hl in range(NHL):
                OP("tensor", lambda e, hl=hl, pb=pb, gs=gs: e.matmul(
                    ps_zb[32 * hl:32 * hl + 16, 0:65],
                    xcts[pb][:, GK * hl + K * gs:GK * hl + K * (gs + 1)],
                    Sb[:, 65 * hl:65 * (hl + 1)],
                    start=True, stop=False, tile_position=(0, 32 * hl)),
                    reads=[f"xcts{pb}_{hl}", "Sb"], writes=[f"pszb{hl}"])
                OP("tensor", lambda e, hl=hl: e.matmul(
                    ps_zb[32 * hl:32 * hl + 16, 0:64],
                    m2b[32 * hl:32 * hl + 16, :],
                    ngb[32 * hl:32 * hl + 16, :],
                    start=False, stop=True, tile_position=(32 * hl, 32 * hl)),
                    reads=["m2b", "ngb"], writes=[f"pszb{hl}"])
            if debug_stop == "zb":
                p.final_wait([v for v in wr.values() if v is not None])
                p.build(nc); a.close(); return nc
            for hl in range(NHL):
                OP("tensor", lambda e, hl=hl, pb=pb, gs=gs: e.matmul(
                    ps_u[0:65, HD * hl:HD * (hl + 1)],
                    cbx[pb][32 * hl:32 * hl + 16, 65 * gs:65 * (gs + 1)],
                    ngb[32 * hl:32 * hl + 16, :],
                    start=True, stop=True, tile_position=(32 * hl, 0)),
                    reads=[f"cbx{pb}_{hl}", "ngb"] +
                          ([f"psu{hl - 1}"] if hl > 0 else []),
                    writes=[f"psu{hl}"])
            if debug_stop == "u":
                p.final_wait([v for v in wr.values() if v is not None])
                p.build(nc); a.close(); return nc
            OP("vector", lambda e: e.tensor_tensor(
                S_all[:].rearrange("p (h n) -> p h n", h=NHL)[:, :, 0:64],
                S_all[:].rearrange("p (h n) -> p h n", h=NHL)[:, :, 0:64],
                ps_u[0:65, 0:NHL * HD].rearrange("p (h n) -> p h n", h=NHL),
                ALU.add),
                reads=["psu0", "psu1", "psu2", "psu3"], writes=["S_all"])
            OP("vector", lambda e: e.reduce_sum(
                S_all[:].rearrange("p (h n) -> p h n", h=NHL)[:, :, 64:65],
                S_all[:].rearrange("p (h n) -> p h n", h=NHL)[:, :, 0:64],
                axis=AX.X), reads=["S_all"], writes=["S_all"])
            OP("vector", lambda e: e.tensor_copy(Sb[:], S_all[:]),
               reads=["S_all"], writes=["Sb"])
            if debug_stop == "v26":
                p.final_wait([v for v in wr.values() if v is not None])
                p.build(nc); a.close(); return nc
            OP("scalar", lambda e: e.activation(
                stt[:, 8:9], ps_zb[:, 64:65], AF.Identity, scale=-1.0 / 64),
                reads=["pszb0", "pszb1", "pszb2", "pszb3"], writes=["nmu2"])
            OP("scalar", lambda e: e.activation(
                zc2b[:], ps_zb[:, 0:64], AF.Identity, bias=stt[:, 8:9]),
                reads=["nmu2", "pszb0", "pszb1", "pszb2", "pszb3"],
                writes=["zc2b"])
            OP("scalar", lambda e: e.activation(
                sqsc[:], zc2b[:], AF.Square, accum_out=stt[:, 9:10]),
                reads=["zc2b"], writes=["sqs2", "sqsc"])
            OP("scalar", lambda e: e.activation(
                stt[:, 10:11], stt[:, 9:10], AF.Sqrt, scale=1.0 / 64, bias=eps_ap),
                reads=["sqs2"], writes=["std2"], extra=[eps_tok])
            OP("vector", lambda e: e.reciprocal(stt[:, 11:12], stt[:, 10:11]),
                reads=["std2"], writes=["rstd2"])
            OP("vector", lambda e: e.tensor_scalar(
                o1b[:], zc2b[:], stt[:, 11:12], None, ALU.mult),
                reads=["rstd2", "zc2b"], writes=["o1b"])
            OP("vector", lambda e: e.tensor_mul(o2b[:], o1b[:], g_t[:]),
                reads=["o1b", "g"], writes=["o2b"])
            OP("vector", lambda e, pb=pb, gs=gs: e.tensor_add(
                outb[pb][:, HD * gs:HD * (gs + 1)], o2b[:],
                xcs[pb][:, HD * gs:HD * (gs + 1)]),
                reads=["o2b"] + [f"xcs{pb}_{hl}" for hl in range(NHL)],
                writes=[f"outb{pb}"])
            if debug_stop and debug_stop.startswith("s") and \
                    gr * G + gs + 1 >= int(debug_stop[1:]):
                p.final_wait([v for v in wr.values() if v is not None])
                p.build(nc)
                a.close()
                return nc
        for hl in range(NHL):
            st = OP("scalar", lambda e, hl=hl, pb=pb, n0=n0: e.dma_start(
                out=xcw.rearrange("(n k) (h g) -> k n g h", k=K, h=NHL)
                    [:, n0:n0 + G, :, hl],
                in_=outb[pb][32 * hl:32 * hl + 16, :].rearrange(
                    "p (n g) -> p n g", g=HD)),
                reads=[f"outb{pb}"], writes=[], dma=True, sem=f"stout{pb}")
            out_stores.append(st)
        wr[f"outb{pb}"] = out_stores[-1]

    p.final_wait(out_stores)
    p.build(nc)
    a.close()
    return nc


def build_reduce(tc=TC, d=D, nparts=NCORES):
    nt = tc // 128
    nc = bass.Bass("TRN2", target_bir_lowering=False)
    x3 = nc.dram_tensor("x3", [tc, d], F32, kind="ExternalInput").ap()
    pp = nc.dram_tensor("pp", [nparts, tc, d], F32, kind="ExternalInput").ap()
    y = nc.dram_tensor("y", [tc, d], F32, kind="ExternalOutput").ap()
    p = Prog()
    a = Alloc(nc, p)
    acc = [a.sb(f"acc{i}", [128, d]) for i in range(2)]
    pt = [a.sb(f"pt{i}", [128, d]) for i in range(3)]
    prev_acc = [None, None]
    prev_pt = [None, None, None]
    stores = []
    for i in range(nt):
        b = i % 2
        lda = p.op("sync", lambda e, i=i, b=b: e.dma_start(
            out=acc[b][:], in_=x3[128 * i:128 * (i + 1), :]),
            after=[prev_acc[b]], dma=True, sem=f"la{b}")
        tok = lda
        for j in range(nparts):
            pb = (i * nparts + j) % 3
            ldp = p.op("gpsimd" if j % 2 else "sync",
                       lambda e, i=i, j=j, pb=pb: e.dma_start(
                out=pt[pb][:], in_=pp[j, 128 * i:128 * (i + 1), :]),
                after=[prev_pt[pb]], dma=True, sem=f"lp{pb}")
            tok = p.op("vector", lambda e, b=b, pb=pb: e.tensor_add(
                acc[b][:], acc[b][:], pt[pb][:]), after=[ldp, tok])
            prev_pt[pb] = tok
        st = p.op("scalar", lambda e, i=i, b=b: e.dma_start(
            out=y[128 * i:128 * (i + 1), :], in_=acc[b][:]),
            after=[tok], dma=True, sem=f"st{b}")
        prev_acc[b] = st
        stores.append(st)
    p.final_wait(stores)
    p.build(nc)
    a.close()
    return nc


def pi_perm(n):
    perm = np.empty(n, dtype=np.int64)
    ar = 2 * np.arange(128)
    for k in range(n // 256):
        perm[256 * k:256 * k + 128] = 256 * k + ar
        perm[256 * k + 128:256 * k + 256] = 256 * k + ar + 1
    return perm


def prep_l1ffn_core(c, ncores, gate_pk, gate_sc, up_pk, up_sc, down_pk,
                    down_sc, d, inter):
    ic = inter // ncores
    i_lo = c * ic
    pil = pi_perm(ic)
    ws_d = np.arange(d // 2) // 16
    ws_i = (i_lo // 2 + np.arange(ic // 2)) // 16
    o = {}
    o["gpk"] = np.ascontiguousarray(gate_pk[i_lo:i_lo + ic].T[:, pil]).astype(np.int32)
    o["upk"] = np.ascontiguousarray(up_pk[i_lo:i_lo + ic].T[:, pil]).astype(np.int32)
    o["gsc"] = np.ascontiguousarray(gate_sc[i_lo + pil][:, ws_d].T).astype(np.float32)
    o["usc"] = np.ascontiguousarray(up_sc[i_lo + pil][:, ws_d].T).astype(np.float32)
    o["dpk"] = np.ascontiguousarray(down_pk.T[i_lo // 2:(i_lo + ic) // 2, :]).astype(np.int32)
    o["dsc"] = np.ascontiguousarray(down_sc[:, ws_i].T).astype(np.float32)
    return o


F32R = mybir.dt.float32r


def build_l1attn(mode, tc=TC, d=D, nh=32, NZ=64):
    hd = 64
    dn = 64
    CH = 256
    nzl = tc // CH
    ndc = d // 128
    nts = tc // 512
    ntk = tc // 128
    full = (mode == "main")
    nc = bass.Bass("TRN2", target_bir_lowering=False)
    xn3 = nc.dram_tensor("xn3", [tc, d], BF16, kind="ExternalInput").ap()
    bwT = nc.dram_tensor("bwT", [d, dn], F32, kind="ExternalInput").ap()
    cwT = nc.dram_tensor("cwT", [d, dn], F32, kind="ExternalInput").ap()
    dtwT = nc.dram_tensor("dtwT", [d, nh], F32, kind="ExternalInput").ap()
    bb = nc.dram_tensor("bb", [dn, 1], F32, kind="ExternalInput").ap()
    cb = nc.dram_tensor("cb", [dn, 1], F32, kind="ExternalInput").ap()
    dtb = nc.dram_tensor("dtb", [nh, 1], F32, kind="ExternalInput").ap()
    alog = nc.dram_tensor("alog", [nh, 1], F32, kind="ExternalInput").ap()
    if full:
        x2 = nc.dram_tensor("x2", [tc, d], F32, kind="ExternalInput").ap()
        drow = nc.dram_tensor("drow", [1, nh], F32, kind="ExternalInput").ap()
        prevc = nc.dram_tensor("prevc", [nh, nzl, dn * hd], BF16,
                               kind="ExternalInput").ap()
        owT = nc.dram_tensor("owT", [d, d], F32, kind="ExternalInput").ap()
        ob = nc.dram_tensor("ob", [1, d], F32, kind="ExternalInput").ap()
        x3o = nc.dram_tensor("x3", [tc, d], F32, kind="ExternalOutput").ap()
        xn4o = nc.dram_tensor("xn4", [tc, d], BF16, kind="ExternalOutput").ap()
        yts = nc.dram_tensor("ytok_s", [tc, d], BF16, kind="Internal").ap()
    else:
        stex = nc.dram_tensor("stex", [nzl, nh * dn * hd], F32,
                              kind="ExternalOutput").ap()
        dasum = nc.dram_tensor("dasum", [nh, nzl], F32, kind="ExternalOutput").ap()

    p = Prog()
    a = Alloc(nc, p)
    KB = 1024
    ov = Overlay(nc, a, "arena", 174 * KB)
    XT = [ov.at(f"XT{i}", i * tc * 2, [128, tc], BF16) for i in range(ndc)]
    xdr = [ov.at(f"xdr{i}", i * d * 2, [128, d], BF16) for i in range(4)]
    corow = ov.at("corow", 16 * KB, [128, 8 * CH], F32R)
    firow = ov.at("firow", 24 * KB, [128, 8 * CH], F32R)
    prevt = [ov.at(f"prevt{i}", 32 * KB + i * 4 * KB, [dn, nh * hd], BF16)
             for i in range(2)]
    yt = [ov.at(f"yt{i}", 40 * KB + i * d * 2, [128, d], BF16) for i in range(2)]
    scb = [ov.at(f"scb{i}", 48 * KB + i * CH * 2, [128, CH], BF16) for i in range(2)]
    expl = ov.at("expl", 49 * KB, [128, CH], F32)
    exp3 = ov.at("exp3", 50 * KB, [dn, CH], F32)
    mst = [ov.at(f"mst{i}", 51 * KB + i * CH * 2, [128, CH], BF16) for i in range(2)]
    mst3 = ov.at("mst3", 52 * KB, [dn, CH], BF16)
    mneg = [ov.at(f"mneg{i}", 53 * KB + i * CH * 4, [128, CH], F32R) for i in range(2)]
    YT = [ov.at(f"YT{i}", i * tc * 2, [128, tc], BF16) for i in range(ndc)]
    X16 = [ov.at(f"X16{i}", 64 * KB + i * d * 2, [128, d], BF16) for i in range(ntk)]
    ow = [ov.at(f"ow{i}", 64 * KB + i * d * 2, [128, d], BF16) for i in range(ndc)]
    BT = ov.at("BT", 128 * KB, [dn, tc], F32)
    CT = ov.at("CT", 128 * KB + tc * 4, [dn, tc], F32)
    BTb = ov.at("BTb", 128 * KB + tc * 8, [dn, tc], BF16)
    CTb = ov.at("CTb", 128 * KB + tc * 10, [dn, tc], BF16)
    dt_sb = ov.at("dt_sb", 128 * KB + tc * 12, [nh, tc], F32)
    x2t = [ov.at(f"x2t{i}", 128 * KB + i * d * 4, [128, d], F32) for i in range(2)]
    x3t = [ov.at(f"x3t{i}", 128 * KB + 16 * KB + i * d * 4, [128, d], F32)
           for i in range(2)]
    dt_bf = ov.at("dt_bf", 160 * KB, [nh, tc], BF16)
    spr = ov.at("spr", 160 * KB + tc * 2, [nh, 512], F32)
    spab = ov.at("spab", 160 * KB + tc * 2 + 2 * KB, [nh, 512], F32)
    xnt = [ov.at(f"xnt{i}", 160 * KB + i * d * 2, [128, d], BF16) for i in range(2)]
    bw_sb = ov.at("bw_sb", 168 * KB, [128, ndc * dn], BF16)
    cw_sb = ov.at("cw_sb", 168 * KB + ndc * dn * 2, [128, ndc * dn], BF16)
    dtw_sb = ov.at("dtw_sb", 168 * KB + ndc * dn * 4, [128, ndc * nh], BF16)
    stats = ov.at("stats", 168 * KB, [128, ntk * 6], F32)
    sqscr = ov.at("sqscr", 168 * KB + ntk * 6 * 4, [128, d], BF16)
    bvec = a.sb("bvec", [dn, 2])
    dtb_sb = a.sb("dtb_sb", [nh, 1])
    alog_sb = a.sb("alog_sb", [nh, 1])
    na = a.sb("na", [nh, 1])
    ident = a.sb("ident", [128, 128])
    id_bf = a.sb("id_bf", [128, 128], BF16)
    id_r = a.sb("id_r", [128, 128], F32R)
    ones_r = a.sb("ones_r", [128, 128], F32R)
    ones_bf = a.sb("ones_bf", [1, 128], BF16)
    dttok = [a.sb(f"dttok{i}", [128, nh]) for i in range(ntk)]
    cs_c = a.sb("cs_c", [nh, CH])
    zro = a.sb("zro", [nh, CH])
    cofi = a.sb("cofi", [nh, 2 * CH])
    cofi_r = a.sb("cofi_r", [nh, 2 * CH], F32R)
    ctsb = a.sb("ctsb", [128, 4 * nh])
    negct = a.sb("negct", [128, 2 * nh])
    dasb = a.sb("dasb", [nh, max(nzl, 2)])
    modi = a.sb("modi", [nh, CH], I32)
    dbc = a.sb("dbc", [128, nh])
    dbr = a.sb("dbr", [1, nh], F32R)
    obrow = a.sb("obrow", [1, d], BF16)
    if not full:
        Btok = [a.sb(f"Btok{i}", [128, dn], BF16) for i in range(ntk)]
        dcy = a.sb("dcy", [nh, CH], BF16)
        dcyT = a.sb("dcyT", [128, 2 * nh])
        xw = [a.sb(f"xw{i}", [128, dn], BF16) for i in range(2)]
        stx = a.sb("stx", [dn, nh * hd])
    ps = [a.ps(f"ps{i}", [128, 512]) for i in range(6)]
    pst = a.ps("pst", [128, 512])

    ld0 = p.op("gpsimd", lambda e: e.dma_start(
        out=bw_sb[:].rearrange("p (c n) -> p c n", c=ndc),
        in_=bwT.rearrange("(c p) n -> p c n", p=128)), dma=True, sem="ldw0")
    ld1 = p.op("gpsimd", lambda e: e.dma_start(
        out=cw_sb[:].rearrange("p (c n) -> p c n", c=ndc),
        in_=cwT.rearrange("(c p) n -> p c n", p=128)), dma=True, sem="ldw0")
    ld2 = p.op("gpsimd", lambda e: e.dma_start(
        out=dtw_sb[:].rearrange("p (c n) -> p c n", c=ndc),
        in_=dtwT.rearrange("(c p) n -> p c n", p=128)), dma=True, sem="ldw0")
    ld3 = p.op("sync", lambda e: e.dma_start(out=bvec[:, 0:1], in_=bb[:]),
               dma=True, sem="ldw1")
    ld4 = p.op("sync", lambda e: e.dma_start(out=bvec[:, 1:2], in_=cb[:]),
               dma=True, sem="ldw1")
    ld5 = p.op("sync", lambda e: e.dma_start(out=dtb_sb[:], in_=dtb[:]),
               dma=True, sem="ldw1")
    ld6 = p.op("sync", lambda e: e.dma_start(out=alog_sb[:], in_=alog[:]),
               dma=True, sem="ldw1")
    one_ap, one_tok = a.const(1.0)
    mz = p.op("gpsimd", lambda e: e.memset(ident[:], 1.0))
    idt = p.op("gpsimd", lambda e: e.affine_select(
        ident[:], ident[:], [[-1, 128]], ALU.is_equal, 0.0,
        base=0, channel_multiplier=1), after=[mz])
    idb = p.op("vector", lambda e: e.tensor_copy(id_bf[:], ident[:]), after=[idt])
    idr = p.op("vector", lambda e: e.tensor_copy(id_r[:], ident[:]), after=[idt])
    ones_f = a.sb("ones_f", [128, 128])
    _onf = p.op("gpsimd", lambda e: e.memset(ones_f[:], 1.0))
    onr = p.op("vector", lambda e: e.tensor_copy(ones_r[:], ones_f[:]), after=[_onf])
    onb = p.op("vector", lambda e: e.tensor_copy(ones_bf[:], ones_f[0:1, :]), after=[_onf])
    zr0 = p.op("gpsimd", lambda e: e.memset(zro[:], 0.0))
    ea = p.op("scalar", lambda e: e.activation(na[:], alog_sb[:], AF.Exp),
              after=[ld6])
    ea2 = p.op("vector", lambda e: e.tensor_scalar(na[:], na[:], -1.0, None,
                                                   ALU.mult), after=[ea])
    if full:
        ldd = p.op("sync", lambda e: e.dma_start(out=dbc[0:1, :], in_=drow[:]),
                   dma=True, sem="ldw2")
        ldo = p.op("gpsimd", lambda e: e.dma_start(out=obrow[:], in_=ob[:]),
                   dma=True, sem="ldob")
        c2r = p.op("vector", lambda e: e.tensor_copy(dbr[:], dbc[0:1, :]), after=[ldd])
        mmD = p.op("tensor", lambda e: e.matmul(
            pst[:, 0:nh], ones_r[0:1, :], dbr[:], start=True, stop=True),
            after=[c2r, onr])
        evD = p.op("vector", lambda e: e.tensor_copy(dbc[:], pst[:, 0:nh]),
                   after=[mmD])

    tpx = []
    for c in range(ndc):
        tpx.append(p.op("sync", lambda e, c=c: e.dma_start(
            out=XT[c][:, 0:tc], in_=xn3[:, 128 * c:128 * (c + 1)], transpose=True),
            dma=True, sem="ldtp"))
    ldx16 = []
    for i in range(ntk):
        ldx16.append(p.op("gpsimd", lambda e, i=i: e.dma_start(
            out=X16[i][:], in_=xn3[128 * i:128 * (i + 1), :]),
            dma=True, sem="ldx16"))

    bank_busy = {i: None for i in range(6)}
    evB = evC = None
    sp_prev = None
    for ts in range(nts):
        for wi, (wsb, width) in enumerate(((bw_sb, dn), (cw_sb, dn), (dtw_sb, nh))):
            bank = ps[wi % 3]
            mms = []
            for c in range(ndc):
                mm = p.op("tensor", lambda e, wsb=wsb, c=c, ts=ts, width=width,
                          bank=bank: e.matmul(
                    bank[0:width, :], wsb[:, width * c:width * (c + 1)],
                    XT[c][:, 512 * ts:512 * (ts + 1)],
                    start=(c == 0), stop=(c == ndc - 1)),
                    after=[ld0, ld1, ld2, tpx[c]] + ([bank_busy[wi % 3]] if c == 0 else []))
                mms.append(mm)
            if wi == 0:
                evB = p.op("scalar", lambda e, ts=ts, bank=bank: e.activation(
                    BT[:, 512 * ts:512 * (ts + 1)], bank[0:dn, :], AF.Identity,
                    bias=bvec[:, 0:1]), after=[mms[-1], ld3])
                bank_busy[wi % 3] = evB
            elif wi == 1:
                evC = p.op("scalar", lambda e, ts=ts, bank=bank: e.activation(
                    CT[:, 512 * ts:512 * (ts + 1)], bank[0:dn, :], AF.Identity,
                    bias=bvec[:, 1:2]), after=[mms[-1], ld4])
                bank_busy[wi % 3] = evC
            else:
                s1 = p.op("scalar", lambda e, bank=bank: e.activation(
                    spr[0:nh, :], bank[0:nh, :], AF.Relu, bias=dtb_sb[:]),
                    after=[mms[-1], ld5, sp_prev])
                s2 = p.op("scalar", lambda e, bank=bank: e.activation(
                    spab[0:nh, :], bank[0:nh, :], AF.Abs, bias=dtb_sb[:]),
                    after=[mms[-1], s1])
                s3 = p.op("scalar", lambda e: e.activation(
                    spab[0:nh, :], spab[0:nh, :], AF.Exp, scale=-1.0), after=[s2])
                s4 = p.op("scalar", lambda e: e.activation(
                    spab[0:nh, :], spab[0:nh, :], AF.Ln, bias=one_ap[0:nh, 0:1]),
                    after=[s3, one_tok])
                s5 = p.op("vector", lambda e, ts=ts: e.tensor_add(
                    dt_sb[:, 512 * ts:512 * (ts + 1)], spr[0:nh, :], spab[0:nh, :]),
                    after=[s4])
                sp_prev = s5
                bank_busy[wi % 3] = s5
    cb1 = p.op("vector", lambda e: e.tensor_copy(BTb[:], BT[:]), after=[evB])
    cb2 = p.op("vector", lambda e: e.tensor_copy(CTb[:], CT[:]), after=[evC])
    cb3 = p.op("vector", lambda e: e.tensor_copy(dt_bf[:], dt_sb[:]), after=[sp_prev])

    proj_done = [cb1, cb2, cb3, bank_busy[0], bank_busy[1], bank_busy[2]]
    tk_toks = []
    prev_e1 = prev_e2 = None
    for i in range(ntk):
        if not full:
            m1 = p.op("tensor", lambda e, i=i: e.matmul(
                pst[:, 0:dn], BTb[:, 128 * i:128 * (i + 1)], id_bf[0:dn, 0:dn],
                start=True, stop=True), after=proj_done + [idb, prev_e1])
            prev_e1 = p.op("vector", lambda e, i=i: e.tensor_copy(
                Btok[i][:], pst[:, 0:dn]), after=[m1])
        m2 = p.op("tensor", lambda e, i=i: e.matmul(
            pst[:, 64:64 + nh], dt_bf[:, 128 * i:128 * (i + 1)], id_bf[0:nh, 0:nh],
            start=True, stop=True), after=proj_done + [idb, prev_e2, prev_e1])
        prev_e2 = p.op("vector", lambda e, i=i: e.tensor_copy(
            dttok[i][:], pst[:, 64:64 + nh]), after=[m2])
        tk_toks.append(prev_e2)

    if full:
        mk_toks = []
        mnegf = a.sb("mnegf", [128, CH])
        for half in range(2):
            mz2 = p.op("gpsimd", lambda e: e.memset(mnegf[:], 0.0),
                       after=proj_done + tk_toks + mk_toks)
            mk0 = p.op("gpsimd", lambda e, half=half: e.affine_select(
                mnegf[:], mnegf[:], [[1, CH]], ALU.is_ge, -1e30,
                base=-128 * half, channel_multiplier=-1), after=[mz2])
            mk = p.op("vector", lambda e, half=half: e.tensor_copy(
                mneg[half][:], mnegf[:]), after=[mk0])
            mk_toks.append(mk)

    st_stores = []
    y_stores = []
    prev_stx = None
    prev_da = None
    last_chunk_toks = list(tk_toks) + proj_done
    pv_busy = [None, None]
    yt_busy = [None, None]
    xdr_busy = [None, None, None, None]
    for z in range(nzl):
        deps = last_chunk_toks
        c0 = CH * z
        xa, xb_ = xdr[2 * (z % 2)], xdr[2 * (z % 2) + 1]
        xd_toks = []
        for li, xti in ((0, xa), (1, xb_)):
            for h in range(nh):
                tkx = p.op("vector", lambda e, z=z, li=li, h=h, xti=xti: e.tensor_scalar(
                    xti[:, hd * h:hd * (h + 1)],
                    X16[2 * z + li][:, hd * h:hd * (h + 1)],
                    dttok[2 * z + li][:, h:h + 1], None, ALU.mult),
                    after=[tk_toks[2 * z + li], ldx16[2 * z + li],
                           xdr_busy[2 * (z % 2) + li]] +
                          ([] if z >= 2 else (tpx if full else tpx)) +
                          ([deps[-1]] if z == 0 else []))
                xd_toks.append(tkx)
        xdt_done = xd_toks[-1]
        t1 = p.op("vector", lambda e, c0=c0: e.tensor_scalar(
            cs_c[:], dt_sb[:, c0:c0 + CH], na[0:nh, 0:1], None, ALU.mult),
            after=deps + [ea2])
        t2 = p.op("vector", lambda e: e.tensor_tensor_scan(
            cs_c[:], cs_c[:], zro[:, 0:CH], 0.0, ALU.add, ALU.add),
            after=[t1, zr0])
        if not full:
            t3 = p.op("scalar", lambda e: e.activation(
                dcy[:], cs_c[:], AF.Exp, scale=-1.0, bias=cs_c[:, CH - 1:CH]),
                after=[t2])
            t4 = p.op("vector", lambda e, z=z: e.tensor_copy(
                dasb[:, z:z + 1], cs_c[:, CH - 1:CH]), after=[t2, prev_da])
            prev_da = t4
        else:
            t5a = p.op("vector", lambda e: e.tensor_scalar(
                cofi[:, CH:2 * CH], cs_c[:], 1.0 / 64.0, None, ALU.mult),
                after=[t2])
            t5b = p.op("vector", lambda e: e.tensor_copy(
                modi[:, 0:CH], cofi[:, CH:2 * CH]), after=[t5a])
            t5c = p.op("vector", lambda e: e.tensor_copy(
                cofi[:, CH:2 * CH], modi[:, 0:CH]), after=[t5b])
            t6a = p.op("vector", lambda e: e.tensor_scalar(
                cofi[:, 0:CH], cofi[:, CH:2 * CH], 64.0, None, ALU.mult),
                after=[t5c])
            t6 = p.op("vector", lambda e: e.tensor_sub(
                cofi[:, CH:2 * CH], cs_c[:], cofi[:, 0:CH]), after=[t6a])
            t7 = p.op("vector", lambda e: e.tensor_copy(cofi_r[:], cofi[:]),
                      after=[t6])
            clps = []
            for gi in range(nh // 8):
                for (buf, off) in ((corow, 0), (firow, CH)):
                    clp = p.op("gpsimd", lambda e, gi=gi, buf=buf, off=off: e.dma_start(
                        out=buf[32 * gi:32 * gi + 1, :].rearrange(
                            "o (h n) -> o h n", h=8),
                        in_=cofi_r[8 * gi:8 * (gi + 1), off:off + CH]),
                        after=[t7], dma=True, sem=f"clp{gi % 2}")
                    clps.append(clp)
            tms = []
            for q in range(4):
                part, half = q // 2, q % 2
                tm = p.op("tensor", lambda e, part=part, half=half, q=q: e.matmul(
                    pst[:, 512 - 4 * nh + nh * q:512 - 4 * nh + nh * (q + 1)],
                    cofi_r[:, CH * part + 128 * half:CH * part + 128 * (half + 1)],
                    id_r[0:nh, 0:nh], start=True, stop=True),
                    after=[t7, idr] + tms)
                tms.append(tm)
            tA = p.op("vector", lambda e: e.tensor_copy(
                ctsb[:, 0:4 * nh], pst[:, 512 - 4 * nh:512]), after=tms)
            ncg = []
            for half in range(2):
                ncg.append(p.op("vector", lambda e, half=half: e.scalar_tensor_tensor(
                    negct[:, nh * half:nh * (half + 1)],
                    ctsb[:, nh * half:nh * (half + 1)], -1.0,
                    ctsb[:, 2 * nh + nh * half:2 * nh + nh * (half + 1)],
                    ALU.mult, ALU.subtract), after=[tA]))
        if not full:
            dts = []
            for half in range(2):
                tm = p.op("tensor", lambda e, half=half: e.matmul(
                    pst[:, 256 + nh * half:256 + nh * (half + 1)],
                    dcy[:, 128 * half:128 * (half + 1)], id_bf[0:nh, 0:nh],
                    start=True, stop=True), after=[t3, idb] + dts)
                dts.append(tm)
            dtv = p.op("vector", lambda e: e.tensor_copy(
                dcyT[:], pst[:, 256:256 + 2 * nh]), after=dts)
            evs = []
            for h in range(nh):
                w1 = p.op("vector", lambda e, h=h, xa=xa: e.tensor_scalar(
                    xw[0][:], xa[:, hd * h:hd * (h + 1)],
                    dcyT[:, h:h + 1], None, ALU.mult),
                    after=[dtv, xdt_done] + evs[-1:])
                w2 = p.op("vector", lambda e, h=h, xb_=xb_: e.tensor_scalar(
                    xw[1][:], xb_[:, hd * h:hd * (h + 1)],
                    dcyT[:, nh + h:nh + h + 1], None, ALU.mult),
                    after=[dtv, xdt_done] + evs[-1:])
                m1 = p.op("tensor", lambda e, z=z: e.matmul(
                    pst[0:dn, 0:hd], Btok[2 * z][:], xw[0][:],
                    start=True, stop=False), after=[w1, prev_e1])
                m2 = p.op("tensor", lambda e, z=z: e.matmul(
                    pst[0:dn, 0:hd], Btok[2 * z + 1][:], xw[1][:],
                    start=False, stop=True), after=[w2])
                ev = p.op("vector", lambda e, h=h: e.tensor_copy(
                    stx[:, hd * h:hd * (h + 1)], pst[0:dn, 0:hd]),
                    after=[m2, prev_stx])
                evs.append(ev)
            sst = p.op("scalar", lambda e, z=z: e.dma_start(
                out=stex[z].rearrange("(h n q) -> n h q", h=nh, n=dn),
                in_=stx[:].rearrange("n (h q) -> n h q", h=nh)),
                after=evs, dma=True, sem="stst")
            st_stores.append(sst)
            prev_stx = sst
            xdr_busy[2 * (z % 2)] = evs[-1]
            xdr_busy[2 * (z % 2) + 1] = evs[-1]
            last_chunk_toks = [evs[-1], t2]
            continue
        sc_toks = []
        for half in range(2):
            mm = p.op("tensor", lambda e, half=half, c0=c0: e.matmul(
                ps[4 + half][:, 0:CH], BTb[:, c0 + 128 * half:c0 + 128 * (half + 1)],
                CTb[:, c0:c0 + CH], start=True, stop=True),
                after=[cb1, cb2] + ([bank_busy[4 + half]] if bank_busy[4 + half] else []))
            evv = p.op("vector", lambda e, half=half: e.tensor_copy(
                scb[half][:], ps[4 + half][:, 0:CH]),
                after=[mm] + y_stores[-2:])
            bank_busy[4 + half] = evv
            sc_toks.append(evv)
        bpv = z % 2
        ldpv = p.op("sync", lambda e, z=z, bpv=bpv: e.dma_start(
            out=prevt[bpv][:].rearrange("n (h q) -> n h q", h=nh),
            in_=prevc[:, z, :].rearrange("h (n q) -> n h q", n=dn)),
            after=[pv_busy[bpv]] + (deps if z < 2 else []),
            dma=True, sem=f"ldpv{bpv}")
        ymms_last = None
        prev_ms = None
        prev_e3 = None
        for h in range(nh):
            gi, gc = h // 8, h % 8
            eL = []
            for half in range(2):
                tp_ = (96, 0) if gi == 3 else None
                b1 = p.op("tensor", lambda e, gi=gi, gc=gc, half=half, tp_=tp_: e.matmul(
                    ps[half][:, 0:CH], ones_r[32 * gi:32 * gi + 1, :],
                    corow[32 * gi:32 * gi + 1, CH * gc:CH * (gc + 1)],
                    start=True, stop=False, tile_position=tp_),
                    after=clps + [onr] + ([bank_busy[half]] if bank_busy[half] else []))
                b2 = p.op("tensor", lambda e, gi=gi, gc=gc, half=half, tp_=tp_: e.matmul(
                    ps[half][:, 0:CH], ones_r[32 * gi:32 * gi + 1, :],
                    firow[32 * gi:32 * gi + 1, CH * gc:CH * (gc + 1)],
                    start=False, stop=False, tile_position=tp_), after=[])
                b3 = p.op("tensor", lambda e, half=half: e.matmul(
                    ps[half][:, 0:CH], id_r[:], mneg[half][:],
                    start=False, stop=True), after=[mk_toks[half]])
                ex = p.op("scalar", lambda e, h=h, half=half: e.activation(
                    expl[:], ps[half][:, 0:CH], AF.Exp,
                    bias=negct[:, nh * half + h:nh * half + h + 1]),
                    after=[b3, prev_ms] + ncg)
                ms = p.op("vector", lambda e, half=half: e.tensor_mul(
                    mst[half][:], scb[half][:], expl[:]),
                    after=[ex, sc_toks[half]])
                prev_ms = ms
                bank_busy[half] = ms
                eL.append(ms)
            tp_ = (96, 0) if gi == 3 else None
            b4 = p.op("tensor", lambda e, gi=gi, gc=gc, tp_=tp_: e.matmul(
                ps[2][0:dn, 0:CH], ones_r[32 * gi:32 * gi + 1, 0:dn],
                corow[32 * gi:32 * gi + 1, CH * gc:CH * (gc + 1)],
                start=True, stop=False, tile_position=tp_),
                after=clps + ([bank_busy[2]] if bank_busy[2] else []))
            b5 = p.op("tensor", lambda e, gi=gi, gc=gc, tp_=tp_: e.matmul(
                ps[2][0:dn, 0:CH], ones_r[32 * gi:32 * gi + 1, 0:dn],
                firow[32 * gi:32 * gi + 1, CH * gc:CH * (gc + 1)],
                start=False, stop=True, tile_position=tp_), after=[])
            ex3 = p.op("scalar", lambda e: e.activation(
                exp3[:], ps[2][0:dn, 0:CH], AF.Exp), after=[b5, prev_e3])
            ms3 = p.op("vector", lambda e, c0=c0: e.tensor_mul(
                mst3[:], CT[:, c0:c0 + CH], exp3[:]), after=[ex3, evC])
            prev_e3 = ms3
            bank_busy[2] = ms3
            for lqh in range(2):
                ya = p.op("tensor", lambda e, h=h, lqh=lqh, xa=xa: e.matmul(
                    ps[3][:, 0:hd], mst[0][:, 128 * lqh:128 * (lqh + 1)],
                    xa[:, hd * h:hd * (h + 1)], start=True, stop=False),
                    after=[eL[0], xdt_done] + ([bank_busy[3]] if bank_busy[3] else []))
                yb = p.op("tensor", lambda e, h=h, lqh=lqh, xb_=xb_: e.matmul(
                    ps[3][:, 0:hd], mst[1][:, 128 * lqh:128 * (lqh + 1)],
                    xb_[:, hd * h:hd * (h + 1)], start=False, stop=False),
                    after=[eL[1]])
                yc = p.op("tensor", lambda e, h=h, lqh=lqh, bpv=bpv: e.matmul(
                    ps[3][:, 0:hd], mst3[:, 128 * lqh:128 * (lqh + 1)],
                    prevt[bpv][:, hd * h:hd * (h + 1)], start=False, stop=True),
                    after=[ms3, ldpv])
                yev = p.op("vector", lambda e, z=z, h=h, lqh=lqh: e.scalar_tensor_tensor(
                    yt[lqh][:, hd * h:hd * (h + 1)],
                    X16[2 * z + lqh][:, hd * h:hd * (h + 1)],
                    dbc[0:128, h:h + 1], ps[3][:, 0:hd],
                    ALU.mult, ALU.add),
                    after=[yc, evD] + ([yt_busy[lqh]] if h == 0 else []))
                bank_busy[3] = yev
                ymms_last = yev
                if h == nh - 1:
                    yst = p.op("scalar", lambda e, z=z, lqh=lqh: e.dma_start(
                        out=yts[128 * (2 * z + lqh):128 * (2 * z + lqh + 1), :],
                        in_=yt[lqh][:]), after=[yev], dma=True, sem=f"sty{lqh}")
                    y_stores.append(yst)
                    yt_busy[lqh] = yst
        pv_busy[bpv] = ymms_last
        xdr_busy[2 * (z % 2)] = ymms_last
        xdr_busy[2 * (z % 2) + 1] = ymms_last
        last_chunk_toks = [ymms_last]

    if not full:
        sda = p.op("scalar", lambda e: e.dma_start(
            out=dasum[:, 0:nzl], in_=dasb[:, 0:nzl]),
            after=[prev_da] + st_stores, dma=True, sem="stda")
        p.final_wait(st_stores + [sda])
        p.build(nc)
        a.close()
        return nc

    ldow_toks = []
    for c in range(ndc):
        ldow_toks.append(p.op("gpsimd", lambda e, c=c: e.dma_start(
            out=ow[c][:], in_=owT[128 * c:128 * (c + 1), :]),
            after=y_stores, dma=True, sem="ldow"))
    ytp = []
    for c in range(ndc):
        ytp.append(p.op("sync", lambda e, c=c: e.dma_start(
            out=YT[c][:, 0:tc], in_=yts[:, 128 * c:128 * (c + 1)], transpose=True),
            after=y_stores, dma=True, sem="ldytp"))
    prev_x2 = [None, None]
    prev_x3 = [None, None]
    prev_xn = [None, None]
    prev_sq = None
    out_stores = []
    for i in range(ntk):
        b = i % 2
        ldx2 = p.op("sync", lambda e, i=i, b=b: e.dma_start(
            out=x2t[b][:], in_=x2[128 * i:128 * (i + 1), :]),
            after=[prev_x2[b]] + (proj_done + y_stores if i < 2 else []),
            dma=True, sem=f"ldx2{b}")
        for s in range(d // 512):
            bank = ps[s % 2]
            mms = []
            for c in range(ndc):
                mm = p.op("tensor", lambda e, i=i, s=s, c=c, bank=bank: e.matmul(
                    bank[:, 0:512], YT[c][:, 128 * i:128 * (i + 1)],
                    ow[c][:, 512 * s:512 * (s + 1)],
                    start=(c == 0), stop=False),
                    after=[ytp[c], ldow_toks[c]] + ([bank_busy[s % 2]] if c == 0 else []))
                mms.append(mm)
            mob = p.op("tensor", lambda e, i=i, s=s, bank=bank: e.matmul(
                bank[:, 0:512], ones_bf[:],
                obrow[0:1, 512 * s:512 * (s + 1)], start=False, stop=True),
                after=[onb, ldo])
            ev2 = p.op("vector", lambda e, s=s, b=b, bank=bank: e.tensor_add(
                x3t[b][:, 512 * s:512 * (s + 1)], bank[:, 0:512],
                x2t[b][:, 512 * s:512 * (s + 1)]),
                after=[mob, ldx2, prev_x3[b]])
            bank_busy[s % 2] = ev2
            prev_x3[b] = ev2
        stx3 = p.op("scalar", lambda e, i=i, b=b: e.dma_start(
            out=x3o[128 * i:128 * (i + 1), :], in_=x3t[b][:]),
            after=[prev_x3[b]], dma=True, sem=f"stx3{b}")
        t8, prev_sq = emit_ln(p, a, x3t[b], xnt[b], i, stats, sqscr,
                              [prev_x3[b], prev_xn[b]], prev_sq, d=d)
        stxn = p.op("scalar", lambda e, i=i, b=b: e.dma_start(
            out=xn4o[128 * i:128 * (i + 1), :], in_=xnt[b][:]),
            after=[t8], dma=True, sem=f"stxn4{b}")
        prev_xn[b] = stxn
        prev_x2[b] = ev2
        prev_x3[b] = stx3
        out_stores += [stx3, stxn]
    p.final_wait(out_stores)
    p.build(nc)
    a.close()
    return nc


def build_l1prev(nh=32, NZ=64, dn=64, hd=64):
    nc = bass.Bass("TRN2", target_bir_lowering=False)
    stin = nc.dram_tensor("stin", [nh, NZ, dn * hd], F32, kind="ExternalInput").ap()
    dasT = nc.dram_tensor("dasT", [nh, NZ], F32, kind="ExternalInput").ap()
    prevo = nc.dram_tensor("prev", [nh, NZ, dn * hd], BF16, kind="ExternalOutput").ap()
    p = Prog()
    a = Alloc(nc, p)
    das = a.sb("das", [nh, NZ])
    zro = a.sb("zro", [nh, NZ])
    css = a.sb("css", [nh, NZ + 1])
    cofi = a.sb("cofi", [nh, 2 * (NZ + 1)])
    cofi_r = a.sb("cofi_r", [nh, 2 * (NZ + 1)], F32R)
    corow = a.sb("corow", [1, nh * (NZ + 1)], F32R)
    firow = a.sb("firow", [1, nh * (NZ + 1)], F32R)
    ctsb = a.sb("ctsb", [NZ + 1, 2 * nh])
    negct = a.sb("negct", [NZ + 1, nh])
    mneg = a.sb("mneg", [NZ + 1, NZ], F32R)
    modi = a.sb("modi", [nh, NZ + 1], I32)
    ident = a.sb("ident", [128, 128])
    id_r = a.sb("id_r", [128, 128], F32R)
    ones_r = a.sb("ones_r", [1, 128], F32R)
    wt = a.sb("wt", [NZ + 1, NZ], BF16)
    sext = a.sb("sext", [NZ + 1, dn * hd], BF16)
    pv = a.sb("pv", [NZ, dn * hd], BF16)
    pst = a.ps("pst", [128, 512])
    ps2 = a.ps("ps2", [128, 512])

    ld1 = p.op("gpsimd", lambda e: e.dma_start(out=das[:], in_=dasT[:]),
               dma=True, sem="ld")
    zr = p.op("gpsimd", lambda e: e.memset(zro[:], 0.0))
    z1 = p.op("gpsimd", lambda e: e.memset(css[:, 0:1], 0.0))
    t2 = p.op("vector", lambda e: e.tensor_tensor_scan(
        css[:, 1:NZ + 1], das[:], zro[:], 0.0, ALU.add, ALU.add),
        after=[ld1, zr])
    t5a = p.op("vector", lambda e: e.tensor_scalar(
        cofi[:, NZ + 1:], css[:], 1.0 / 64.0, None, ALU.mult), after=[t2, z1])
    t5b = p.op("vector", lambda e: e.tensor_copy(modi[:], cofi[:, NZ + 1:]),
               after=[t5a])
    t5c = p.op("vector", lambda e: e.tensor_copy(cofi[:, NZ + 1:], modi[:]),
               after=[t5b])
    t6a = p.op("vector", lambda e: e.tensor_scalar(
        cofi[:, 0:NZ + 1], cofi[:, NZ + 1:], 64.0, None, ALU.mult), after=[t5c])
    t6 = p.op("vector", lambda e: e.tensor_sub(
        cofi[:, NZ + 1:], css[:], cofi[:, 0:NZ + 1]), after=[t6a])
    t7 = p.op("vector", lambda e: e.tensor_copy(cofi_r[:], cofi[:]), after=[t6])
    t8 = p.op("gpsimd", lambda e: e.dma_start(
        out=corow[0:1, :].rearrange("o (h n) -> o h n", h=nh),
        in_=cofi_r[:, 0:NZ + 1]), after=[t7], dma=True, sem="clp0")
    t9 = p.op("gpsimd", lambda e: e.dma_start(
        out=firow[0:1, :].rearrange("o (h n) -> o h n", h=nh),
        in_=cofi_r[:, NZ + 1:]), after=[t7], dma=True, sem="clp1")
    mz = p.op("gpsimd", lambda e: e.memset(ident[:], 1.0))
    idt = p.op("gpsimd", lambda e: e.affine_select(
        ident[:], ident[:], [[-1, 128]], ALU.is_equal, 0.0,
        base=0, channel_multiplier=1), after=[mz])
    idr = p.op("vector", lambda e: e.tensor_copy(id_r[:], ident[:]), after=[idt])
    ones_f = a.sb("ones_f", [1, 128])
    mnegf = a.sb("mnegf", [NZ + 1, NZ])
    _onf = p.op("gpsimd", lambda e: e.memset(ones_f[:], 1.0))
    onr = p.op("vector", lambda e: e.tensor_copy(ones_r[:], ones_f[:]), after=[_onf])
    mz2 = p.op("gpsimd", lambda e: e.memset(mnegf[:], 0.0))
    mk0 = p.op("gpsimd", lambda e: e.affine_select(
        mnegf[:], mnegf[:], [[1, NZ]], ALU.is_ge, -1e30,
        base=0, channel_multiplier=-1), after=[mz2])
    mk = p.op("vector", lambda e: e.tensor_copy(mneg[:], mnegf[:]), after=[mk0])
    tm1 = p.op("tensor", lambda e: e.matmul(
        pst[0:NZ + 1, 0:nh], cofi_r[:, 0:NZ + 1], id_r[0:nh, 0:nh],
        start=True, stop=True), after=[t7, idr])
    tm2 = p.op("tensor", lambda e: e.matmul(
        pst[0:NZ + 1, nh:2 * nh], cofi_r[:, NZ + 1:], id_r[0:nh, 0:nh],
        start=True, stop=True), after=[t7, idr, tm1])
    tA = p.op("vector", lambda e: e.tensor_copy(
        ctsb[:], pst[0:NZ + 1, 0:2 * nh]), after=[tm1, tm2])
    ncg = p.op("vector", lambda e: e.scalar_tensor_tensor(
        negct[:], ctsb[:, 0:nh], -1.0, ctsb[:, nh:2 * nh],
        ALU.mult, ALU.subtract), after=[tA])
    stores = []
    prev_ld = None
    prev_wt = None
    for h in range(nh):
        b1 = p.op("tensor", lambda e, h=h: e.matmul(
            ps2[0:NZ + 1, 0:NZ], ones_r[0:1, 0:NZ + 1],
            corow[:, (NZ + 1) * h:(NZ + 1) * h + NZ], start=True, stop=False),
            after=[t8, onr, prev_wt])
        b2 = p.op("tensor", lambda e, h=h: e.matmul(
            ps2[0:NZ + 1, 0:NZ], ones_r[0:1, 0:NZ + 1],
            firow[:, (NZ + 1) * h:(NZ + 1) * h + NZ], start=False, stop=False),
            after=[t9])
        b3 = p.op("tensor", lambda e: e.matmul(
            ps2[0:NZ + 1, 0:NZ], id_r[0:NZ + 1, 0:NZ + 1], mneg[:],
            start=False, stop=True), after=[mk])
        exw = p.op("scalar", lambda e, h=h: e.activation(
            wt[:], ps2[0:NZ + 1, 0:NZ], AF.Exp, bias=negct[:, h:h + 1]),
            after=[b3, ncg, prev_wt])
        prev_wt = exw
        z0m = p.op("gpsimd", lambda e: e.memset(sext[0:1, :], 0.0),
                   after=[prev_ld])
        ldst = p.op("gpsimd", lambda e, h=h: e.dma_start(
            out=sext[1:NZ + 1, :], in_=stin[h]), after=[z0m], dma=True, sem="ldst")
        evs = []
        for s in range(dn * hd // 512):
            mm = p.op("tensor", lambda e, s=s: e.matmul(
                pst[0:NZ, 0:512], wt[:], sext[:, 512 * s:512 * (s + 1)],
                start=True, stop=True), after=[exw, ldst] + evs)
            ev = p.op("vector", lambda e, s=s: e.tensor_copy(
                pv[:, 512 * s:512 * (s + 1)], pst[0:NZ, 0:512]),
                after=[mm] + (stores[-1:] if stores else []))
            evs.append(ev)
        st = p.op("scalar", lambda e, h=h: e.dma_start(
            out=prevo[h], in_=pv[:]), after=evs, dma=True, sem="stpv")
        stores.append(st)
        prev_ld = evs[-1]
    p.final_wait(stores)
    p.build(nc)
    a.close()
    return nc


_BUILD_CACHE = {}


def _cached(name, fn):
    if name not in _BUILD_CACHE:
        _BUILD_CACHE[name] = fn()
    return _BUILD_CACHE[name]


def kernel(**inputs):
    ii = {k: np.asarray(v) for k, v in inputs.items()}
    x = ii["L0_attn_hidden_states"][0].astype(np.float32)
    cores = list(range(NCORES))

    ncA = _cached("A", build_ln_launch)
    res = run_bass_kernel_spmd(ncA, [
        {"x": np.ascontiguousarray(x[c * TC:(c + 1) * TC])} for c in cores],
        core_ids=cores)
    ln1 = np.concatenate([res.results[c]["y"] for c in cores], 0)

    ln1T = np.ascontiguousarray(ln1.T)
    Wq, Wk, Wv = ii["L0_attn_Wq"], ii["L0_attn_Wk"], ii["L0_attn_Wv"]
    lwf = ii["L0_attn_linear_weight"][:, 0, :]
    lbf = ii["L0_attn_linear_bias"][:, 0]
    lnw, lnb = ii["L0_attn_ln_weight"], ii["L0_attn_ln_bias"]
    W1i, b1i = ii["L0_attn_W1_init"], ii["L0_attn_b1_init"]
    ncB = _cached("B", build_l0attn)
    mapsB = []
    for c in cores:
        hs = slice(4 * c, 4 * c + 4)
        lnw_rep = np.zeros((128, 64), np.float32)
        for hl in range(4):
            lnw_rep[32 * hl:32 * hl + 16] = lnw[4 * c + hl]
        s0 = np.zeros((65, 256), np.float32)
        for hl in range(4):
            s0[0:64, 64 * hl:64 * (hl + 1)] = W1i[4 * c + hl]
            s0[64, 64 * hl:64 * (hl + 1)] = b1i[4 * c + hl, 0]
        mapsB.append({
            "ln1T": ln1T,
            "wqT": np.ascontiguousarray(Wq[256 * c:256 * (c + 1)].T).astype(np.float32),
            "wkT": np.ascontiguousarray(Wk[256 * c:256 * (c + 1)].T).astype(np.float32),
            "wvT": np.ascontiguousarray(Wv[256 * c:256 * (c + 1)].T).astype(np.float32),
            "lwT4": np.ascontiguousarray(lwf[hs].T).astype(np.float32),
            "lb4": np.ascontiguousarray(lbf[hs].reshape(4, 1)).astype(np.float32),
            "lnw_rep": lnw_rep,
            "lnb_row": np.ascontiguousarray(lnb[hs].reshape(1, 256)).astype(np.float32),
            "s0": s0,
        })
    res = run_bass_kernel_spmd(ncB, mapsB, core_ids=cores)
    XCW = np.concatenate([np.asarray(res.results[c]["xcw"]) for c in cores], 1)

    ncC = _cached("C", build_l0tail)
    wo_t = np.ascontiguousarray(ii["L0_attn_Wo"].T).astype(np.float32)
    gw_t = np.ascontiguousarray(ii["L0_ffn_gate_w"].T).astype(np.float32)
    uw_t = np.ascontiguousarray(ii["L0_ffn_up_w"].T).astype(np.float32)
    dw_t = np.ascontiguousarray(ii["L0_ffn_down_w"].T).astype(np.float32)
    gs = np.ascontiguousarray(ii["L0_ffn_gate_s"].reshape(-1, 128).T).astype(np.float32)
    us = np.ascontiguousarray(ii["L0_ffn_up_s"].reshape(-1, 128).T).astype(np.float32)
    ds = ii["L0_ffn_down_s"].reshape(1, -1).astype(np.float32)
    mapsC = [{
        "x": np.ascontiguousarray(x[c * TC:(c + 1) * TC]),
        "xcw": np.ascontiguousarray(XCW[c * TC:(c + 1) * TC]),
        "wo_t": wo_t, "gw_t": gw_t, "uw_t": uw_t, "dw_t": dw_t,
        "gs": gs, "us": us, "ds": ds} for c in cores]
    res = run_bass_kernel_spmd(ncC, mapsC, core_ids=cores)
    x2 = np.concatenate([np.asarray(res.results[c]["x2"]) for c in cores], 0)
    xn3 = np.concatenate([np.asarray(res.results[c]["xn3"]) for c in cores], 0)

    bwT = np.ascontiguousarray(ii["L1_attn_B_w"].T).astype(np.float32)
    cwT = np.ascontiguousarray(ii["L1_attn_C_w"].T).astype(np.float32)
    dtwT = np.ascontiguousarray(ii["L1_attn_dt_w"].T).astype(np.float32)
    bb = ii["L1_attn_B_b"].reshape(-1, 1).astype(np.float32)
    cb = ii["L1_attn_C_b"].reshape(-1, 1).astype(np.float32)
    dtb = ii["L1_attn_dt_b"].reshape(-1, 1).astype(np.float32)
    alog = ii["L1_attn_A_log"].reshape(-1, 1).astype(np.float32)
    baseD = {"bwT": bwT, "cwT": cwT, "dtwT": dtwT, "bb": bb, "cb": cb,
             "dtb": dtb, "alog": alog}
    ncD1 = _cached("D1", lambda: build_l1attn("states"))
    mapsD1 = [dict(baseD, xn3=np.ascontiguousarray(xn3[c * TC:(c + 1) * TC]))
              for c in cores]
    res = run_bass_kernel_spmd(ncD1, mapsD1, core_ids=cores)
    nh, dn, hd, nzl = 32, 64, 64, TC // 256
    NZ = NCORES * nzl
    states = np.zeros((nh, NZ, dn * hd), np.float32)
    dasT = np.zeros((nh, NZ), np.float32)
    for c in cores:
        st = np.asarray(res.results[c]["stex"]).reshape(nzl, nh, dn * hd)
        states[:, nzl * c:nzl * (c + 1), :] = st.transpose(1, 0, 2)
        dasT[:, nzl * c:nzl * (c + 1)] = np.asarray(res.results[c]["dasum"])

    ncD2a = _cached("D2a", lambda: build_l1prev(nh=nh, NZ=NZ))
    res = run_bass_kernel_spmd(ncD2a, [{"stin": states, "dasT": dasT}] * NCORES,
                               core_ids=cores)
    prev = np.asarray(res.results[0]["prev"])

    owT = np.ascontiguousarray(ii["L1_attn_out_w"].T).astype(np.float32)
    ob = ii["L1_attn_out_b"].reshape(1, -1).astype(np.float32)
    drow = ii["L1_attn_D"].reshape(1, -1).astype(np.float32)
    ncD2b = _cached("D2b", lambda: build_l1attn("main"))
    mapsD2 = [dict(baseD,
                   xn3=np.ascontiguousarray(xn3[c * TC:(c + 1) * TC]),
                   x2=np.ascontiguousarray(x2[c * TC:(c + 1) * TC]),
                   drow=drow, owT=owT, ob=ob,
                   prevc=np.ascontiguousarray(prev[:, nzl * c:nzl * (c + 1), :]))
              for c in cores]
    res = run_bass_kernel_spmd(ncD2b, mapsD2, core_ids=cores)
    x3 = np.concatenate([np.asarray(res.results[c]["x3"]) for c in cores], 0)
    xn4 = np.concatenate([np.asarray(res.results[c]["xn4"]) for c in cores], 0)

    pid = pi_perm(D)
    xn4T = np.ascontiguousarray(xn4.T[pid, :])
    ncE = _cached("E", lambda: build_l1ffn())
    mapsE = []
    for c in cores:
        m = {"xn4T": xn4T}
        m.update(prep_l1ffn_core(c, NCORES, ii["L1_ffn_gate_W_packed"],
                                 ii["L1_ffn_gate_scales"],
                                 ii["L1_ffn_up_W_packed"], ii["L1_ffn_up_scales"],
                                 ii["L1_ffn_down_W_packed"],
                                 ii["L1_ffn_down_scales"], D, 16384))
        mapsE.append(m)
    res = run_bass_kernel_spmd(ncE, mapsE, core_ids=cores)
    parts = [np.asarray(res.results[c]["part"]) for c in cores]

    ncF = _cached("F", build_reduce)
    mapsF = []
    for c in cores:
        sl = slice(c * TC, (c + 1) * TC)
        pp = np.stack([pt[sl] for pt in parts], 0)
        mapsF.append({"x3": np.ascontiguousarray(x3[sl]),
                      "pp": np.ascontiguousarray(pp)})
    res = run_bass_kernel_spmd(ncF, mapsF, core_ids=cores)
    x4 = np.concatenate([np.asarray(res.results[c]["y"]) for c in cores], 0)
    return x4[None].astype(np.float32)
